# version 03
# baseline (speedup 1.0000x reference)
import numpy as np
import ml_dtypes
import concourse.bass as bass
import concourse.mybir as mybir
from concourse.bass_utils import run_bass_kernel_spmd

F32 = mybir.dt.float32
BF16 = mybir.dt.bfloat16
I32 = mybir.dt.int32
AF = mybir.ActivationFunctionType
ALU = mybir.AluOpType

P = 128
TC = 1024
NCH = 8
TH = 512
DM = 8
DC = 8
S = 16
SG = 4
NSG = S // SG
EPS = 1e-5
RG = [[0, 1], [2, 3], [4, 5], [6, 7]]
PHL = 99


def build_nc():
    nc = bass.Bass("TRN2", target_bir_lowering=False, detect_race_conditions=False)
    dt = lambda n, s, d, k="Internal": nc.dram_tensor(n, s, d, kind=k).ap()

    xT = dt("xT", [1024, 8192], F32, "ExternalInput")
    xoT = dt("xoT", [1024, 4096], F32, "ExternalInput")
    w_in_T = dt("w_in_T", [1024, 2048], F32, "ExternalInput")
    w_x_T = dt("w_x_T", [1024, 33], F32, "ExternalInput")
    w_out_T = dt("w_out_T", [1024, 1024], F32, "ExternalInput")
    a_log_c = dt("a_log_c", [P, DC * S], F32, "ExternalInput")
    conv_w_c = dt("conv_w_c", [P, DC * 4], F32, "ExternalInput")
    conv_b_c = dt("conv_b_c", [P, DC], F32, "ExternalInput")
    d_c = dt("d_c", [P, DC], F32, "ExternalInput")
    gam_c = dt("gam_c", [P, DM], F32, "ExternalInput")
    bet_c = dt("bet_c", [P, DM], F32, "ExternalInput")
    q1 = dt("q1", [1024, 4096], I32, "ExternalInput")
    s1 = dt("s1", [8, 4096], F32, "ExternalInput")
    b1_c = dt("b1_c", [P, 32], F32, "ExternalInput")
    q2 = dt("q2", [4096, 1024], I32, "ExternalInput")
    s2 = dt("s2", [32, 1024], F32, "ExternalInput")
    b2_c = dt("b2_c", [P, DM], F32, "ExternalInput")
    out = dt("out", [1024, 4096], F32, "ExternalOutput")

    xbf = dt("xbf", [1024, 8192], BF16)
    xobf = dt("xobf", [1024, 4096], BF16)
    w_in_bf = dt("w_in_bf", [1024, 2048], BF16)
    w_out_bf = dt("w_out_bf", [1024, 1024], BF16)
    w1bf = dt("w1bf", [1024, 4096], BF16)
    w2bf = dt("w2bf", [4096, 1024], BF16)
    rows32 = dt("rows32", [1, 4 * TC], F32)
    rows16 = dt("rows16", [32, TC], BF16)
    ssm_in = dt("ssm_in", [33, TC], F32)
    ssm_out = dt("ssm_out", [33, TC], F32)
    mix_in = dt("mix_in", [2, 1024, TH], F32)
    mix_out = dt("mix_out", [1024, TH], F32)

    sbuf = nc.alloc_sbuf_tensor
    w_x_sb = sbuf("w_x_sb", [P, DC * 33], BF16)
    negA = sbuf("negA", [P, DC * S], F32)
    convw = sbuf("convw", [P, DC * 4], F32)
    convb = sbuf("convb", [P, DC], F32)
    ddc = sbuf("ddc", [P, DC], F32)
    gamc = sbuf("gamc", [P, DM], F32)
    b1eff = sbuf("b1eff", [P, 32], F32)
    b2h = sbuf("b2h", [P, DM], F32)
    betc = sbuf("betc", [P, DM], F32)
    ones1 = sbuf("ones1", [P, 1], BF16)
    ones16 = sbuf("ones16", [1, S], BF16)
    cst = sbuf("cst", [P, 2], F32)
    state = sbuf("state", [P, DC * S], F32)
    halo = sbuf("halo", [P, DC * 3], BF16)

    xt = sbuf("xt", [P, DM * TC], BF16)
    xin = sbuf("xin", [P, DC * (TC + 3)], BF16)
    xsi = sbuf("xsi", [P, DC * TC], BF16)
    zg = sbuf("zg", [P, DC * TC], BF16)
    yb = sbuf("yb", [P, DC * TC], BF16)
    x2s = sbuf("x2s", [P, TC], BF16)
    x1b = sbuf("x1b", [P, DM * TH], BF16)
    u2b = sbuf("u2b", [P, DM * TH], BF16)
    o2b = sbuf("o2b", [P, DM * TH], BF16)
    wst = [sbuf(f"wst{i}", [P, 1024], BF16) for i in range(2)]
    rbc = sbuf("rbc", [P, TC], F32)
    mbc = sbuf("mbc", [P, TC], F32)
    srow = sbuf("srow", [1, 4 * TC], F32)
    tmpb = sbuf("tmpb", [P, TC], BF16)

    A = sbuf("A", [P, 12288], F32)
    Ab = A[:].bitcast(BF16)
    u_v = lambda: Ab[:, 0:DM * TC]
    hg_v = [Ab[:, 0:SG * TC], Ab[:, SG * TC:2 * SG * TC]]
    dbb_v = [Ab[:, 8192 + i * SG * TC: 8192 + (i + 1) * SG * TC] for i in range(2)]
    cbb_v = [Ab[:, 16384 + i * SG * TC: 16384 + (i + 1) * SG * TC] for i in range(2)]
    ffnh_v = Ab[:, 0:32 * TH]
    xo = Ab[:, 16384:16384 + DM * TH]

    U = sbuf("U", [P, 4096], F32)
    Ub = U[:].bitcast(BF16)
    prew_v = U[:, 0:4096]
    ssm_v = U[0:33, 0:1024]
    Bt_v = U[0:16, 0:1024]
    Ct_v = U[0:16, 1024:2048]
    dBt_v = Ub[0:16, 4096:5120]
    Cbt_v = Ub[0:16, 5120:6144]
    dbc_v = U[:, 0:1024]
    dA_v = [U[:, 1024:2048], U[:, 2048:3072]]
    pp_v = [Ub[:, 6144:7168], Ub[:, 7168:8192]]
    mixh_v = U[:, 0:4096]
    fstg_v = [U[:, 0:512], U[:, 512:1024]]
    r4bc_v = U[:, 1024:1536]

    preb = xt[:, 0:4096]
    pres = xt[:, 4096:8192]
    pres_f = A[:, 0:4096]

    ps = [nc.alloc_psum_tensor(f"ps{i}", [P, 512], F32) for i in range(8)]

    names = ["ld", "wsA", "wsB", "row", "bc", "bcA", "bcB", "osA", "osB", "cc", "mm", "ev", "ea", "hv", "cp", "wf", "lg", "sg", "sdA", "sdB"]
    sem = {n: nc.alloc_semaphore(n) for n in names}
    cnt = {n: 0 for n in names}

    def inc(n, k=1):
        cnt[n] += k
        return cnt[n]

    with nc.Block() as B:
        @B.sync
        def _(q):
            for dst, src in [(negA, a_log_c), (convw, conv_w_c), (convb, conv_b_c),
                             (ddc, d_c), (gamc, gam_c), (betc, bet_c),
                             (b1eff, b1_c), (b2h, b2_c)]:
                q.dma_start(out=dst[:], in_=src).then_inc(sem["ld"], 16)
                inc("ld", 16)
            q.wait_ge(sem["ld"], cnt["ld"])

        @B.gpsimd
        def _(g):
            for kc in range(DC):
                g.dma_start(out=w_x_sb[:, kc * 33:(kc + 1) * 33],
                            in_=w_x_T[kc * P:(kc + 1) * P, :]).then_inc(sem["lg"], 16)
                inc("lg", 16)
            g.wait_ge(sem["lg"], cnt["lg"])

        @B.vector
        def _(v):
            v.memset(state[:], 0.0)
            v.memset(halo[:], 0.0)
            v.memset(ones1[:], 1.0)
            v.memset(ones16[:], 1.0)
            v.memset(cst[:, 0:1], EPS)
            v.memset(cst[:, 1:2], 1.0)

    with nc.Block() as B:
        @B.scalar
        def _(s):
            s.activation(negA[:], negA[:], AF.Exp, bias=0.0, scale=1.0)

    with nc.Block() as B:
        @B.vector
        def _(v):
            v.tensor_scalar_mul(negA[:], negA[:], -1.0)
            v.tensor_scalar_mul(b2h[:], b2h[:], 0.5)

    rounds = []
    for kc in range(8):
        for h in range(2):
            rounds.append((xT[kc * P:(kc + 1) * P, h * 4096:(h + 1) * 4096],
                           xbf[kc * P:(kc + 1) * P, h * 4096:(h + 1) * 4096], 4096))
    for kc in range(8):
        rounds.append((xoT[kc * P:(kc + 1) * P, :], xobf[kc * P:(kc + 1) * P, :], 4096))
    for kc in range(8):
        rounds.append((w_in_T[kc * P:(kc + 1) * P, :], w_in_bf[kc * P:(kc + 1) * P, :], 2048))
    for kc in range(8):
        rounds.append((w_out_T[kc * P:(kc + 1) * P, :], w_out_bf[kc * P:(kc + 1) * P, :], 1024))

    with nc.Block() as B:
        sg0 = cnt["sg"]

        @B.gpsimd
        def _(g):
            for i, (src, dst, n) in enumerate(rounds):
                bufn = (preb if i % 2 == 0 else pres)[:, 0:n]
                if i >= 2:
                    g.wait_ge(sem["sg"], sg0 + (i - 1) * 16)
                g.dma_start(out=bufn, in_=src).then_inc(sem["lg"], 16)
                g.wait_ge(sem["lg"], inc("lg", 16))
                if i >= 1:
                    g.wait_ge(sem["sg"], sg0 + i * 16)
                g.dma_start(out=dst, in_=bufn).then_inc(sem["sg"], 16)
                inc("sg", 16)
            g.wait_ge(sem["sg"], cnt["sg"])

    with nc.Block() as B:
        ea0, hv0 = cnt["ea"], cnt["hv"]

        @B.gpsimd
        def _(g):
            for kc in range(8):
                if kc >= 1:
                    g.wait_ge(sem["hv"], hv0 + kc)
                    g.dma_start(out=w1bf[(kc - 1) * P:kc * P, :], in_=preb[:]).then_inc(sem["sg"], 16)
                    g.wait_ge(sem["sg"], inc("sg", 16))
                g.dma_start(out=preb[:], in_=q1[kc * P:(kc + 1) * P, :]).then_inc(sem["lg"], 16)
                inc("lg", 16)
                g.dma_start(out=pres_f, in_=s1[kc:kc + 1, :].broadcast_to([P, 4096])).then_inc(sem["lg"], 16)
                g.wait_ge(sem["lg"], inc("lg", 16))
                g.sem_inc(sem["ea"], 1)
                inc("ea")
            g.wait_ge(sem["hv"], hv0 + 8)
            g.dma_start(out=w1bf[7 * P:8 * P, :], in_=preb[:]).then_inc(sem["sg"], 16)
            g.wait_ge(sem["sg"], inc("sg", 16))
            for kc in range(32):
                if kc >= 1:
                    g.wait_ge(sem["hv"], hv0 + 8 + kc)
                    g.dma_start(out=w2bf[(kc - 1) * P:kc * P, :],
                                in_=preb[:, 1024:2048]).then_inc(sem["sg"], 16)
                    g.wait_ge(sem["sg"], inc("sg", 16))
                g.dma_start(out=preb[:, 0:1024], in_=q2[kc * P:(kc + 1) * P, :]).then_inc(sem["lg"], 16)
                inc("lg", 16)
                g.dma_start(out=pres_f[:, 0:1024], in_=s2[kc:kc + 1, :].broadcast_to([P, 1024])).then_inc(sem["lg"], 16)
                g.wait_ge(sem["lg"], inc("lg", 16))
                g.sem_inc(sem["ea"], 1)
                inc("ea")
            g.wait_ge(sem["hv"], hv0 + 40)
            g.dma_start(out=w2bf[31 * P:32 * P, :], in_=preb[:, 1024:2048]).then_inc(sem["sg"], 16)
            g.wait_ge(sem["sg"], inc("sg", 16))

        @B.vector
        def _(v):
            for kc in range(8):
                v.wait_ge(sem["ea"], ea0 + kc + 1)
                v.scalar_tensor_tensor(prew_v, preb[:], -128.0, pres_f, ALU.add, ALU.mult)
                v.tensor_scalar(preb[:], prew_v, gamc[:, kc:kc + 1], None, ALU.mult)
                v.drain().then_inc(sem["hv"], 1)
                inc("hv")
            for kc in range(32):
                v.wait_ge(sem["ea"], ea0 + 8 + kc + 1)
                v.scalar_tensor_tensor(prew_v[:, 0:1024], preb[:, 0:1024], -128.0,
                                       pres_f[:, 0:1024], ALU.add, ALU.mult)
                v.tensor_copy(preb[:, 1024:2048], prew_v[:, 0:1024])
                v.drain().then_inc(sem["hv"], 1)
                inc("hv")

    with nc.Block() as B:
        ea0, hv0, wf0 = cnt["ea"], cnt["hv"], cnt["wf"]

        @B.gpsimd
        def _(g):
            for kc in range(8):
                if kc >= 1:
                    g.wait_ge(sem["wf"], wf0 + kc)
                g.dma_start(out=preb[:], in_=q1[kc * P:(kc + 1) * P, :]).then_inc(sem["lg"], 16)
                inc("lg", 16)
                g.dma_start(out=pres_f, in_=s1[kc:kc + 1, :].broadcast_to([P, 4096])).then_inc(sem["lg"], 16)
                g.wait_ge(sem["lg"], inc("lg", 16))
                g.sem_inc(sem["ea"], 1)
                inc("ea")

        @B.vector
        def _(v):
            for kc in range(8):
                v.wait_ge(sem["ea"], ea0 + kc + 1)
                v.scalar_tensor_tensor(prew_v, preb[:], -128.0, pres_f, ALU.add, ALU.mult)
                v.tensor_scalar(preb[:], prew_v, betc[:, kc:kc + 1], None, ALU.mult)
                v.drain().then_inc(sem["hv"], 1)
                inc("hv")

        @B.tensor
        def _(t):
            for kc in range(8):
                t.wait_ge(sem["hv"], hv0 + kc + 1)
                for n in range(8):
                    mi = t.matmul(ps[n][0:1, :], ones1[:, 0:1], preb[:, n * 512:(n + 1) * 512],
                             start=(kc == 0), stop=(kc == 7))
                mi.then_inc(sem["wf"], 1)
                inc("wf")

    with nc.Block() as B:
        hv0b = cnt["hv"]
        row0b = cnt["row"]

        @B.vector
        def _(v):
            for n in range(8):
                v.tensor_copy(srow[0:1, 0:512], ps[n][0:1, :])
                v.drain().then_inc(sem["hv"], 1)
                inc("hv")
                v.wait_ge(sem["row"], row0b + (n + 1) * 32)
                v.tensor_add(b1eff[:, n * 4:(n + 1) * 4], b1eff[:, n * 4:(n + 1) * 4], mbc[:, 0:4])

        @B.sync
        def _(q):
            for n in range(8):
                q.wait_ge(sem["hv"], hv0b + n + 1)
                q.dma_start(out=rows32[0:1, 0:512], in_=srow[0:1, 0:512]).then_inc(sem["row"], 16)
                q.wait_ge(sem["row"], inc("row", 16))
                q.dma_start(out=mbc[:, 0:4],
                            in_=rows32[0:1, 0:512].rearrange("o (c p) -> (o p) c", p=P),
                            allow_slow_non_contiguous=True).then_inc(sem["row"], 16)
                q.wait_ge(sem["row"], inc("row", 16))

    for ch in range(NCH):
        T0 = ch * TC
        H0 = ch * TH

        if PHL < 1:
            pass
        else:
            with nc.Block() as B:
                @B.sync
                def _(q):
                    for dm in range(DM):
                        q.dma_start(out=xt[:, dm * TC:(dm + 1) * TC],
                                    in_=xbf[dm * P:(dm + 1) * P, T0:T0 + TC]).then_inc(sem["ld"], 16)
                        inc("ld", 16)
                    q.wait_ge(sem["ld"], cnt["ld"])

        if PHL < 2:
            pass
        else:
            with nc.Block() as B:
                ea0, mm0, hv0, cp0 = cnt["ea"], cnt["mm"], cnt["hv"], cnt["cp"]

                @B.scalar
                def _(s):
                    for kc in range(DM):
                        if kc >= 1:
                            s.wait_ge(sem["mm"], mm0 + kc)
                        s.activation(x2s[:, 0:TC],
                                     xt[:, kc * TC:(kc + 1) * TC], AF.Square,
                                     bias=0.0, scale=1.0).then_inc(sem["ea"], 1)
                        inc("ea")
                    s.wait_ge(sem["hv"], hv0 + 1)
                    s.activation(srow[0:1, 2 * TC:2 * TC + TC], srow[0:1, 1 * TC:1 * TC + TC], AF.Sqrt, bias=cst[0:1, 0:1], scale=1.0)
                    s.drain().then_inc(sem["cp"], 1)
                    inc("cp")

                @B.tensor
                def _(t):
                    for kc in range(DM):
                        for n in range(2):
                            t.matmul(ps[n][0:1, :], ones1[:, 0:1],
                                     xt[:, kc * TC + n * 512:kc * TC + (n + 1) * 512],
                                     start=(kc == 0), stop=(kc == DM - 1))
                        t.wait_ge(sem["ea"], ea0 + kc + 1)
                        for n in range(2):
                            mi = t.matmul(ps[2 + n][0:1, :], ones1[:, 0:1],
                                     x2s[:, n * 512:(n + 1) * 512],
                                     start=(kc == 0), stop=(kc == DM - 1))
                        mi.then_inc(sem["mm"], 1)
                        inc("mm")

                @B.vector
                def _(v):
                    v.wait_ge(sem["mm"], mm0 + DM)
                    v.tensor_copy(srow[0:1, 0:512], ps[0][0:1, :])
                    v.tensor_copy(srow[0:1, 512:1024], ps[1][0:1, :])
                    v.tensor_copy(srow[0:1, 1 * TC:1 * TC + 512], ps[2][0:1, :])
                    v.tensor_copy(srow[0:1, 1 * TC + 512:1 * TC + 1024], ps[3][0:1, :])
                    v.tensor_scalar_mul(srow[0:1, 0:TC], srow[0:1, 0:TC], 1.0 / 1024.0)
                    v.tensor_scalar_mul(srow[0:1, 1 * TC:1 * TC + TC], srow[0:1, 1 * TC:1 * TC + TC], 1.0 / 1024.0)
                    v.tensor_mul(srow[0:1, 2 * TC:2 * TC + TC], srow[0:1, 0:TC], srow[0:1, 0:TC])
                    v.tensor_sub(srow[0:1, 1 * TC:1 * TC + TC], srow[0:1, 1 * TC:1 * TC + TC], srow[0:1, 2 * TC:2 * TC + TC])
                    v.drain().then_inc(sem["hv"], 1)
                    inc("hv")
                    v.wait_ge(sem["cp"], cp0 + 1)
                    v.reciprocal(srow[0:1, 2 * TC:2 * TC + TC], srow[0:1, 2 * TC:2 * TC + TC])
                    v.tensor_mul(srow[0:1, 3 * TC:3 * TC + TC], srow[0:1, 0:TC], srow[0:1, 2 * TC:2 * TC + TC])
                    v.tensor_scalar_mul(srow[0:1, 3 * TC:3 * TC + TC], srow[0:1, 3 * TC:3 * TC + TC], -1.0)
                    v.drain().then_inc(sem["hv"], 1)
                    inc("hv")

                @B.sync
                def _(q):
                    q.wait_ge(sem["hv"], hv0 + 2)
                    q.dma_start(out=rows32[0:1, 2 * TC:4 * TC], in_=srow[0:1, 2 * TC:4 * TC]).then_inc(sem["row"], 16)
                    q.wait_ge(sem["row"], inc("row", 16))
                    q.dma_start(out=rbc[:], in_=rows32[0:1, 2 * TC:2 * TC + TC].broadcast_to([P, TC])).then_inc(sem["bc"], 16)
                    inc("bc", 16)
                    q.dma_start(out=mbc[:], in_=rows32[0:1, 3 * TC:3 * TC + TC].broadcast_to([P, TC])).then_inc(sem["bc"], 16)
                    inc("bc", 16)
                    q.wait_ge(sem["bc"], cnt["bc"])

        if PHL < 2:
            pass
        else:
            with nc.Block() as B:
                @B.vector
                def _(v):
                    u = u_v()
                    for dm in range(DM):
                        v.tensor_mul(u[:, dm * TC:(dm + 1) * TC], xt[:, dm * TC:(dm + 1) * TC], rbc[:])
                        v.tensor_add(u[:, dm * TC:(dm + 1) * TC], u[:, dm * TC:(dm + 1) * TC], mbc[:])

        if PHL < 3:
            pass
        else:
            with nc.Block() as B:
                wsA0, wsB0 = cnt["wsA"], cnt["wsB"]
                wf0, mm0, ev0 = cnt["wf"], cnt["mm"], cnt["ev"]

                @B.sync
                def _(q):
                    for i in range(32):
                        q4, kc = divmod(i, 8)
                        if i >= 2:
                            q.wait_ge(sem["wf"], wf0 + i - 1)
                        pn = "wsA" if i % 2 == 0 else "wsB"
                        q.dma_start(out=wst[i % 2][:, 0:512],
                                    in_=w_in_bf[kc * P:(kc + 1) * P, q4 * 512:(q4 + 1) * 512]
                                    ).then_inc(sem[pn], 16)
                        inc(pn, 16)

                @B.tensor
                def _(t):
                    u = u_v()
                    for i in range(32):
                        q4, kc = divmod(i, 8)
                        t.wait_ge(sem["wsA" if i % 2 == 0 else "wsB"],
                                  (wsA0 if i % 2 == 0 else wsB0) + (i // 2 + 1) * 16)
                        if kc == 0 and q4 >= 1:
                            t.wait_ge(sem["ev"], ev0 + q4)
                        for n in range(2):
                            for m in range(4):
                                mmi = t.matmul(ps[n * 4 + m][:],
                                               wst[i % 2][:, m * P:(m + 1) * P],
                                               u[:, kc * TC + n * 512:kc * TC + (n + 1) * 512],
                                               start=(kc == 0), stop=(kc == 7))
                        mmi.then_inc(sem["wf"], 1)
                        inc("wf")

                @B.vector
                def _(v):
                    for q4 in range(2):
                        v.wait_ge(sem["wf"], wf0 + (q4 + 1) * 8)
                        for n in range(2):
                            for m in range(4):
                                dc = q4 * 4 + m
                                base = dc * (TC + 3) + 3
                                v.tensor_copy(xin[:, base + n * 512:base + (n + 1) * 512],
                                              ps[n * 4 + m][:])
                        v.drain().then_inc(sem["ev"], 1)
                        inc("ev")

                @B.scalar
                def _(s):
                    for q4 in range(2, 4):
                        s.wait_ge(sem["wf"], wf0 + (q4 + 1) * 8)
                        for n in range(2):
                            for m in range(4):
                                dc = (q4 - 2) * 4 + m
                                s.activation(zg[:, dc * TC + n * 512:dc * TC + (n + 1) * 512],
                                             ps[n * 4 + m][:], AF.Silu, bias=0.0, scale=1.0)
                        s.drain().then_inc(sem["ev"], 1)
                        inc("ev")

        if PHL < 4:
            pass
        else:
            with nc.Block() as B:
                hv0 = cnt["hv"]

                @B.vector
                def _(v):
                    for dc in range(DC):
                        base = dc * (TC + 3)
                        v.tensor_copy(xin[:, base:base + 3], halo[:, dc * 3:(dc + 1) * 3])
                        v.tensor_copy(halo[:, dc * 3:(dc + 1) * 3], xin[:, base + TC:base + TC + 3])
                        v.tensor_scalar(xsi[:, dc * TC:(dc + 1) * TC], xin[:, base:base + TC],
                                        convw[:, dc * 4:dc * 4 + 1], None, ALU.mult)
                        for k in range(1, 4):
                            v.scalar_tensor_tensor(xsi[:, dc * TC:(dc + 1) * TC],
                                                   xin[:, base + k:base + k + TC],
                                                   convw[:, dc * 4 + k:dc * 4 + k + 1],
                                                   xsi[:, dc * TC:(dc + 1) * TC],
                                                   ALU.mult, ALU.add)
                        v.drain().then_inc(sem["hv"], 1)
                        inc("hv")

                @B.scalar
                def _(s):
                    for dc in range(DC):
                        s.wait_ge(sem["hv"], hv0 + dc + 1)
                        s.activation(xsi[:, dc * TC:(dc + 1) * TC], xsi[:, dc * TC:(dc + 1) * TC],
                                     AF.Silu, bias=convb[:, dc:dc + 1], scale=1.0)

        if PHL < 5:
            pass
        else:
            with nc.Block() as B:
                mm0, hv0, cc0, row0 = cnt["mm"], cnt["hv"], cnt["cc"], cnt["row"]

                @B.tensor
                def _(t):
                    for n in range(2):
                        for kc in range(DC):
                            mi = t.matmul(ps[n][0:33, :], w_x_sb[:, kc * 33:(kc + 1) * 33],
                                          xsi[:, kc * TC + n * 512:kc * TC + (n + 1) * 512],
                                          start=(kc == 0), stop=(kc == DC - 1))
                    mi.then_inc(sem["mm"], 1)
                    inc("mm")

                @B.vector
                def _(v):
                    v.wait_ge(sem["mm"], mm0 + 1)
                    v.tensor_copy(ssm_v[:, 0:512], ps[0][0:33, :])
                    v.tensor_copy(ssm_v[:, 512:1024], ps[1][0:33, :])
                    v.drain().then_inc(sem["hv"], 1)
                    inc("hv")

                @B.gpsimd
                def _(g):
                    g.wait_ge(sem["hv"], hv0 + 1)
                    g.dma_start(out=ssm_in, in_=ssm_v).then_inc(sem["lg"], 16)
                    g.wait_ge(sem["lg"], inc("lg", 16))
                    g.collective_compute("AllReduce", ALU.add, replica_groups=RG,
                                         ins=[ssm_in.opt()], outs=[ssm_out.opt()]).then_inc(sem["cc"], 1)
                    g.wait_ge(sem["cc"], inc("cc"))
                    g.dma_start(out=srow[0:1, 0:TC], in_=ssm_out[32:33, :]).then_inc(sem["lg"], 16)
                    inc("lg", 16)
                    g.dma_start(out=Bt_v, in_=ssm_out[0:16, :]).then_inc(sem["lg"], 16)
                    inc("lg", 16)
                    g.dma_start(out=Ct_v, in_=ssm_out[16:32, :]).then_inc(sem["lg"], 16)
                    g.wait_ge(sem["lg"], inc("lg", 16))

            with nc.Block() as B:
                ea0, hv0, mm0, row0 = cnt["ea"], cnt["hv"], cnt["mm"], cnt["row"]

                @B.scalar
                def _(s):
                    s.activation(srow[0:1, TC:2 * TC], srow[0:1, 0:TC], AF.Abs, bias=0.0, scale=1.0)
                    s.activation(srow[0:1, TC:2 * TC], srow[0:1, TC:2 * TC], AF.Exp, bias=0.0, scale=-1.0)
                    s.activation(srow[0:1, TC:2 * TC], srow[0:1, TC:2 * TC], AF.Ln, bias=cst[0:1, 1:2], scale=1.0)
                    s.activation(srow[0:1, 2 * TC:3 * TC], srow[0:1, 0:TC], AF.Relu, bias=0.0, scale=1.0)
                    s.drain().then_inc(sem["ea"], 1)
                    inc("ea")

                @B.vector
                def _(v):
                    v.wait_ge(sem["ea"], ea0 + 1)
                    v.tensor_add(srow[0:1, 0:TC], srow[0:1, TC:2 * TC], srow[0:1, 2 * TC:3 * TC])
                    v.tensor_copy(tmpb[0:1, :], srow[0:1, 0:TC])
                    v.drain().then_inc(sem["hv"], 1)
                    inc("hv")
                    v.wait_ge(sem["mm"], mm0 + 1)
                    for n in range(2):
                        v.tensor_tensor(dBt_v[:, n * 512:(n + 1) * 512],
                                        Bt_v[:, n * 512:(n + 1) * 512],
                                        ps[2 + n][0:16, :], ALU.mult)
                    v.tensor_copy(Cbt_v, Ct_v)
                    v.drain().then_inc(sem["hv"], 1)
                    inc("hv")

                @B.tensor
                def _(t):
                    t.wait_ge(sem["hv"], hv0 + 1)
                    for n in range(2):
                        mi = t.matmul(ps[2 + n][0:16, :], ones16[0:1, :],
                                      tmpb[0:1, n * 512:(n + 1) * 512], start=True, stop=True)
                    mi.then_inc(sem["mm"], 1)
                    inc("mm")

                @B.sync
                def _(q):
                    q.wait_ge(sem["hv"], hv0 + 2)
                    q.dma_start(out=rows16[0:16, :], in_=dBt_v).then_inc(sem["row"], 16)
                    inc("row", 16)
                    q.dma_start(out=rows16[16:32, :], in_=Cbt_v).then_inc(sem["row"], 16)
                    inc("row", 16)
                    q.dma_start(out=rows32[0:1, 0:TC], in_=srow[0:1, 0:TC]).then_inc(sem["row"], 16)
                    q.wait_ge(sem["row"], inc("row", 16))
                    q.dma_start(out=dbc_v, in_=rows32[0:1, 0:TC].broadcast_to([P, TC])).then_inc(sem["bc"], 16)
                    inc("bc", 16)
                    q.wait_ge(sem["bc"], cnt["bc"])

        if PHL < 6:
            pass
        else:
            with nc.Block() as B:
                bc0, ea0, hv0, cp0 = cnt["bc"], cnt["ea"], cnt["hv"], cnt["cp"]
                NG = NSG * DC

                def g_of(sg, dc):
                    return sg * DC + dc

                def hv_after_scan(g, si):
                    return hv0 + g * 5 + si + 1

                def hv_after_save(g):
                    return hv0 + g * 5 + 5

                bcA0, bcB0 = cnt["bcA"], cnt["bcB"]

                @B.sync
                def _(q):
                    for sg in range(NSG):
                        if sg >= 2:
                            q.wait_ge(sem["hv"], hv_after_save(g_of(sg - 2, DC - 1)))
                            q.wait_ge(sem["cp"], cp0 + (sg - 1) * DC)
                        pn = "bcA" if sg % 2 == 0 else "bcB"
                        for si in range(SG):
                            s_ = sg * SG + si
                            q.dma_start(out=dbb_v[sg % 2][:, si * TC:(si + 1) * TC],
                                        in_=rows16[s_:s_ + 1, :].broadcast_to([P, TC])
                                        ).then_inc(sem[pn], 16)
                            inc(pn, 16)
                            q.dma_start(out=cbb_v[sg % 2][:, si * TC:(si + 1) * TC],
                                        in_=rows16[16 + s_:17 + s_, :].broadcast_to([P, TC])
                                        ).then_inc(sem[pn], 16)
                            inc(pn, 16)

                @B.scalar
                def _(s):
                    k = 0
                    for sg in range(NSG):
                        for dc in range(DC):
                            for si in range(SG):
                                s_ = sg * SG + si
                                if k >= 2:
                                    km2 = k - 2
                                    g2, si2 = divmod(km2, SG)
                                    s.wait_ge(sem["hv"], hv_after_scan(g2, si2))
                                s.activation(dA_v[k % 2], dbc_v, AF.Exp, bias=0.0,
                                             scale=negA[:, dc * S + s_:dc * S + s_ + 1]
                                             ).then_inc(sem["ea"], 1)
                                inc("ea")
                                k += 1

                @B.vector
                def _(v):
                    k = 0
                    for sg in range(NSG):
                        v.wait_ge(sem["bcA" if sg % 2 == 0 else "bcB"],
                                  (bcA0 if sg % 2 == 0 else bcB0) + (sg // 2 + 1) * SG * 2 * 16)
                        for dc in range(DC):
                            g = g_of(sg, dc)
                            hb = hg_v[g % 2]
                            for si in range(SG):
                                s_ = sg * SG + si
                                v.tensor_tensor(pp_v[k % 2], xsi[:, dc * TC:(dc + 1) * TC],
                                                dbb_v[sg % 2][:, si * TC:(si + 1) * TC], ALU.mult)
                                v.wait_ge(sem["ea"], ea0 + k + 1)
                                v.tensor_tensor_scan(hb[:, si * TC:(si + 1) * TC],
                                                     dA_v[k % 2], pp_v[k % 2],
                                                     state[:, dc * S + s_:dc * S + s_ + 1],
                                                     ALU.mult, ALU.add).then_inc(sem["hv"], 1)
                                inc("hv")
                                k += 1
                            v.tensor_copy(state[:, dc * S + sg * SG:dc * S + (sg + 1) * SG],
                                          hb[:, TC - 1::TC]).then_inc(sem["hv"], 1)
                            inc("hv")
                            if g >= 1:
                                gp = g - 1
                                sgp, dcp = divmod(gp, DC)
                                hp = hg_v[gp % 2]
                                v.wait_ge(sem["cp"], cp0 + gp + 1)
                                v.tensor_add(hp[:, 0:TC], hp[:, 0:TC], hp[:, TC:2 * TC])
                                v.tensor_add(hp[:, 2 * TC:3 * TC], hp[:, 2 * TC:3 * TC], hp[:, 3 * TC:4 * TC])
                                v.tensor_add(hp[:, 0:TC], hp[:, 0:TC], hp[:, 2 * TC:3 * TC])
                                if sgp == 0:
                                    v.tensor_copy(yb[:, dcp * TC:(dcp + 1) * TC], hp[:, 0:TC])
                                else:
                                    v.tensor_add(yb[:, dcp * TC:(dcp + 1) * TC],
                                                 yb[:, dcp * TC:(dcp + 1) * TC], hp[:, 0:TC])
                    gp = NG - 1
                    sgp, dcp = divmod(gp, DC)
                    hp = hg_v[gp % 2]
                    v.wait_ge(sem["cp"], cp0 + gp + 1)
                    v.tensor_add(hp[:, 0:TC], hp[:, 0:TC], hp[:, TC:2 * TC])
                    v.tensor_add(hp[:, 2 * TC:3 * TC], hp[:, 2 * TC:3 * TC], hp[:, 3 * TC:4 * TC])
                    v.tensor_add(hp[:, 0:TC], hp[:, 0:TC], hp[:, 2 * TC:3 * TC])
                    v.tensor_add(yb[:, dcp * TC:(dcp + 1) * TC],
                                 yb[:, dcp * TC:(dcp + 1) * TC], hp[:, 0:TC])

                @B.gpsimd
                def _(g_):
                    for sg in range(NSG):
                        g_.wait_ge(sem["bcA" if sg % 2 == 0 else "bcB"],
                                   (bcA0 if sg % 2 == 0 else bcB0) + (sg // 2 + 1) * SG * 2 * 16)
                        for dc in range(DC):
                            g = g_of(sg, dc)
                            hb = hg_v[g % 2]
                            g_.wait_ge(sem["hv"], hv_after_save(g))
                            for si in range(SG):
                                g_.tensor_tensor(hb[:, si * TC:(si + 1) * TC],
                                                 hb[:, si * TC:(si + 1) * TC],
                                                 cbb_v[sg % 2][:, si * TC:(si + 1) * TC], ALU.mult)
                            g_.drain().then_inc(sem["cp"], 1)
                            inc("cp")

        if PHL < 7:
            pass
        else:
            with nc.Block() as B:
                @B.vector
                def _(v):
                    for dc in range(DC):
                        v.scalar_tensor_tensor(yb[:, dc * TC:(dc + 1) * TC],
                                               xsi[:, dc * TC:(dc + 1) * TC],
                                               ddc[:, dc:dc + 1],
                                               yb[:, dc * TC:(dc + 1) * TC], ALU.mult, ALU.add)
                        v.tensor_mul(yb[:, dc * TC:(dc + 1) * TC], yb[:, dc * TC:(dc + 1) * TC],
                                     zg[:, dc * TC:(dc + 1) * TC])

            with nc.Block() as B:
                wsA0, wsB0 = cnt["wsA"], cnt["wsB"]
                wf0, mm0, ev0 = cnt["wf"], cnt["mm"], cnt["ev"]

                @B.sync
                def _(q):
                    for i in range(16):
                        mh, kc = divmod(i, 8)
                        if i >= 2:
                            q.wait_ge(sem["wf"], wf0 + i - 1)
                        pn = "wsA" if i % 2 == 0 else "wsB"
                        q.dma_start(out=wst[i % 2][:],
                                    in_=w_out_bf[kc * P:(kc + 1) * P, :]).then_inc(sem[pn], 16)
                        inc(pn, 16)

                @B.tensor
                def _(t):
                    for i in range(16):
                        mh, kc = divmod(i, 8)
                        t.wait_ge(sem["wsA" if i % 2 == 0 else "wsB"],
                                  (wsA0 if i % 2 == 0 else wsB0) + (i // 2 + 1) * 16)
                        if kc == 0 and mh == 1:
                            t.wait_ge(sem["ev"], ev0 + 1)
                        for n in range(2):
                            for m in range(4):
                                mmi = t.matmul(ps[n * 4 + m][:],
                                         wst[i % 2][:, (mh * 4 + m) * P:(mh * 4 + m + 1) * P],
                                         yb[:, kc * TC + n * 512:kc * TC + (n + 1) * 512],
                                         start=(kc == 0), stop=(kc == 7))
                        mmi.then_inc(sem["wf"], 1)
                        inc("wf")

                hv0p7 = cnt["hv"]
                sdA0, sdB0 = cnt["sdA"], cnt["sdB"]

                @B.vector
                def _(v):
                    pieces = 0
                    for mh in range(2):
                        v.wait_ge(sem["wf"], wf0 + (mh + 1) * 8)
                        for n in range(2):
                            for m in range(4):
                                if pieces >= 2:
                                    v.wait_ge(sem["sdA" if pieces % 2 == 0 else "sdB"],
                                              (sdA0 if pieces % 2 == 0 else sdB0) + (pieces // 2) * 16)
                                v.tensor_copy(fstg_v[pieces % 2], ps[n * 4 + m][:])
                                v.drain().then_inc(sem["hv"], 1)
                                inc("hv")
                                pieces += 1
                        v.drain().then_inc(sem["ev"], 1)
                        inc("ev")

                @B.scalar
                def _(s):
                    pieces = 0
                    for mh in range(2):
                        for n in range(2):
                            for m in range(4):
                                dm = mh * 4 + m
                                s.wait_ge(sem["hv"], hv0p7 + pieces + 1)
                                pn = "sdA" if pieces % 2 == 0 else "sdB"
                                s.dma_start(out=mix_in[n, dm * P:(dm + 1) * P, :],
                                            in_=fstg_v[pieces % 2]).then_inc(sem[pn], 16)
                                inc(pn, 16)
                                pieces += 1

                @B.gpsimd
                def _(g):
                    for dm in range(DM):
                        g.dma_start(out=xo[:, dm * TH:(dm + 1) * TH],
                                    in_=xobf[dm * P:(dm + 1) * P, H0:H0 + TH]).then_inc(sem["lg"], 16)
                        inc("lg", 16)
                    g.wait_ge(sem["lg"], cnt["lg"])
                    g.wait_ge(sem["sdA"], sdA0 + 8 * 16)
                    g.wait_ge(sem["sdB"], sdB0 + 8 * 16)
                    g.collective_compute("ReduceScatter", ALU.add, replica_groups=RG,
                                         ins=[mix_in.opt()], outs=[mix_out.opt()]).then_inc(sem["cc"], 1)
                    g.wait_ge(sem["cc"], inc("cc"))
                    for dm in range(DM):
                        g.dma_start(out=mixh_v[:, dm * TH:(dm + 1) * TH],
                                    in_=mix_out[dm * P:(dm + 1) * P, :]).then_inc(sem["lg"], 16)
                        inc("lg", 16)
                    g.wait_ge(sem["lg"], cnt["lg"])

        if PHL < 8:
            pass
        else:
            def ln_block(src_bf, width, n_halves, extra=None):
                with nc.Block() as B:
                    ea0, mm0, hv0, cp0 = cnt["ea"], cnt["mm"], cnt["hv"], cnt["cp"]

                    @B.scalar
                    def _(s):
                        for kc in range(DM):
                            if kc >= 1:
                                s.wait_ge(sem["mm"], mm0 + kc)
                            s.activation(x2s[:, 0:width],
                                         src_bf[:, kc * width:(kc + 1) * width], AF.Square,
                                         bias=0.0, scale=1.0).then_inc(sem["ea"], 1)
                            inc("ea")
                        s.wait_ge(sem["hv"], hv0 + 1)
                        s.activation(srow[0:1, 2 * TC:2 * TC + width], srow[0:1, 1 * TC:1 * TC + width], AF.Sqrt,
                                     bias=cst[0:1, 0:1], scale=1.0)
                        s.drain().then_inc(sem["cp"], 1)
                        inc("cp")

                    @B.tensor
                    def _(t):
                        for kc in range(DM):
                            for n in range(n_halves):
                                t.matmul(ps[n][0:1, :], ones1[:, 0:1],
                                         src_bf[:, kc * width + n * 512:kc * width + (n + 1) * 512],
                                         start=(kc == 0), stop=(kc == DM - 1))
                            t.wait_ge(sem["ea"], ea0 + kc + 1)
                            for n in range(n_halves):
                                mi = t.matmul(ps[2 + n][0:1, :], ones1[:, 0:1],
                                         x2s[:, n * 512:(n + 1) * 512],
                                         start=(kc == 0), stop=(kc == DM - 1))
                            mi.then_inc(sem["mm"], 1)
                            inc("mm")

                    @B.vector
                    def _(v):
                        v.wait_ge(sem["mm"], mm0 + DM)
                        for n in range(n_halves):
                            v.tensor_copy(srow[0:1, n * 512:(n + 1) * 512], ps[n][0:1, :])
                            v.tensor_copy(srow[0:1, 1 * TC + n * 512:1 * TC + (n + 1) * 512], ps[2 + n][0:1, :])
                        w = width
                        v.tensor_scalar_mul(srow[0:1, 0:w], srow[0:1, 0:w], 1.0 / 1024.0)
                        v.tensor_scalar_mul(srow[0:1, 1 * TC:1 * TC + w], srow[0:1, 1 * TC:1 * TC + w], 1.0 / 1024.0)
                        v.tensor_mul(srow[0:1, 2 * TC:2 * TC + w], srow[0:1, 0:w], srow[0:1, 0:w])
                        v.tensor_sub(srow[0:1, 1 * TC:1 * TC + w], srow[0:1, 1 * TC:1 * TC + w], srow[0:1, 2 * TC:2 * TC + w])
                        v.drain().then_inc(sem["hv"], 1)
                        inc("hv")
                        v.wait_ge(sem["cp"], cp0 + 1)
                        v.reciprocal(srow[0:1, 2 * TC:2 * TC + w], srow[0:1, 2 * TC:2 * TC + w])
                        v.tensor_mul(srow[0:1, 3 * TC:3 * TC + w], srow[0:1, 0:w], srow[0:1, 2 * TC:2 * TC + w])
                        v.tensor_scalar_mul(srow[0:1, 3 * TC:3 * TC + w], srow[0:1, 3 * TC:3 * TC + w], -1.0)
                        if extra is not None:
                            extra(v)
                        v.drain().then_inc(sem["hv"], 1)
                        inc("hv")

                    @B.sync
                    def _(q):
                        q.wait_ge(sem["hv"], hv0 + 2)
                        q.dma_start(out=rows32[0:1, 2 * TC:4 * TC], in_=srow[0:1, 2 * TC:4 * TC]).then_inc(sem["row"], 16)
                        q.wait_ge(sem["row"], inc("row", 16))
                        q.dma_start(out=rbc[:, 0:width],
                                    in_=rows32[0:1, 2 * TC:2 * TC + width].broadcast_to([P, width])).then_inc(sem["bc"], 16)
                        inc("bc", 16)
                        q.dma_start(out=mbc[:, 0:width],
                                    in_=rows32[0:1, 3 * TC:3 * TC + width].broadcast_to([P, width])).then_inc(sem["bc"], 16)
                        inc("bc", 16)
                        q.wait_ge(sem["bc"], cnt["bc"])

            with nc.Block() as B:
                @B.vector
                def _(v):
                    for dm in range(DM):
                        v.tensor_copy(yb[:, dm * TH:(dm + 1) * TH], mixh_v[:, dm * TH:(dm + 1) * TH])

            mxb = yb[:, 0:DM * TH]
            ln_block(mxb, TH, 1)

            with nc.Block() as B:
                @B.vector
                def _(v):
                    for dm in range(DM):
                        v.tensor_mul(tmpb[:, 0:TH], mxb[:, dm * TH:(dm + 1) * TH], rbc[:, 0:TH])
                        v.tensor_add(tmpb[:, 0:TH], tmpb[:, 0:TH], mbc[:, 0:TH])
                        v.tensor_add(x1b[:, dm * TH:(dm + 1) * TH], tmpb[:, 0:TH],
                                     xo[:, dm * TH:(dm + 1) * TH])

        if PHL < 9:
            pass
        else:
            ln_block(x1b, TH, 1)

            with nc.Block() as B:
                ea0, hv0 = cnt["ea"], cnt["hv"]

                @B.vector
                def _(v):
                    v.tensor_mul(srow[0:1, 0:TH], srow[0:1, 2 * TC:2 * TC + TH], srow[0:1, 2 * TC:2 * TC + TH])
                    v.tensor_mul(srow[0:1, 0:TH], srow[0:1, 0:TH], srow[0:1, 1 * TC:1 * TC + TH])
                    v.drain().then_inc(sem["hv"], 1)
                    inc("hv")
                    v.wait_ge(sem["ea"], ea0 + 1)
                    v.reciprocal(srow[0:1, 0:TH], srow[0:1, 0:TH])

                @B.scalar
                def _(s):
                    s.wait_ge(sem["hv"], hv0 + 1)
                    s.activation(srow[0:1, 0:TH], srow[0:1, 0:TH], AF.Sqrt, bias=cst[0:1, 0:1], scale=1.0)
                    s.drain().then_inc(sem["ea"], 1)
                    inc("ea")

                @B.sync
                def _(q):
                    pass

            with nc.Block() as B:
                hv0 = cnt["hv"]

                @B.vector
                def _(v):
                    v.drain().then_inc(sem["hv"], 1)
                    inc("hv")

                @B.sync
                def _(q):
                    q.wait_ge(sem["hv"], cnt["hv"])
                    q.dma_start(out=rows32[0:1, 0:TH], in_=srow[0:1, 0:TH]).then_inc(sem["row"], 16)
                    q.wait_ge(sem["row"], inc("row", 16))
                    q.dma_start(out=r4bc_v, in_=rows32[0:1, 0:TH].broadcast_to([P, TH])).then_inc(sem["bc"], 16)
                    q.wait_ge(sem["bc"], inc("bc", 16))

            with nc.Block() as B:
                @B.vector
                def _(v):
                    hb = yb[:, 0:DM * TH]
                    for dm in range(DM):
                        v.tensor_mul(tmpb[:, 0:TH], x1b[:, dm * TH:(dm + 1) * TH], rbc[:, 0:TH])
                        v.tensor_add(u2b[:, dm * TH:(dm + 1) * TH], tmpb[:, 0:TH], mbc[:, 0:TH])
                        v.tensor_mul(hb[:, dm * TH:(dm + 1) * TH],
                                     u2b[:, dm * TH:(dm + 1) * TH], r4bc_v)

            with nc.Block() as B:
                wsA0, wsB0 = cnt["wsA"], cnt["wsB"]
                wf0, mm0, ev0 = cnt["wf"], cnt["mm"], cnt["ev"]
                hb = yb[:, 0:DM * TH]

                @B.sync
                def _(q):
                    for i in range(32):
                        mg, kc = divmod(i, 8)
                        if i >= 2:
                            q.wait_ge(sem["wf"], wf0 + i - 1)
                        pn = "wsA" if i % 2 == 0 else "wsB"
                        q.dma_start(out=wst[i % 2][:],
                                    in_=w1bf[kc * P:(kc + 1) * P, mg * 1024:(mg + 1) * 1024]
                                    ).then_inc(sem[pn], 16)
                        inc(pn, 16)

                @B.tensor
                def _(t):
                    for i in range(32):
                        mg, kc = divmod(i, 8)
                        t.wait_ge(sem["wsA" if i % 2 == 0 else "wsB"],
                                  (wsA0 if i % 2 == 0 else wsB0) + (i // 2 + 1) * 16)
                        if kc == 0 and mg >= 1:
                            t.wait_ge(sem["ev"], ev0 + mg)
                        for m in range(8):
                            mmi = t.matmul(ps[m][:], wst[i % 2][:, m * P:(m + 1) * P],
                                     hb[:, kc * TH:(kc + 1) * TH],
                                     start=(kc == 0), stop=(kc == 7))
                        mmi.then_inc(sem["wf"], 1)
                        inc("wf")

                @B.scalar
                def _(s):
                    for mg in range(4):
                        s.wait_ge(sem["wf"], wf0 + (mg + 1) * 8)
                        for m in range(8):
                            mc = mg * 8 + m
                            s.activation(ffnh_v[:, mc * TH:(mc + 1) * TH], ps[m][:],
                                         AF.Silu, bias=b1eff[:, mc:mc + 1], scale=1.0)
                        s.drain().then_inc(sem["ev"], 1)
                        inc("ev")

            with nc.Block() as B:
                wsA0, wsB0 = cnt["wsA"], cnt["wsB"]
                wf0, mm0, ea0 = cnt["wf"], cnt["mm"], cnt["ea"]

                @B.sync
                def _(q):
                    for kc in range(32):
                        if kc >= 2:
                            q.wait_ge(sem["wf"], wf0 + kc - 1)
                        pn = "wsA" if kc % 2 == 0 else "wsB"
                        q.dma_start(out=wst[kc % 2][:],
                                    in_=w2bf[kc * P:(kc + 1) * P, :]).then_inc(sem[pn], 16)
                        inc(pn, 16)

                @B.tensor
                def _(t):
                    for kc in range(32):
                        t.wait_ge(sem["wsA" if kc % 2 == 0 else "wsB"],
                                  (wsA0 if kc % 2 == 0 else wsB0) + (kc // 2 + 1) * 16)
                        for m in range(8):
                            mmi = t.matmul(ps[m][:], wst[kc % 2][:, m * P:(m + 1) * P],
                                     ffnh_v[:, kc * TH:(kc + 1) * TH],
                                     start=(kc == 0), stop=(kc == 31))
                        mmi.then_inc(sem["wf"], 1)
                        inc("wf")

                hv0m5 = cnt["hv"]

                @B.scalar
                def _(s):
                    s.wait_ge(sem["wf"], wf0 + 32)
                    for m in range(8):
                        if m >= 2:
                            s.wait_ge(sem["hv"], hv0m5 + m - 1)
                        s.activation(x2s[:, (m % 2) * TH:(m % 2) * TH + TH], ps[m][:],
                                     AF.Identity, bias=b2h[:, m:m + 1], scale=0.5)
                        s.drain().then_inc(sem["ea"], 1)
                        inc("ea")

                @B.vector
                def _(v):
                    for m in range(8):
                        v.wait_ge(sem["ea"], ea0 + m + 1)
                        v.tensor_add(o2b[:, m * TH:(m + 1) * TH],
                                     x2s[:, (m % 2) * TH:(m % 2) * TH + TH],
                                     u2b[:, m * TH:(m + 1) * TH])
                        v.drain().then_inc(sem["hv"], 1)
                        inc("hv")

            ln_block(o2b, TH, 1)

            with nc.Block() as B:
                osA0, osB0 = cnt["osA"], cnt["osB"]
                hv0f = cnt["hv"]

                @B.vector
                def _(v):
                    for dm in range(DM):
                        if dm >= 2:
                            v.wait_ge(sem["osA" if dm % 2 == 0 else "osB"],
                                      (osA0 if dm % 2 == 0 else osB0) + (dm // 2) * 16)
                        v.tensor_mul(tmpb[:, 0:TH], o2b[:, dm * TH:(dm + 1) * TH], rbc[:, 0:TH])
                        v.tensor_add(fstg_v[dm % 2], tmpb[:, 0:TH], mbc[:, 0:TH])
                        v.tensor_add(fstg_v[dm % 2], fstg_v[dm % 2], x1b[:, dm * TH:(dm + 1) * TH])
                        v.drain().then_inc(sem["hv"], 1)
                        inc("hv")

                @B.sync
                def _(q):
                    for dm in range(DM):
                        q.wait_ge(sem["hv"], hv0f + dm + 1)
                        pn = "osA" if dm % 2 == 0 else "osB"
                        q.dma_start(out=out[dm * P:(dm + 1) * P, H0:H0 + TH],
                                    in_=fstg_v[dm % 2]).then_inc(sem[pn], 16)
                        inc(pn, 16)

    with nc.Block() as B:
        @B.sync
        def _(q):
            q.wait_ge(sem["osA"], cnt["osA"])
            q.wait_ge(sem["osB"], cnt["osB"])

    nc.finalize()
    return nc


_NC_CACHE = None


def kernel(L0_attn_x, L0_attn_W_in, L0_attn_conv_w, L0_attn_conv_b, L0_attn_W_x,
           L0_attn_A_log, L0_attn_D, L0_attn_W_out, L0_ffn_ln_gamma, L0_ffn_ln_beta,
           L0_ffn_fc1_q, L0_ffn_fc1_scales, L0_ffn_fc1_bias, L0_ffn_fc2_q,
           L0_ffn_fc2_scales, L0_ffn_fc2_bias):
    global _NC_CACHE
    x = np.asarray(L0_attn_x, dtype=np.float32)
    W_in = np.ascontiguousarray(np.asarray(L0_attn_W_in, np.float32))
    conv_w = np.asarray(L0_attn_conv_w, np.float32)
    conv_b = np.asarray(L0_attn_conv_b, np.float32)
    W_x = np.asarray(L0_attn_W_x, np.float32)
    A_log = np.asarray(L0_attn_A_log, np.float32)
    Dv = np.asarray(L0_attn_D, np.float32)
    W_out = np.asarray(L0_attn_W_out, np.float32)
    gam = np.asarray(L0_ffn_ln_gamma, np.float32)
    bet = np.asarray(L0_ffn_ln_beta, np.float32)
    q1 = np.asarray(L0_ffn_fc1_q, np.int32)
    s1 = np.asarray(L0_ffn_fc1_scales, np.float32)
    b1 = np.asarray(L0_ffn_fc1_bias, np.float32)
    q2 = np.asarray(L0_ffn_fc2_q, np.int32)
    s2 = np.asarray(L0_ffn_fc2_scales, np.float32)
    b2 = np.asarray(L0_ffn_fc2_bias, np.float32)

    col = lambda v, nc_: np.ascontiguousarray(v.reshape(nc_, P).T)

    in_maps = []
    for c in range(8):
        b, j = divmod(c, 2)
        sl = slice(j * 1024, (j + 1) * 1024)
        xTb = np.ascontiguousarray(x[b].T)
        xoT = np.ascontiguousarray(
            xTb.reshape(1024, NCH, 2, TH)[:, :, j].reshape(1024, NCH * TH))
        w_in_slc = np.concatenate([W_in[sl], W_in[2048 + (j * 1024):2048 + (j + 1) * 1024]], 0)
        a_slc = A_log[sl]
        a_c = np.ascontiguousarray(a_slc.reshape(DC, P, S).transpose(1, 0, 2).reshape(P, DC * S))
        cw = conv_w[sl, 0, :]
        cw_c = np.ascontiguousarray(cw.reshape(DC, P, 4).transpose(1, 0, 2).reshape(P, DC * 4))
        in_maps.append({
            "xT": xTb,
            "xoT": xoT,
            "w_in_T": np.ascontiguousarray(w_in_slc.T),
            "w_x_T": np.ascontiguousarray(W_x[:, sl].T),
            "w_out_T": np.ascontiguousarray(W_out[:, sl].T),
            "a_log_c": a_c,
            "conv_w_c": cw_c,
            "conv_b_c": col(conv_b[sl], DC),
            "d_c": col(Dv[sl], DC),
            "gam_c": col(gam, DM),
            "bet_c": col(bet, DM),
            "q1": q1,
            "s1": s1,
            "b1_c": col(b1, 32),
            "q2": q2,
            "s2": s2,
            "b2_c": col(b2, DM),
        })

    if _NC_CACHE is None:
        _NC_CACHE = build_nc()
    res = run_bass_kernel_spmd(_NC_CACHE, in_maps, core_ids=list(range(8)))

    outp = np.empty((4, 8192, 1024), dtype=np.float32)
    for c in range(8):
        b, j = divmod(c, 2)
        o = res.results[c]["out"]
        o = o.reshape(1024, NCH, TH)
        for ch in range(NCH):
            outp[b, ch * TC + j * TH: ch * TC + (j + 1) * TH, :] = o[:, ch, :].T
    return outp


if __name__ == "__main__":
    import reference
    inp = reference.setup_inputs(0)
    got = kernel(**{k: np.asarray(v) for k, v in inp.items()})
    ref = np.asarray(reference.reference(**inp))
    num = float(((got - ref) ** 2).mean())
    den = float((ref ** 2).mean())
    print("res_var_ratio:", num / den)


# version 06
# speedup vs baseline: 1.0010x; 1.0010x over previous
import numpy as np
import ml_dtypes
import concourse.bass as bass
import concourse.mybir as mybir
from concourse.bass_utils import run_bass_kernel_spmd

F32 = mybir.dt.float32
BF16 = mybir.dt.bfloat16
I32 = mybir.dt.int32
AF = mybir.ActivationFunctionType
ALU = mybir.AluOpType

P = 128
TC = 1024
NCH = 8
TH = 512
DM = 8
DC = 8
S = 16
SG = 4
NSG = S // SG
EPS = 1e-5
RG = [[0, 1], [2, 3], [4, 5], [6, 7]]
PHL = 99


def build_nc():
    nc = bass.Bass("TRN2", target_bir_lowering=False, detect_race_conditions=False)
    dt = lambda n, s, d, k="Internal": nc.dram_tensor(n, s, d, kind=k).ap()

    xT = dt("xT", [1024, 8192], F32, "ExternalInput")
    xoT = dt("xoT", [1024, 4096], F32, "ExternalInput")
    w_in_T = dt("w_in_T", [1024, 2048], F32, "ExternalInput")
    w_x_T = dt("w_x_T", [1024, 33], F32, "ExternalInput")
    w_out_T = dt("w_out_T", [1024, 1024], F32, "ExternalInput")
    a_log_c = dt("a_log_c", [P, DC * S], F32, "ExternalInput")
    conv_w_c = dt("conv_w_c", [P, DC * 4], F32, "ExternalInput")
    conv_b_c = dt("conv_b_c", [P, DC], F32, "ExternalInput")
    d_c = dt("d_c", [P, DC], F32, "ExternalInput")
    gam_c = dt("gam_c", [P, DM], F32, "ExternalInput")
    bet_c = dt("bet_c", [P, DM], F32, "ExternalInput")
    q1 = dt("q1", [1024, 4096], I32, "ExternalInput")
    s1 = dt("s1", [8, 4096], F32, "ExternalInput")
    b1_c = dt("b1_c", [P, 32], F32, "ExternalInput")
    q2 = dt("q2", [4096, 1024], I32, "ExternalInput")
    s2 = dt("s2", [32, 1024], F32, "ExternalInput")
    b2_c = dt("b2_c", [P, DM], F32, "ExternalInput")
    out = dt("out", [1024, 4096], F32, "ExternalOutput")

    xbf = dt("xbf", [1024, 8192], BF16)
    xobf = dt("xobf", [1024, 4096], BF16)
    w_in_bf = dt("w_in_bf", [1024, 2048], BF16)
    w_out_bf = dt("w_out_bf", [1024, 1024], BF16)
    w1bf = dt("w1bf", [1024, 4096], BF16)
    w2bf = dt("w2bf", [4096, 1024], BF16)
    rows32 = dt("rows32", [1, 4 * TC], F32)
    rows16 = dt("rows16", [32, TC], BF16)
    ssm_in = dt("ssm_in", [33, TC], F32)
    ssm_out = dt("ssm_out", [33, TC], F32)
    mix_in = dt("mix_in", [2, 1024, TH], F32)
    mix_out = dt("mix_out", [1024, TH], F32)

    sbuf = nc.alloc_sbuf_tensor
    w_x_sb = sbuf("w_x_sb", [P, DC * 33], BF16)
    negA = sbuf("negA", [P, DC * S], F32)
    convw = sbuf("convw", [P, DC * 4], F32)
    convb = sbuf("convb", [P, DC], F32)
    ddc = sbuf("ddc", [P, DC], F32)
    gamc = sbuf("gamc", [P, DM], F32)
    b1eff = sbuf("b1eff", [P, 32], F32)
    b2h = sbuf("b2h", [P, DM], F32)
    betc = sbuf("betc", [P, DM], F32)
    ones1 = sbuf("ones1", [P, 1], BF16)
    ones16 = sbuf("ones16", [1, S], BF16)
    cst = sbuf("cst", [P, 2], F32)
    state = sbuf("state", [P, DC * S], F32)
    halo = sbuf("halo", [P, DC * 3], BF16)

    xt = sbuf("xt", [P, DM * TC], BF16)
    xin = sbuf("xin", [P, DC * (TC + 3)], BF16)
    xsi = sbuf("xsi", [P, DC * TC], BF16)
    zg = sbuf("zg", [P, DC * TC], BF16)
    yb = sbuf("yb", [P, DC * TC], BF16)
    x2s = sbuf("x2s", [P, TC], BF16)
    x1b = sbuf("x1b", [P, DM * TH], BF16)
    u2b = sbuf("u2b", [P, DM * TH], BF16)
    o2b = sbuf("o2b", [P, DM * TH], BF16)
    wst = [sbuf(f"wst{i}", [P, 1024], BF16) for i in range(2)]
    rbc = sbuf("rbc", [P, TC], F32)
    mbc = sbuf("mbc", [P, TC], F32)
    srow = sbuf("srow", [1, 4 * TC], F32)
    tmpb = sbuf("tmpb", [P, TC], BF16)

    A = sbuf("A", [P, 12288], F32)
    Ab = A[:].bitcast(BF16)
    u_v = lambda: Ab[:, 0:DM * TC]
    hg_v = [Ab[:, 0:SG * TC], Ab[:, SG * TC:2 * SG * TC]]
    dbb_v = [Ab[:, 8192 + i * SG * TC: 8192 + (i + 1) * SG * TC] for i in range(2)]
    cbb_v = [Ab[:, 16384 + i * SG * TC: 16384 + (i + 1) * SG * TC] for i in range(2)]
    ffnh_v = Ab[:, 0:32 * TH]
    xo = Ab[:, 16384:16384 + DM * TH]

    U = sbuf("U", [P, 4096], F32)
    Ub = U[:].bitcast(BF16)
    prew_v = U[:, 0:4096]
    ssm_v = U[0:33, 0:1024]
    Bt_v = U[0:16, 0:1024]
    Ct_v = U[0:16, 1024:2048]
    dBt_v = Ub[0:16, 4096:5120]
    Cbt_v = Ub[0:16, 5120:6144]
    dbc_v = U[:, 0:1024]
    dA_v = [U[:, 1024:2048], U[:, 2048:3072]]
    pp_v = [Ub[:, 6144:7168], Ub[:, 7168:8192]]
    mixh_v = U[:, 0:4096]
    fstg_v = [U[:, 0:512], U[:, 512:1024]]
    r4bc_v = U[:, 1024:1536]

    preb = xt[:, 0:4096]
    pres = xt[:, 4096:8192]
    pres_f = A[:, 0:4096]

    ps = [nc.alloc_psum_tensor(f"ps{i}", [P, 512], F32) for i in range(8)]

    names = ["ld", "wsA", "wsB", "row", "bc", "bcA", "bcB", "osA", "osB", "cc", "mm", "ev", "ea", "hv", "cp", "wf", "lg", "sg", "sdA", "sdB"]
    sem = {n: nc.alloc_semaphore(n) for n in names}
    cnt = {n: 0 for n in names}

    def inc(n, k=1):
        cnt[n] += k
        return cnt[n]

    with nc.Block() as B:
        @B.sync
        def _(q):
            for dst, src in [(negA, a_log_c), (convw, conv_w_c), (convb, conv_b_c),
                             (ddc, d_c), (gamc, gam_c), (betc, bet_c),
                             (b1eff, b1_c), (b2h, b2_c)]:
                q.dma_start(out=dst[:], in_=src).then_inc(sem["ld"], 16)
                inc("ld", 16)
            q.wait_ge(sem["ld"], cnt["ld"])

        @B.gpsimd
        def _(g):
            for kc in range(DC):
                g.dma_start(out=w_x_sb[:, kc * 33:(kc + 1) * 33],
                            in_=w_x_T[kc * P:(kc + 1) * P, :]).then_inc(sem["lg"], 16)
                inc("lg", 16)
            g.wait_ge(sem["lg"], cnt["lg"])

        @B.vector
        def _(v):
            v.memset(state[:], 0.0)
            v.memset(halo[:], 0.0)
            v.memset(ones1[:], 1.0)
            v.memset(ones16[:], 1.0)
            v.memset(cst[:, 0:1], EPS)
            v.memset(cst[:, 1:2], 1.0)

    with nc.Block() as B:
        @B.scalar
        def _(s):
            s.activation(negA[:], negA[:], AF.Exp, bias=0.0, scale=1.0)

    with nc.Block() as B:
        @B.vector
        def _(v):
            v.tensor_scalar_mul(negA[:], negA[:], -1.0)
            v.tensor_scalar_mul(b2h[:], b2h[:], 0.5)

    rounds = []
    for kc in range(8):
        for h in range(2):
            rounds.append((xT[kc * P:(kc + 1) * P, h * 4096:(h + 1) * 4096],
                           xbf[kc * P:(kc + 1) * P, h * 4096:(h + 1) * 4096], 4096))
    for kc in range(8):
        rounds.append((xoT[kc * P:(kc + 1) * P, :], xobf[kc * P:(kc + 1) * P, :], 4096))
    for kc in range(8):
        rounds.append((w_in_T[kc * P:(kc + 1) * P, :], w_in_bf[kc * P:(kc + 1) * P, :], 2048))
    for kc in range(8):
        rounds.append((w_out_T[kc * P:(kc + 1) * P, :], w_out_bf[kc * P:(kc + 1) * P, :], 1024))

    with nc.Block() as B:
        sg0 = cnt["sg"]

        @B.gpsimd
        def _(g):
            for i, (src, dst, n) in enumerate(rounds):
                bufn = (preb if i % 2 == 0 else pres)[:, 0:n]
                if i >= 2:
                    g.wait_ge(sem["sg"], sg0 + (i - 1) * 16)
                g.dma_start(out=bufn, in_=src).then_inc(sem["lg"], 16)
                g.wait_ge(sem["lg"], inc("lg", 16))
                if i >= 1:
                    g.wait_ge(sem["sg"], sg0 + i * 16)
                g.dma_start(out=dst, in_=bufn).then_inc(sem["sg"], 16)
                inc("sg", 16)
            g.wait_ge(sem["sg"], cnt["sg"])

    with nc.Block() as B:
        ea0, hv0 = cnt["ea"], cnt["hv"]

        @B.gpsimd
        def _(g):
            for kc in range(8):
                if kc >= 1:
                    g.wait_ge(sem["hv"], hv0 + kc)
                    g.dma_start(out=w1bf[(kc - 1) * P:kc * P, :], in_=preb[:]).then_inc(sem["sg"], 16)
                    g.wait_ge(sem["sg"], inc("sg", 16))
                g.dma_start(out=preb[:], in_=q1[kc * P:(kc + 1) * P, :]).then_inc(sem["lg"], 16)
                inc("lg", 16)
                g.dma_start(out=pres_f, in_=s1[kc:kc + 1, :].broadcast_to([P, 4096])).then_inc(sem["lg"], 16)
                g.wait_ge(sem["lg"], inc("lg", 16))
                g.sem_inc(sem["ea"], 1)
                inc("ea")
            g.wait_ge(sem["hv"], hv0 + 8)
            g.dma_start(out=w1bf[7 * P:8 * P, :], in_=preb[:]).then_inc(sem["sg"], 16)
            g.wait_ge(sem["sg"], inc("sg", 16))
            for kc in range(32):
                if kc >= 1:
                    g.wait_ge(sem["hv"], hv0 + 8 + kc)
                    g.dma_start(out=w2bf[(kc - 1) * P:kc * P, :],
                                in_=preb[:, 1024:2048]).then_inc(sem["sg"], 16)
                    g.wait_ge(sem["sg"], inc("sg", 16))
                g.dma_start(out=preb[:, 0:1024], in_=q2[kc * P:(kc + 1) * P, :]).then_inc(sem["lg"], 16)
                inc("lg", 16)
                g.dma_start(out=pres_f[:, 0:1024], in_=s2[kc:kc + 1, :].broadcast_to([P, 1024])).then_inc(sem["lg"], 16)
                g.wait_ge(sem["lg"], inc("lg", 16))
                g.sem_inc(sem["ea"], 1)
                inc("ea")
            g.wait_ge(sem["hv"], hv0 + 40)
            g.dma_start(out=w2bf[31 * P:32 * P, :], in_=preb[:, 1024:2048]).then_inc(sem["sg"], 16)
            g.wait_ge(sem["sg"], inc("sg", 16))

        @B.vector
        def _(v):
            for kc in range(8):
                v.wait_ge(sem["ea"], ea0 + kc + 1)
                v.scalar_tensor_tensor(prew_v, preb[:], -128.0, pres_f, ALU.add, ALU.mult)
                v.tensor_scalar(preb[:], prew_v, gamc[:, kc:kc + 1], None, ALU.mult)
                v.drain().then_inc(sem["hv"], 1)
                inc("hv")
            for kc in range(32):
                v.wait_ge(sem["ea"], ea0 + 8 + kc + 1)
                v.scalar_tensor_tensor(prew_v[:, 0:1024], preb[:, 0:1024], -128.0,
                                       pres_f[:, 0:1024], ALU.add, ALU.mult)
                v.tensor_copy(preb[:, 1024:2048], prew_v[:, 0:1024])
                v.drain().then_inc(sem["hv"], 1)
                inc("hv")

    with nc.Block() as B:
        ea0, hv0, wf0 = cnt["ea"], cnt["hv"], cnt["wf"]

        @B.gpsimd
        def _(g):
            for kc in range(8):
                if kc >= 1:
                    g.wait_ge(sem["wf"], wf0 + kc)
                g.dma_start(out=preb[:], in_=q1[kc * P:(kc + 1) * P, :]).then_inc(sem["lg"], 16)
                inc("lg", 16)
                g.dma_start(out=pres_f, in_=s1[kc:kc + 1, :].broadcast_to([P, 4096])).then_inc(sem["lg"], 16)
                g.wait_ge(sem["lg"], inc("lg", 16))
                g.sem_inc(sem["ea"], 1)
                inc("ea")

        @B.vector
        def _(v):
            for kc in range(8):
                v.wait_ge(sem["ea"], ea0 + kc + 1)
                v.scalar_tensor_tensor(prew_v, preb[:], -128.0, pres_f, ALU.add, ALU.mult)
                v.tensor_scalar(preb[:], prew_v, betc[:, kc:kc + 1], None, ALU.mult)
                v.drain().then_inc(sem["hv"], 1)
                inc("hv")

        @B.tensor
        def _(t):
            for kc in range(8):
                t.wait_ge(sem["hv"], hv0 + kc + 1)
                for n in range(8):
                    mi = t.matmul(ps[n][0:1, :], ones1[:, 0:1], preb[:, n * 512:(n + 1) * 512],
                             start=(kc == 0), stop=(kc == 7))
                mi.then_inc(sem["wf"], 1)
                inc("wf")

    with nc.Block() as B:
        hv0b = cnt["hv"]
        row0b = cnt["row"]

        @B.vector
        def _(v):
            for n in range(8):
                v.tensor_copy(srow[0:1, 0:512], ps[n][0:1, :])
                v.drain().then_inc(sem["hv"], 1)
                inc("hv")
                v.wait_ge(sem["row"], row0b + (n + 1) * 32)
                v.tensor_add(b1eff[:, n * 4:(n + 1) * 4], b1eff[:, n * 4:(n + 1) * 4], mbc[:, 0:4])

        @B.sync
        def _(q):
            for n in range(8):
                q.wait_ge(sem["hv"], hv0b + n + 1)
                q.dma_start(out=rows32[0:1, 0:512], in_=srow[0:1, 0:512]).then_inc(sem["row"], 16)
                q.wait_ge(sem["row"], inc("row", 16))
                q.dma_start(out=mbc[:, 0:4],
                            in_=rows32[0:1, 0:512].rearrange("o (c p) -> (o p) c", p=P),
                            allow_slow_non_contiguous=True).then_inc(sem["row"], 16)
                q.wait_ge(sem["row"], inc("row", 16))

    for ch in range(NCH):
        T0 = ch * TC
        H0 = ch * TH

        if PHL < 1:
            pass
        else:
            with nc.Block() as B:
                @B.sync
                def _(q):
                    for dm in range(DM):
                        q.dma_start(out=xt[:, dm * TC:(dm + 1) * TC],
                                    in_=xbf[dm * P:(dm + 1) * P, T0:T0 + TC]).then_inc(sem["ld"], 16)
                        inc("ld", 16)
                    q.wait_ge(sem["ld"], cnt["ld"])

        if PHL < 2:
            pass
        else:
            with nc.Block() as B:
                ea0, mm0, hv0, cp0 = cnt["ea"], cnt["mm"], cnt["hv"], cnt["cp"]

                @B.scalar
                def _(s):
                    for kc in range(DM):
                        if kc >= 1:
                            s.wait_ge(sem["mm"], mm0 + kc)
                        s.activation(x2s[:, 0:TC],
                                     xt[:, kc * TC:(kc + 1) * TC], AF.Square,
                                     bias=0.0, scale=1.0).then_inc(sem["ea"], 1)
                        inc("ea")
                    s.wait_ge(sem["hv"], hv0 + 1)
                    s.activation(srow[0:1, 2 * TC:2 * TC + TC], srow[0:1, 1 * TC:1 * TC + TC], AF.Sqrt, bias=cst[0:1, 0:1], scale=1.0)
                    s.drain().then_inc(sem["cp"], 1)
                    inc("cp")

                @B.tensor
                def _(t):
                    for kc in range(DM):
                        for n in range(2):
                            t.matmul(ps[n][0:1, :], ones1[:, 0:1],
                                     xt[:, kc * TC + n * 512:kc * TC + (n + 1) * 512],
                                     start=(kc == 0), stop=(kc == DM - 1))
                        t.wait_ge(sem["ea"], ea0 + kc + 1)
                        for n in range(2):
                            mi = t.matmul(ps[2 + n][0:1, :], ones1[:, 0:1],
                                     x2s[:, n * 512:(n + 1) * 512],
                                     start=(kc == 0), stop=(kc == DM - 1))
                        mi.then_inc(sem["mm"], 1)
                        inc("mm")

                @B.vector
                def _(v):
                    v.wait_ge(sem["mm"], mm0 + DM)
                    v.tensor_copy(srow[0:1, 0:512], ps[0][0:1, :])
                    v.tensor_copy(srow[0:1, 512:1024], ps[1][0:1, :])
                    v.tensor_copy(srow[0:1, 1 * TC:1 * TC + 512], ps[2][0:1, :])
                    v.tensor_copy(srow[0:1, 1 * TC + 512:1 * TC + 1024], ps[3][0:1, :])
                    v.tensor_scalar_mul(srow[0:1, 0:TC], srow[0:1, 0:TC], 1.0 / 1024.0)
                    v.tensor_scalar_mul(srow[0:1, 1 * TC:1 * TC + TC], srow[0:1, 1 * TC:1 * TC + TC], 1.0 / 1024.0)
                    v.tensor_mul(srow[0:1, 2 * TC:2 * TC + TC], srow[0:1, 0:TC], srow[0:1, 0:TC])
                    v.tensor_sub(srow[0:1, 1 * TC:1 * TC + TC], srow[0:1, 1 * TC:1 * TC + TC], srow[0:1, 2 * TC:2 * TC + TC])
                    v.drain().then_inc(sem["hv"], 1)
                    inc("hv")
                    v.wait_ge(sem["cp"], cp0 + 1)
                    v.reciprocal(srow[0:1, 2 * TC:2 * TC + TC], srow[0:1, 2 * TC:2 * TC + TC])
                    v.tensor_mul(srow[0:1, 3 * TC:3 * TC + TC], srow[0:1, 0:TC], srow[0:1, 2 * TC:2 * TC + TC])
                    v.tensor_scalar_mul(srow[0:1, 3 * TC:3 * TC + TC], srow[0:1, 3 * TC:3 * TC + TC], -1.0)
                    v.drain().then_inc(sem["hv"], 1)
                    inc("hv")

                @B.sync
                def _(q):
                    q.wait_ge(sem["hv"], hv0 + 2)
                    q.dma_start(out=rows32[0:1, 2 * TC:4 * TC], in_=srow[0:1, 2 * TC:4 * TC]).then_inc(sem["row"], 16)
                    q.wait_ge(sem["row"], inc("row", 16))
                    q.dma_start(out=rbc[:], in_=rows32[0:1, 2 * TC:2 * TC + TC].broadcast_to([P, TC])).then_inc(sem["bc"], 16)
                    inc("bc", 16)
                    q.dma_start(out=mbc[:], in_=rows32[0:1, 3 * TC:3 * TC + TC].broadcast_to([P, TC])).then_inc(sem["bc"], 16)
                    inc("bc", 16)
                    q.wait_ge(sem["bc"], cnt["bc"])

        if PHL < 2:
            pass
        else:
            with nc.Block() as B:
                @B.vector
                def _(v):
                    u = u_v()
                    for dm in range(DM):
                        v.tensor_mul(u[:, dm * TC:(dm + 1) * TC], xt[:, dm * TC:(dm + 1) * TC], rbc[:])
                        v.tensor_add(u[:, dm * TC:(dm + 1) * TC], u[:, dm * TC:(dm + 1) * TC], mbc[:])

        if PHL < 3:
            pass
        else:
            with nc.Block() as B:
                wsA0, wsB0 = cnt["wsA"], cnt["wsB"]
                wf0, mm0, ev0 = cnt["wf"], cnt["mm"], cnt["ev"]

                @B.sync
                def _(q):
                    for i in range(32):
                        q4, kc = divmod(i, 8)
                        if i >= 2:
                            q.wait_ge(sem["wf"], wf0 + i - 1)
                        pn = "wsA" if i % 2 == 0 else "wsB"
                        q.dma_start(out=wst[i % 2][:, 0:512],
                                    in_=w_in_bf[kc * P:(kc + 1) * P, q4 * 512:(q4 + 1) * 512]
                                    ).then_inc(sem[pn], 16)
                        inc(pn, 16)

                @B.tensor
                def _(t):
                    u = u_v()
                    for i in range(32):
                        q4, kc = divmod(i, 8)
                        t.wait_ge(sem["wsA" if i % 2 == 0 else "wsB"],
                                  (wsA0 if i % 2 == 0 else wsB0) + (i // 2 + 1) * 16)
                        if kc == 0 and q4 >= 1:
                            t.wait_ge(sem["ev"], ev0 + q4)
                        for n in range(2):
                            for m in range(4):
                                mmi = t.matmul(ps[n * 4 + m][:],
                                               wst[i % 2][:, m * P:(m + 1) * P],
                                               u[:, kc * TC + n * 512:kc * TC + (n + 1) * 512],
                                               start=(kc == 0), stop=(kc == 7))
                        mmi.then_inc(sem["wf"], 1)
                        inc("wf")

                @B.vector
                def _(v):
                    for q4 in range(2):
                        v.wait_ge(sem["wf"], wf0 + (q4 + 1) * 8)
                        for n in range(2):
                            for m in range(4):
                                dc = q4 * 4 + m
                                base = dc * (TC + 3) + 3
                                v.tensor_copy(xin[:, base + n * 512:base + (n + 1) * 512],
                                              ps[n * 4 + m][:])
                        v.drain().then_inc(sem["ev"], 1)
                        inc("ev")

                @B.scalar
                def _(s):
                    for q4 in range(2, 4):
                        s.wait_ge(sem["wf"], wf0 + (q4 + 1) * 8)
                        for n in range(2):
                            for m in range(4):
                                dc = (q4 - 2) * 4 + m
                                s.activation(zg[:, dc * TC + n * 512:dc * TC + (n + 1) * 512],
                                             ps[n * 4 + m][:], AF.Silu, bias=0.0, scale=1.0)
                        s.drain().then_inc(sem["ev"], 1)
                        inc("ev")

        if PHL < 4:
            pass
        else:
            with nc.Block() as B:
                hv0 = cnt["hv"]

                @B.vector
                def _(v):
                    for dc in range(DC):
                        base = dc * (TC + 3)
                        v.tensor_copy(xin[:, base:base + 3], halo[:, dc * 3:(dc + 1) * 3])
                        v.tensor_copy(halo[:, dc * 3:(dc + 1) * 3], xin[:, base + TC:base + TC + 3])
                        v.tensor_scalar(xsi[:, dc * TC:(dc + 1) * TC], xin[:, base:base + TC],
                                        convw[:, dc * 4:dc * 4 + 1], None, ALU.mult)
                        for k in range(1, 4):
                            v.scalar_tensor_tensor(xsi[:, dc * TC:(dc + 1) * TC],
                                                   xin[:, base + k:base + k + TC],
                                                   convw[:, dc * 4 + k:dc * 4 + k + 1],
                                                   xsi[:, dc * TC:(dc + 1) * TC],
                                                   ALU.mult, ALU.add)
                        v.drain().then_inc(sem["hv"], 1)
                        inc("hv")

                @B.scalar
                def _(s):
                    for dc in range(DC):
                        s.wait_ge(sem["hv"], hv0 + dc + 1)
                        s.activation(xsi[:, dc * TC:(dc + 1) * TC], xsi[:, dc * TC:(dc + 1) * TC],
                                     AF.Silu, bias=convb[:, dc:dc + 1], scale=1.0)

        if PHL < 5:
            pass
        else:
            with nc.Block() as B:
                mm0, hv0, cc0, row0 = cnt["mm"], cnt["hv"], cnt["cc"], cnt["row"]

                @B.tensor
                def _(t):
                    for n in range(2):
                        for kc in range(DC):
                            mi = t.matmul(ps[n][0:33, :], w_x_sb[:, kc * 33:(kc + 1) * 33],
                                          xsi[:, kc * TC + n * 512:kc * TC + (n + 1) * 512],
                                          start=(kc == 0), stop=(kc == DC - 1))
                    mi.then_inc(sem["mm"], 1)
                    inc("mm")

                @B.vector
                def _(v):
                    v.wait_ge(sem["mm"], mm0 + 1)
                    v.tensor_copy(ssm_v[:, 0:512], ps[0][0:33, :])
                    v.tensor_copy(ssm_v[:, 512:1024], ps[1][0:33, :])
                    v.drain().then_inc(sem["hv"], 1)
                    inc("hv")

                @B.gpsimd
                def _(g):
                    g.wait_ge(sem["hv"], hv0 + 1)
                    g.dma_start(out=ssm_in, in_=ssm_v).then_inc(sem["lg"], 16)
                    g.wait_ge(sem["lg"], inc("lg", 16))
                    g.collective_compute("AllReduce", ALU.add, replica_groups=RG,
                                         ins=[ssm_in.opt()], outs=[ssm_out.opt()]).then_inc(sem["cc"], 1)
                    g.wait_ge(sem["cc"], inc("cc"))
                    g.dma_start(out=srow[0:1, 0:TC], in_=ssm_out[32:33, :]).then_inc(sem["lg"], 16)
                    inc("lg", 16)
                    g.dma_start(out=Bt_v, in_=ssm_out[0:16, :]).then_inc(sem["lg"], 16)
                    inc("lg", 16)
                    g.dma_start(out=Ct_v, in_=ssm_out[16:32, :]).then_inc(sem["lg"], 16)
                    g.wait_ge(sem["lg"], inc("lg", 16))

            with nc.Block() as B:
                ea0, hv0, mm0, row0 = cnt["ea"], cnt["hv"], cnt["mm"], cnt["row"]

                @B.scalar
                def _(s):
                    s.activation(srow[0:1, TC:2 * TC], srow[0:1, 0:TC], AF.Abs, bias=0.0, scale=1.0)
                    s.activation(srow[0:1, TC:2 * TC], srow[0:1, TC:2 * TC], AF.Exp, bias=0.0, scale=-1.0)
                    s.activation(srow[0:1, TC:2 * TC], srow[0:1, TC:2 * TC], AF.Ln, bias=cst[0:1, 1:2], scale=1.0)
                    s.activation(srow[0:1, 2 * TC:3 * TC], srow[0:1, 0:TC], AF.Relu, bias=0.0, scale=1.0)
                    s.drain().then_inc(sem["ea"], 1)
                    inc("ea")

                @B.vector
                def _(v):
                    v.wait_ge(sem["ea"], ea0 + 1)
                    v.tensor_add(srow[0:1, 0:TC], srow[0:1, TC:2 * TC], srow[0:1, 2 * TC:3 * TC])
                    v.tensor_copy(tmpb[0:1, :], srow[0:1, 0:TC])
                    v.drain().then_inc(sem["hv"], 1)
                    inc("hv")
                    v.wait_ge(sem["mm"], mm0 + 1)
                    for n in range(2):
                        v.tensor_tensor(dBt_v[:, n * 512:(n + 1) * 512],
                                        Bt_v[:, n * 512:(n + 1) * 512],
                                        ps[2 + n][0:16, :], ALU.mult)
                    v.tensor_copy(Cbt_v, Ct_v)
                    v.drain().then_inc(sem["hv"], 1)
                    inc("hv")

                @B.tensor
                def _(t):
                    t.wait_ge(sem["hv"], hv0 + 1)
                    for n in range(2):
                        mi = t.matmul(ps[2 + n][0:16, :], ones16[0:1, :],
                                      tmpb[0:1, n * 512:(n + 1) * 512], start=True, stop=True)
                    mi.then_inc(sem["mm"], 1)
                    inc("mm")

                @B.sync
                def _(q):
                    q.wait_ge(sem["hv"], hv0 + 2)
                    q.dma_start(out=rows16[0:16, :], in_=dBt_v).then_inc(sem["row"], 16)
                    inc("row", 16)
                    q.dma_start(out=rows16[16:32, :], in_=Cbt_v).then_inc(sem["row"], 16)
                    inc("row", 16)
                    q.dma_start(out=rows32[0:1, 0:TC], in_=srow[0:1, 0:TC]).then_inc(sem["row"], 16)
                    q.wait_ge(sem["row"], inc("row", 16))
                    q.dma_start(out=dbc_v, in_=rows32[0:1, 0:TC].broadcast_to([P, TC])).then_inc(sem["bc"], 16)
                    inc("bc", 16)
                    q.wait_ge(sem["bc"], cnt["bc"])

        if PHL < 6:
            pass
        else:
            with nc.Block() as B:
                bc0, ea0, hv0, cp0 = cnt["bc"], cnt["ea"], cnt["hv"], cnt["cp"]
                NG = NSG * DC

                def g_of(sg, dc):
                    return sg * DC + dc

                def hv_after_scan(g, si):
                    return hv0 + g * 5 + si + 1

                def hv_after_save(g):
                    return hv0 + g * 5 + 5

                bcA0, bcB0 = cnt["bcA"], cnt["bcB"]

                @B.sync
                def _(q):
                    for sg in range(NSG):
                        if sg >= 2:
                            q.wait_ge(sem["hv"], hv_after_save(g_of(sg - 2, DC - 1)))
                            q.wait_ge(sem["cp"], cp0 + (sg - 1) * DC)
                        pn = "bcA" if sg % 2 == 0 else "bcB"
                        for si in range(SG):
                            s_ = sg * SG + si
                            q.dma_start(out=dbb_v[sg % 2][:, si * TC:(si + 1) * TC],
                                        in_=rows16[s_:s_ + 1, :].broadcast_to([P, TC])
                                        ).then_inc(sem[pn], 16)
                            inc(pn, 16)
                            q.dma_start(out=cbb_v[sg % 2][:, si * TC:(si + 1) * TC],
                                        in_=rows16[16 + s_:17 + s_, :].broadcast_to([P, TC])
                                        ).then_inc(sem[pn], 16)
                            inc(pn, 16)

                @B.scalar
                def _(s):
                    k = 0
                    for sg in range(NSG):
                        for dc in range(DC):
                            for si in range(SG):
                                s_ = sg * SG + si
                                if k >= 2:
                                    km2 = k - 2
                                    g2, si2 = divmod(km2, SG)
                                    s.wait_ge(sem["hv"], hv_after_scan(g2, si2))
                                s.activation(dA_v[k % 2], dbc_v, AF.Exp, bias=0.0,
                                             scale=negA[:, dc * S + s_:dc * S + s_ + 1]
                                             ).then_inc(sem["ea"], 1)
                                inc("ea")
                                k += 1

                @B.vector
                def _(v):
                    k = 0
                    for sg in range(NSG):
                        v.wait_ge(sem["bcA" if sg % 2 == 0 else "bcB"],
                                  (bcA0 if sg % 2 == 0 else bcB0) + (sg // 2 + 1) * SG * 2 * 16)
                        for dc in range(DC):
                            g = g_of(sg, dc)
                            hb = hg_v[g % 2]
                            for si in range(SG):
                                s_ = sg * SG + si
                                v.tensor_tensor(pp_v[k % 2], xsi[:, dc * TC:(dc + 1) * TC],
                                                dbb_v[sg % 2][:, si * TC:(si + 1) * TC], ALU.mult)
                                v.wait_ge(sem["ea"], ea0 + k + 1)
                                v.tensor_tensor_scan(hb[:, si * TC:(si + 1) * TC],
                                                     dA_v[k % 2], pp_v[k % 2],
                                                     state[:, dc * S + s_:dc * S + s_ + 1],
                                                     ALU.mult, ALU.add).then_inc(sem["hv"], 1)
                                inc("hv")
                                k += 1
                            v.tensor_copy(state[:, dc * S + sg * SG:dc * S + (sg + 1) * SG],
                                          hb[:, TC - 1::TC]).then_inc(sem["hv"], 1)
                            inc("hv")
                            if g >= 1:
                                gp = g - 1
                                sgp, dcp = divmod(gp, DC)
                                hp = hg_v[gp % 2]
                                v.wait_ge(sem["cp"], cp0 + gp + 1)
                                v.tensor_add(hp[:, 0:TC], hp[:, 0:TC], hp[:, TC:2 * TC])
                                v.tensor_add(hp[:, 2 * TC:3 * TC], hp[:, 2 * TC:3 * TC], hp[:, 3 * TC:4 * TC])
                                v.tensor_add(hp[:, 0:TC], hp[:, 0:TC], hp[:, 2 * TC:3 * TC])
                                if sgp == 0:
                                    v.tensor_copy(yb[:, dcp * TC:(dcp + 1) * TC], hp[:, 0:TC])
                                else:
                                    v.tensor_add(yb[:, dcp * TC:(dcp + 1) * TC],
                                                 yb[:, dcp * TC:(dcp + 1) * TC], hp[:, 0:TC])
                    gp = NG - 1
                    sgp, dcp = divmod(gp, DC)
                    hp = hg_v[gp % 2]
                    v.wait_ge(sem["cp"], cp0 + gp + 1)
                    v.tensor_add(hp[:, 0:TC], hp[:, 0:TC], hp[:, TC:2 * TC])
                    v.tensor_add(hp[:, 2 * TC:3 * TC], hp[:, 2 * TC:3 * TC], hp[:, 3 * TC:4 * TC])
                    v.tensor_add(hp[:, 0:TC], hp[:, 0:TC], hp[:, 2 * TC:3 * TC])
                    v.tensor_add(yb[:, dcp * TC:(dcp + 1) * TC],
                                 yb[:, dcp * TC:(dcp + 1) * TC], hp[:, 0:TC])

                @B.gpsimd
                def _(g_):
                    for sg in range(NSG):
                        g_.wait_ge(sem["bcA" if sg % 2 == 0 else "bcB"],
                                   (bcA0 if sg % 2 == 0 else bcB0) + (sg // 2 + 1) * SG * 2 * 16)
                        for dc in range(DC):
                            g = g_of(sg, dc)
                            hb = hg_v[g % 2]
                            g_.wait_ge(sem["hv"], hv_after_save(g))
                            for si in range(SG):
                                g_.tensor_tensor(hb[:, si * TC:(si + 1) * TC],
                                                 hb[:, si * TC:(si + 1) * TC],
                                                 cbb_v[sg % 2][:, si * TC:(si + 1) * TC], ALU.mult)
                            g_.drain().then_inc(sem["cp"], 1)
                            inc("cp")

        if PHL < 7:
            pass
        else:
            with nc.Block() as B:
                @B.vector
                def _(v):
                    for dc in range(DC):
                        v.scalar_tensor_tensor(yb[:, dc * TC:(dc + 1) * TC],
                                               xsi[:, dc * TC:(dc + 1) * TC],
                                               ddc[:, dc:dc + 1],
                                               yb[:, dc * TC:(dc + 1) * TC], ALU.mult, ALU.add)
                        v.tensor_mul(yb[:, dc * TC:(dc + 1) * TC], yb[:, dc * TC:(dc + 1) * TC],
                                     zg[:, dc * TC:(dc + 1) * TC])

            with nc.Block() as B:
                wsA0, wsB0 = cnt["wsA"], cnt["wsB"]
                wf0, mm0, ev0 = cnt["wf"], cnt["mm"], cnt["ev"]

                @B.sync
                def _(q):
                    for i in range(16):
                        mh, kc = divmod(i, 8)
                        if i >= 2:
                            q.wait_ge(sem["wf"], wf0 + i - 1)
                        pn = "wsA" if i % 2 == 0 else "wsB"
                        q.dma_start(out=wst[i % 2][:],
                                    in_=w_out_bf[kc * P:(kc + 1) * P, :]).then_inc(sem[pn], 16)
                        inc(pn, 16)

                @B.tensor
                def _(t):
                    for i in range(16):
                        mh, kc = divmod(i, 8)
                        t.wait_ge(sem["wsA" if i % 2 == 0 else "wsB"],
                                  (wsA0 if i % 2 == 0 else wsB0) + (i // 2 + 1) * 16)
                        if kc == 0 and mh == 1:
                            t.wait_ge(sem["ev"], ev0 + 1)
                        for n in range(2):
                            for m in range(4):
                                mmi = t.matmul(ps[n * 4 + m][:],
                                         wst[i % 2][:, (mh * 4 + m) * P:(mh * 4 + m + 1) * P],
                                         yb[:, kc * TC + n * 512:kc * TC + (n + 1) * 512],
                                         start=(kc == 0), stop=(kc == 7))
                        mmi.then_inc(sem["wf"], 1)
                        inc("wf")

                hv0p7 = cnt["hv"]
                sdA0, sdB0 = cnt["sdA"], cnt["sdB"]

                @B.scalar
                def _(s):
                    pieces = 0
                    for mh in range(2):
                        s.wait_ge(sem["wf"], wf0 + (mh + 1) * 8)
                        for n in range(2):
                            for m in range(4):
                                dm = mh * 4 + m
                                if pieces >= 2:
                                    s.wait_ge(sem["sdA" if pieces % 2 == 0 else "sdB"],
                                              (sdA0 if pieces % 2 == 0 else sdB0) + (pieces // 2) * 16)
                                s.activation(fstg_v[pieces % 2], ps[n * 4 + m][:],
                                             AF.Identity, bias=0.0, scale=1.0)
                                if n == 1 and m == 3:
                                    s.drain().then_inc(sem["ev"], 1)
                                    inc("ev")
                                else:
                                    s.drain()
                                pn = "sdA" if pieces % 2 == 0 else "sdB"
                                s.dma_start(out=mix_in[n, dm * P:(dm + 1) * P, :],
                                            in_=fstg_v[pieces % 2]).then_inc(sem[pn], 16)
                                inc(pn, 16)
                                pieces += 1

                @B.gpsimd
                def _(g):
                    for dm in range(DM):
                        g.dma_start(out=xo[:, dm * TH:(dm + 1) * TH],
                                    in_=xobf[dm * P:(dm + 1) * P, H0:H0 + TH]).then_inc(sem["lg"], 16)
                        inc("lg", 16)
                    g.wait_ge(sem["lg"], cnt["lg"])
                    g.wait_ge(sem["sdA"], sdA0 + 8 * 16)
                    g.wait_ge(sem["sdB"], sdB0 + 8 * 16)
                    g.collective_compute("ReduceScatter", ALU.add, replica_groups=RG,
                                         ins=[mix_in.opt()], outs=[mix_out.opt()]).then_inc(sem["cc"], 1)
                    g.wait_ge(sem["cc"], inc("cc"))
                    for dm in range(DM):
                        g.dma_start(out=mixh_v[:, dm * TH:(dm + 1) * TH],
                                    in_=mix_out[dm * P:(dm + 1) * P, :]).then_inc(sem["lg"], 16)
                        inc("lg", 16)
                    g.wait_ge(sem["lg"], cnt["lg"])

        if PHL < 8:
            pass
        else:
            def ln_block(src_bf, width, n_halves, extra=None):
                with nc.Block() as B:
                    ea0, mm0, hv0, cp0 = cnt["ea"], cnt["mm"], cnt["hv"], cnt["cp"]

                    @B.scalar
                    def _(s):
                        for kc in range(DM):
                            if kc >= 1:
                                s.wait_ge(sem["mm"], mm0 + kc)
                            s.activation(x2s[:, 0:width],
                                         src_bf[:, kc * width:(kc + 1) * width], AF.Square,
                                         bias=0.0, scale=1.0).then_inc(sem["ea"], 1)
                            inc("ea")
                        s.wait_ge(sem["hv"], hv0 + 1)
                        s.activation(srow[0:1, 2 * TC:2 * TC + width], srow[0:1, 1 * TC:1 * TC + width], AF.Sqrt,
                                     bias=cst[0:1, 0:1], scale=1.0)
                        s.drain().then_inc(sem["cp"], 1)
                        inc("cp")

                    @B.tensor
                    def _(t):
                        for kc in range(DM):
                            for n in range(n_halves):
                                t.matmul(ps[n][0:1, :], ones1[:, 0:1],
                                         src_bf[:, kc * width + n * 512:kc * width + (n + 1) * 512],
                                         start=(kc == 0), stop=(kc == DM - 1))
                            t.wait_ge(sem["ea"], ea0 + kc + 1)
                            for n in range(n_halves):
                                mi = t.matmul(ps[2 + n][0:1, :], ones1[:, 0:1],
                                         x2s[:, n * 512:(n + 1) * 512],
                                         start=(kc == 0), stop=(kc == DM - 1))
                            mi.then_inc(sem["mm"], 1)
                            inc("mm")

                    @B.vector
                    def _(v):
                        v.wait_ge(sem["mm"], mm0 + DM)
                        for n in range(n_halves):
                            v.tensor_copy(srow[0:1, n * 512:(n + 1) * 512], ps[n][0:1, :])
                            v.tensor_copy(srow[0:1, 1 * TC + n * 512:1 * TC + (n + 1) * 512], ps[2 + n][0:1, :])
                        w = width
                        v.tensor_scalar_mul(srow[0:1, 0:w], srow[0:1, 0:w], 1.0 / 1024.0)
                        v.tensor_scalar_mul(srow[0:1, 1 * TC:1 * TC + w], srow[0:1, 1 * TC:1 * TC + w], 1.0 / 1024.0)
                        v.tensor_mul(srow[0:1, 2 * TC:2 * TC + w], srow[0:1, 0:w], srow[0:1, 0:w])
                        v.tensor_sub(srow[0:1, 1 * TC:1 * TC + w], srow[0:1, 1 * TC:1 * TC + w], srow[0:1, 2 * TC:2 * TC + w])
                        v.drain().then_inc(sem["hv"], 1)
                        inc("hv")
                        v.wait_ge(sem["cp"], cp0 + 1)
                        v.reciprocal(srow[0:1, 2 * TC:2 * TC + w], srow[0:1, 2 * TC:2 * TC + w])
                        v.tensor_mul(srow[0:1, 3 * TC:3 * TC + w], srow[0:1, 0:w], srow[0:1, 2 * TC:2 * TC + w])
                        v.tensor_scalar_mul(srow[0:1, 3 * TC:3 * TC + w], srow[0:1, 3 * TC:3 * TC + w], -1.0)
                        if extra is not None:
                            extra(v)
                        v.drain().then_inc(sem["hv"], 1)
                        inc("hv")

                    @B.sync
                    def _(q):
                        q.wait_ge(sem["hv"], hv0 + 2)
                        q.dma_start(out=rows32[0:1, 2 * TC:4 * TC], in_=srow[0:1, 2 * TC:4 * TC]).then_inc(sem["row"], 16)
                        q.wait_ge(sem["row"], inc("row", 16))
                        q.dma_start(out=rbc[:, 0:width],
                                    in_=rows32[0:1, 2 * TC:2 * TC + width].broadcast_to([P, width])).then_inc(sem["bc"], 16)
                        inc("bc", 16)
                        q.dma_start(out=mbc[:, 0:width],
                                    in_=rows32[0:1, 3 * TC:3 * TC + width].broadcast_to([P, width])).then_inc(sem["bc"], 16)
                        inc("bc", 16)
                        q.wait_ge(sem["bc"], cnt["bc"])

            with nc.Block() as B:
                @B.vector
                def _(v):
                    for dm in range(DM):
                        v.tensor_copy(yb[:, dm * TH:(dm + 1) * TH], mixh_v[:, dm * TH:(dm + 1) * TH])

            mxb = yb[:, 0:DM * TH]
            ln_block(mxb, TH, 1)

            with nc.Block() as B:
                @B.vector
                def _(v):
                    for dm in range(DM):
                        v.tensor_mul(tmpb[:, 0:TH], mxb[:, dm * TH:(dm + 1) * TH], rbc[:, 0:TH])
                        v.tensor_add(tmpb[:, 0:TH], tmpb[:, 0:TH], mbc[:, 0:TH])
                        v.tensor_add(x1b[:, dm * TH:(dm + 1) * TH], tmpb[:, 0:TH],
                                     xo[:, dm * TH:(dm + 1) * TH])

        if PHL < 9:
            pass
        else:
            ln_block(x1b, TH, 1)

            with nc.Block() as B:
                ea0, hv0 = cnt["ea"], cnt["hv"]

                @B.vector
                def _(v):
                    v.tensor_mul(srow[0:1, 0:TH], srow[0:1, 2 * TC:2 * TC + TH], srow[0:1, 2 * TC:2 * TC + TH])
                    v.tensor_mul(srow[0:1, 0:TH], srow[0:1, 0:TH], srow[0:1, 1 * TC:1 * TC + TH])
                    v.drain().then_inc(sem["hv"], 1)
                    inc("hv")
                    v.wait_ge(sem["ea"], ea0 + 1)
                    v.reciprocal(srow[0:1, 0:TH], srow[0:1, 0:TH])

                @B.scalar
                def _(s):
                    s.wait_ge(sem["hv"], hv0 + 1)
                    s.activation(srow[0:1, 0:TH], srow[0:1, 0:TH], AF.Sqrt, bias=cst[0:1, 0:1], scale=1.0)
                    s.drain().then_inc(sem["ea"], 1)
                    inc("ea")

                @B.sync
                def _(q):
                    pass

            with nc.Block() as B:
                hv0 = cnt["hv"]

                @B.vector
                def _(v):
                    v.drain().then_inc(sem["hv"], 1)
                    inc("hv")

                @B.sync
                def _(q):
                    q.wait_ge(sem["hv"], cnt["hv"])
                    q.dma_start(out=rows32[0:1, 0:TH], in_=srow[0:1, 0:TH]).then_inc(sem["row"], 16)
                    q.wait_ge(sem["row"], inc("row", 16))
                    q.dma_start(out=r4bc_v, in_=rows32[0:1, 0:TH].broadcast_to([P, TH])).then_inc(sem["bc"], 16)
                    q.wait_ge(sem["bc"], inc("bc", 16))

            with nc.Block() as B:
                @B.vector
                def _(v):
                    hb = yb[:, 0:DM * TH]
                    for dm in range(DM):
                        v.tensor_mul(tmpb[:, 0:TH], x1b[:, dm * TH:(dm + 1) * TH], rbc[:, 0:TH])
                        v.tensor_add(u2b[:, dm * TH:(dm + 1) * TH], tmpb[:, 0:TH], mbc[:, 0:TH])
                        v.tensor_mul(hb[:, dm * TH:(dm + 1) * TH],
                                     u2b[:, dm * TH:(dm + 1) * TH], r4bc_v)

            with nc.Block() as B:
                wsA0, wsB0 = cnt["wsA"], cnt["wsB"]
                wf0, mm0, ev0 = cnt["wf"], cnt["mm"], cnt["ev"]
                hb = yb[:, 0:DM * TH]

                @B.sync
                def _(q):
                    for i in range(32):
                        mg, kc = divmod(i, 8)
                        if i >= 2:
                            q.wait_ge(sem["wf"], wf0 + i - 1)
                        pn = "wsA" if i % 2 == 0 else "wsB"
                        q.dma_start(out=wst[i % 2][:],
                                    in_=w1bf[kc * P:(kc + 1) * P, mg * 1024:(mg + 1) * 1024]
                                    ).then_inc(sem[pn], 16)
                        inc(pn, 16)

                @B.tensor
                def _(t):
                    for i in range(32):
                        mg, kc = divmod(i, 8)
                        t.wait_ge(sem["wsA" if i % 2 == 0 else "wsB"],
                                  (wsA0 if i % 2 == 0 else wsB0) + (i // 2 + 1) * 16)
                        if kc == 0 and mg >= 1:
                            t.wait_ge(sem["ev"], ev0 + mg)
                        for m in range(8):
                            mmi = t.matmul(ps[m][:], wst[i % 2][:, m * P:(m + 1) * P],
                                     hb[:, kc * TH:(kc + 1) * TH],
                                     start=(kc == 0), stop=(kc == 7))
                        mmi.then_inc(sem["wf"], 1)
                        inc("wf")

                @B.scalar
                def _(s):
                    for mg in range(4):
                        s.wait_ge(sem["wf"], wf0 + (mg + 1) * 8)
                        for m in range(8):
                            mc = mg * 8 + m
                            s.activation(ffnh_v[:, mc * TH:(mc + 1) * TH], ps[m][:],
                                         AF.Silu, bias=b1eff[:, mc:mc + 1], scale=1.0)
                        s.drain().then_inc(sem["ev"], 1)
                        inc("ev")

            with nc.Block() as B:
                wsA0, wsB0 = cnt["wsA"], cnt["wsB"]
                wf0, mm0, ea0 = cnt["wf"], cnt["mm"], cnt["ea"]

                @B.sync
                def _(q):
                    for kc in range(32):
                        if kc >= 2:
                            q.wait_ge(sem["wf"], wf0 + kc - 1)
                        pn = "wsA" if kc % 2 == 0 else "wsB"
                        q.dma_start(out=wst[kc % 2][:],
                                    in_=w2bf[kc * P:(kc + 1) * P, :]).then_inc(sem[pn], 16)
                        inc(pn, 16)

                @B.tensor
                def _(t):
                    for kc in range(32):
                        t.wait_ge(sem["wsA" if kc % 2 == 0 else "wsB"],
                                  (wsA0 if kc % 2 == 0 else wsB0) + (kc // 2 + 1) * 16)
                        for m in range(8):
                            mmi = t.matmul(ps[m][:], wst[kc % 2][:, m * P:(m + 1) * P],
                                     ffnh_v[:, kc * TH:(kc + 1) * TH],
                                     start=(kc == 0), stop=(kc == 31))
                        mmi.then_inc(sem["wf"], 1)
                        inc("wf")

                hv0m5 = cnt["hv"]

                @B.scalar
                def _(s):
                    s.wait_ge(sem["wf"], wf0 + 32)
                    for m in range(8):
                        if m >= 2:
                            s.wait_ge(sem["hv"], hv0m5 + m - 1)
                        s.activation(x2s[:, (m % 2) * TH:(m % 2) * TH + TH], ps[m][:],
                                     AF.Identity, bias=b2h[:, m:m + 1], scale=0.5)
                        s.drain().then_inc(sem["ea"], 1)
                        inc("ea")

                @B.vector
                def _(v):
                    for m in range(8):
                        v.wait_ge(sem["ea"], ea0 + m + 1)
                        v.tensor_add(o2b[:, m * TH:(m + 1) * TH],
                                     x2s[:, (m % 2) * TH:(m % 2) * TH + TH],
                                     u2b[:, m * TH:(m + 1) * TH])
                        v.drain().then_inc(sem["hv"], 1)
                        inc("hv")

            ln_block(o2b, TH, 1)

            with nc.Block() as B:
                osA0, osB0 = cnt["osA"], cnt["osB"]
                hv0f = cnt["hv"]

                @B.vector
                def _(v):
                    for dm in range(DM):
                        if dm >= 2:
                            v.wait_ge(sem["osA" if dm % 2 == 0 else "osB"],
                                      (osA0 if dm % 2 == 0 else osB0) + (dm // 2) * 16)
                        v.tensor_mul(tmpb[:, 0:TH], o2b[:, dm * TH:(dm + 1) * TH], rbc[:, 0:TH])
                        v.tensor_add(fstg_v[dm % 2], tmpb[:, 0:TH], mbc[:, 0:TH])
                        v.tensor_add(fstg_v[dm % 2], fstg_v[dm % 2], x1b[:, dm * TH:(dm + 1) * TH])
                        v.drain().then_inc(sem["hv"], 1)
                        inc("hv")

                @B.sync
                def _(q):
                    for dm in range(DM):
                        q.wait_ge(sem["hv"], hv0f + dm + 1)
                        pn = "osA" if dm % 2 == 0 else "osB"
                        q.dma_start(out=out[dm * P:(dm + 1) * P, H0:H0 + TH],
                                    in_=fstg_v[dm % 2]).then_inc(sem[pn], 16)
                        inc(pn, 16)

    with nc.Block() as B:
        @B.sync
        def _(q):
            q.wait_ge(sem["osA"], cnt["osA"])
            q.wait_ge(sem["osB"], cnt["osB"])

    nc.finalize()
    return nc


_NC_CACHE = None


def kernel(L0_attn_x, L0_attn_W_in, L0_attn_conv_w, L0_attn_conv_b, L0_attn_W_x,
           L0_attn_A_log, L0_attn_D, L0_attn_W_out, L0_ffn_ln_gamma, L0_ffn_ln_beta,
           L0_ffn_fc1_q, L0_ffn_fc1_scales, L0_ffn_fc1_bias, L0_ffn_fc2_q,
           L0_ffn_fc2_scales, L0_ffn_fc2_bias):
    global _NC_CACHE
    x = np.asarray(L0_attn_x, dtype=np.float32)
    W_in = np.ascontiguousarray(np.asarray(L0_attn_W_in, np.float32))
    conv_w = np.asarray(L0_attn_conv_w, np.float32)
    conv_b = np.asarray(L0_attn_conv_b, np.float32)
    W_x = np.asarray(L0_attn_W_x, np.float32)
    A_log = np.asarray(L0_attn_A_log, np.float32)
    Dv = np.asarray(L0_attn_D, np.float32)
    W_out = np.asarray(L0_attn_W_out, np.float32)
    gam = np.asarray(L0_ffn_ln_gamma, np.float32)
    bet = np.asarray(L0_ffn_ln_beta, np.float32)
    q1 = np.asarray(L0_ffn_fc1_q, np.int32)
    s1 = np.asarray(L0_ffn_fc1_scales, np.float32)
    b1 = np.asarray(L0_ffn_fc1_bias, np.float32)
    q2 = np.asarray(L0_ffn_fc2_q, np.int32)
    s2 = np.asarray(L0_ffn_fc2_scales, np.float32)
    b2 = np.asarray(L0_ffn_fc2_bias, np.float32)

    col = lambda v, nc_: np.ascontiguousarray(v.reshape(nc_, P).T)

    in_maps = []
    for c in range(8):
        b, j = divmod(c, 2)
        sl = slice(j * 1024, (j + 1) * 1024)
        xTb = np.ascontiguousarray(x[b].T)
        xoT = np.ascontiguousarray(
            xTb.reshape(1024, NCH, 2, TH)[:, :, j].reshape(1024, NCH * TH))
        w_in_slc = np.concatenate([W_in[sl], W_in[2048 + (j * 1024):2048 + (j + 1) * 1024]], 0)
        a_slc = A_log[sl]
        a_c = np.ascontiguousarray(a_slc.reshape(DC, P, S).transpose(1, 0, 2).reshape(P, DC * S))
        cw = conv_w[sl, 0, :]
        cw_c = np.ascontiguousarray(cw.reshape(DC, P, 4).transpose(1, 0, 2).reshape(P, DC * 4))
        in_maps.append({
            "xT": xTb,
            "xoT": xoT,
            "w_in_T": np.ascontiguousarray(w_in_slc.T),
            "w_x_T": np.ascontiguousarray(W_x[:, sl].T),
            "w_out_T": np.ascontiguousarray(W_out[:, sl].T),
            "a_log_c": a_c,
            "conv_w_c": cw_c,
            "conv_b_c": col(conv_b[sl], DC),
            "d_c": col(Dv[sl], DC),
            "gam_c": col(gam, DM),
            "bet_c": col(bet, DM),
            "q1": q1,
            "s1": s1,
            "b1_c": col(b1, 32),
            "q2": q2,
            "s2": s2,
            "b2_c": col(b2, DM),
        })

    if _NC_CACHE is None:
        _NC_CACHE = build_nc()
    res = run_bass_kernel_spmd(_NC_CACHE, in_maps, core_ids=list(range(8)))

    outp = np.empty((4, 8192, 1024), dtype=np.float32)
    for c in range(8):
        b, j = divmod(c, 2)
        o = res.results[c]["out"]
        o = o.reshape(1024, NCH, TH)
        for ch in range(NCH):
            outp[b, ch * TC + j * TH: ch * TC + (j + 1) * TH, :] = o[:, ch, :].T
    return outp


if __name__ == "__main__":
    import reference
    inp = reference.setup_inputs(0)
    got = kernel(**{k: np.asarray(v) for k, v in inp.items()})
    ref = np.asarray(reference.reference(**inp))
    num = float(((got - ref) ** 2).mean())
    den = float((ref ** 2).mean())
    print("res_var_ratio:", num / den)


# version 08 (final)
# speedup vs baseline: 1.1100x; 1.1089x over previous
import numpy as np
import ml_dtypes
import concourse.bass as bass
import concourse.mybir as mybir
from concourse.bass_utils import run_bass_kernel_spmd

F32 = mybir.dt.float32
BF16 = mybir.dt.bfloat16
I32 = mybir.dt.int32
AF = mybir.ActivationFunctionType
ALU = mybir.AluOpType

P = 128
TC = 1024
NCH = 8
TH = 512
DM = 8
DC = 8
S = 16
SG = 4
NSG = S // SG
EPS = 1e-5
RG = [[0, 1], [2, 3], [4, 5], [6, 7]]
PHL = 99


def build_nc():
    nc = bass.Bass("TRN2", target_bir_lowering=False, detect_race_conditions=False)
    dt = lambda n, s, d, k="Internal": nc.dram_tensor(n, s, d, kind=k).ap()

    xT = dt("xT", [1024, 8192], F32, "ExternalInput")
    xoT = dt("xoT", [1024, 4096], F32, "ExternalInput")
    w_in_T = dt("w_in_T", [1024, 2048], F32, "ExternalInput")
    w_x_T = dt("w_x_T", [1024, 33], F32, "ExternalInput")
    w_out_T = dt("w_out_T", [1024, 1024], F32, "ExternalInput")
    a_log_c = dt("a_log_c", [P, DC * S], F32, "ExternalInput")
    conv_w_c = dt("conv_w_c", [P, DC * 4], F32, "ExternalInput")
    conv_b_c = dt("conv_b_c", [P, DC], F32, "ExternalInput")
    d_c = dt("d_c", [P, DC], F32, "ExternalInput")
    gam_c = dt("gam_c", [P, DM], F32, "ExternalInput")
    bet_c = dt("bet_c", [P, DM], F32, "ExternalInput")
    q1 = dt("q1", [1024, 4096], I32, "ExternalInput")
    s1 = dt("s1", [8, 4096], F32, "ExternalInput")
    b1_c = dt("b1_c", [P, 32], F32, "ExternalInput")
    q2 = dt("q2", [4096, 1024], I32, "ExternalInput")
    s2 = dt("s2", [32, 1024], F32, "ExternalInput")
    b2_c = dt("b2_c", [P, DM], F32, "ExternalInput")
    out = dt("out", [1024, 4096], F32, "ExternalOutput")

    xbf = dt("xbf", [1024, 8192], BF16)
    xobf = dt("xobf", [1024, 4096], BF16)
    w_in_bf = dt("w_in_bf", [1024, 2048], BF16)
    w_out_bf = dt("w_out_bf", [1024, 1024], BF16)
    w1bf = dt("w1bf", [1024, 4096], BF16)
    w2bf = dt("w2bf", [4096, 1024], BF16)
    rows32 = dt("rows32", [1, 4 * TC], F32)
    rows16 = dt("rows16", [32, TC], BF16)
    ssm_in = dt("ssm_in", [33, TC], F32)
    ssm_out = dt("ssm_out", [33, TC], F32)
    mix_in = dt("mix_in", [2, 1024, TH], F32)
    mix_out = dt("mix_out", [1024, TH], F32)

    sbuf = nc.alloc_sbuf_tensor
    w_x_sb = sbuf("w_x_sb", [P, DC * 33], BF16)
    negA = sbuf("negA", [P, DC * S], F32)
    convw = sbuf("convw", [P, DC * 4], F32)
    convb = sbuf("convb", [P, DC], F32)
    ddc = sbuf("ddc", [P, DC], F32)
    gamc = sbuf("gamc", [P, DM], F32)
    b1eff = sbuf("b1eff", [P, 32], F32)
    b2h = sbuf("b2h", [P, DM], F32)
    betc = sbuf("betc", [P, DM], F32)
    ones1 = sbuf("ones1", [P, 1], BF16)
    ones16 = sbuf("ones16", [1, S], BF16)
    cst = sbuf("cst", [P, 2], F32)
    state = sbuf("state", [P, DC * S], F32)
    halo = sbuf("halo", [P, DC * 3], BF16)

    xt = sbuf("xt", [P, DM * TC], BF16)
    xin = sbuf("xin", [P, DC * (TC + 3)], BF16)
    xsi = sbuf("xsi", [P, DC * TC], BF16)
    zg = sbuf("zg", [P, DC * TC], BF16)
    yb = sbuf("yb", [P, DC * TC], BF16)
    x2s = sbuf("x2s", [P, TC], BF16)
    x1b = sbuf("x1b", [P, DM * TH], BF16)
    u2b = sbuf("u2b", [P, DM * TH], BF16)
    o2b = sbuf("o2b", [P, DM * TH], BF16)
    wst = [sbuf(f"wst{i}", [P, 1024], BF16) for i in range(2)]
    rbc = sbuf("rbc", [P, TC], F32)
    mbc = sbuf("mbc", [P, TC], F32)
    srow = sbuf("srow", [1, 4 * TC], F32)
    tmpb = sbuf("tmpb", [P, TC], BF16)

    A = sbuf("A", [P, 12288], F32)
    Ab = A[:].bitcast(BF16)
    u_v = lambda: Ab[:, 0:DM * TC]
    hg_v = [Ab[:, 0:SG * TC], Ab[:, SG * TC:2 * SG * TC]]
    dbb_v = [Ab[:, 8192 + i * SG * TC: 8192 + (i + 1) * SG * TC] for i in range(2)]
    cbb_v = [Ab[:, 16384 + i * SG * TC: 16384 + (i + 1) * SG * TC] for i in range(2)]
    ffnh_v = Ab[:, 0:32 * TH]
    xo = Ab[:, 16384:16384 + DM * TH]

    U = sbuf("U", [P, 4096], F32)
    Ub = U[:].bitcast(BF16)
    prew_v = U[:, 0:4096]
    ssm_v = U[0:33, 0:1024]
    Bt_v = U[0:16, 0:1024]
    Ct_v = U[0:16, 1024:2048]
    dBt_v = Ub[0:16, 4096:5120]
    Cbt_v = Ub[0:16, 5120:6144]
    dbc_v = U[:, 0:1024]
    dA_v = [U[:, 1024:2048], U[:, 2048:3072]]
    pp_v = [Ub[:, 6144:7168], Ub[:, 7168:8192]]
    mixh_v = U[:, 0:4096]
    fstg_v = [U[:, 0:512], U[:, 512:1024]]
    r4bc_v = U[:, 1024:1536]

    preb = xt[:, 0:4096]
    pres = xt[:, 4096:8192]
    pres_f = A[:, 0:4096]

    ps = [nc.alloc_psum_tensor(f"ps{i}", [P, 512], F32) for i in range(8)]

    names = ["ld", "wsA", "wsB", "row", "bc", "bcA", "bcB", "osA", "osB", "cc", "mm", "ev", "ea", "hv", "cp", "wf", "lg", "sg", "sdA", "sdB"]
    sem = {n: nc.alloc_semaphore(n) for n in names}
    cnt = {n: 0 for n in names}

    def inc(n, k=1):
        cnt[n] += k
        return cnt[n]

    with nc.Block() as B:
        @B.sync
        def _(q):
            for dst, src in [(negA, a_log_c), (convw, conv_w_c), (convb, conv_b_c),
                             (ddc, d_c), (gamc, gam_c), (betc, bet_c),
                             (b1eff, b1_c), (b2h, b2_c)]:
                q.dma_start(out=dst[:], in_=src).then_inc(sem["ld"], 16)
                inc("ld", 16)
            q.wait_ge(sem["ld"], cnt["ld"])

        @B.gpsimd
        def _(g):
            for kc in range(DC):
                g.dma_start(out=w_x_sb[:, kc * 33:(kc + 1) * 33],
                            in_=w_x_T[kc * P:(kc + 1) * P, :]).then_inc(sem["lg"], 16)
                inc("lg", 16)
            g.wait_ge(sem["lg"], cnt["lg"])

        @B.vector
        def _(v):
            v.memset(state[:], 0.0)
            v.memset(halo[:], 0.0)
            v.memset(ones1[:], 1.0)
            v.memset(ones16[:], 1.0)
            v.memset(cst[:, 0:1], EPS)
            v.memset(cst[:, 1:2], 1.0)

    with nc.Block() as B:
        @B.scalar
        def _(s):
            s.activation(negA[:], negA[:], AF.Exp, bias=0.0, scale=1.0)

    with nc.Block() as B:
        @B.vector
        def _(v):
            v.tensor_scalar_mul(negA[:], negA[:], -1.0)
            v.tensor_scalar_mul(b2h[:], b2h[:], 0.5)

    rounds = []
    for kc in range(8):
        for h in range(2):
            rounds.append((xT[kc * P:(kc + 1) * P, h * 4096:(h + 1) * 4096],
                           xbf[kc * P:(kc + 1) * P, h * 4096:(h + 1) * 4096], 4096))
    for kc in range(8):
        rounds.append((xoT[kc * P:(kc + 1) * P, :], xobf[kc * P:(kc + 1) * P, :], 4096))
    for kc in range(8):
        rounds.append((w_in_T[kc * P:(kc + 1) * P, :], w_in_bf[kc * P:(kc + 1) * P, :], 2048))
    for kc in range(8):
        rounds.append((w_out_T[kc * P:(kc + 1) * P, :], w_out_bf[kc * P:(kc + 1) * P, :], 1024))

    with nc.Block() as B:
        sg0 = cnt["sg"]

        @B.gpsimd
        def _(g):
            for i, (src, dst, n) in enumerate(rounds):
                bufn = (preb if i % 2 == 0 else pres)[:, 0:n]
                if i >= 2:
                    g.wait_ge(sem["sg"], sg0 + (i - 1) * 16)
                g.dma_start(out=bufn, in_=src).then_inc(sem["lg"], 16)
                g.wait_ge(sem["lg"], inc("lg", 16))
                if i >= 1:
                    g.wait_ge(sem["sg"], sg0 + i * 16)
                g.dma_start(out=dst, in_=bufn).then_inc(sem["sg"], 16)
                inc("sg", 16)
            g.wait_ge(sem["sg"], cnt["sg"])

    with nc.Block() as B:
        ea0, hv0 = cnt["ea"], cnt["hv"]

        @B.gpsimd
        def _(g):
            for kc in range(8):
                if kc >= 1:
                    g.wait_ge(sem["hv"], hv0 + kc)
                    g.dma_start(out=w1bf[(kc - 1) * P:kc * P, :], in_=preb[:]).then_inc(sem["sg"], 16)
                    g.wait_ge(sem["sg"], inc("sg", 16))
                g.dma_start(out=preb[:], in_=q1[kc * P:(kc + 1) * P, :]).then_inc(sem["lg"], 16)
                inc("lg", 16)
                g.dma_start(out=pres_f, in_=s1[kc:kc + 1, :].broadcast_to([P, 4096])).then_inc(sem["lg"], 16)
                g.wait_ge(sem["lg"], inc("lg", 16))
                g.sem_inc(sem["ea"], 1)
                inc("ea")
            g.wait_ge(sem["hv"], hv0 + 8)
            g.dma_start(out=w1bf[7 * P:8 * P, :], in_=preb[:]).then_inc(sem["sg"], 16)
            g.wait_ge(sem["sg"], inc("sg", 16))
            for kc in range(32):
                if kc >= 1:
                    g.wait_ge(sem["hv"], hv0 + 8 + kc)
                    g.dma_start(out=w2bf[(kc - 1) * P:kc * P, :],
                                in_=preb[:, 1024:2048]).then_inc(sem["sg"], 16)
                    g.wait_ge(sem["sg"], inc("sg", 16))
                g.dma_start(out=preb[:, 0:1024], in_=q2[kc * P:(kc + 1) * P, :]).then_inc(sem["lg"], 16)
                inc("lg", 16)
                g.dma_start(out=pres_f[:, 0:1024], in_=s2[kc:kc + 1, :].broadcast_to([P, 1024])).then_inc(sem["lg"], 16)
                g.wait_ge(sem["lg"], inc("lg", 16))
                g.sem_inc(sem["ea"], 1)
                inc("ea")
            g.wait_ge(sem["hv"], hv0 + 40)
            g.dma_start(out=w2bf[31 * P:32 * P, :], in_=preb[:, 1024:2048]).then_inc(sem["sg"], 16)
            g.wait_ge(sem["sg"], inc("sg", 16))

        @B.vector
        def _(v):
            for kc in range(8):
                v.wait_ge(sem["ea"], ea0 + kc + 1)
                v.scalar_tensor_tensor(prew_v, preb[:], -128.0, pres_f, ALU.add, ALU.mult)
                v.tensor_scalar(preb[:], prew_v, gamc[:, kc:kc + 1], None, ALU.mult)
                v.drain().then_inc(sem["hv"], 1)
                inc("hv")
            for kc in range(32):
                v.wait_ge(sem["ea"], ea0 + 8 + kc + 1)
                v.scalar_tensor_tensor(prew_v[:, 0:1024], preb[:, 0:1024], -128.0,
                                       pres_f[:, 0:1024], ALU.add, ALU.mult)
                v.tensor_copy(preb[:, 1024:2048], prew_v[:, 0:1024])
                v.drain().then_inc(sem["hv"], 1)
                inc("hv")

    with nc.Block() as B:
        ea0, hv0, wf0 = cnt["ea"], cnt["hv"], cnt["wf"]

        @B.gpsimd
        def _(g):
            for kc in range(8):
                if kc >= 1:
                    g.wait_ge(sem["wf"], wf0 + kc)
                g.dma_start(out=preb[:], in_=q1[kc * P:(kc + 1) * P, :]).then_inc(sem["lg"], 16)
                inc("lg", 16)
                g.dma_start(out=pres_f, in_=s1[kc:kc + 1, :].broadcast_to([P, 4096])).then_inc(sem["lg"], 16)
                g.wait_ge(sem["lg"], inc("lg", 16))
                g.sem_inc(sem["ea"], 1)
                inc("ea")

        @B.vector
        def _(v):
            for kc in range(8):
                v.wait_ge(sem["ea"], ea0 + kc + 1)
                v.scalar_tensor_tensor(prew_v, preb[:], -128.0, pres_f, ALU.add, ALU.mult)
                v.tensor_scalar(preb[:], prew_v, betc[:, kc:kc + 1], None, ALU.mult)
                v.drain().then_inc(sem["hv"], 1)
                inc("hv")

        @B.tensor
        def _(t):
            for kc in range(8):
                t.wait_ge(sem["hv"], hv0 + kc + 1)
                for n in range(8):
                    mi = t.matmul(ps[n][0:1, :], ones1[:, 0:1], preb[:, n * 512:(n + 1) * 512],
                             start=(kc == 0), stop=(kc == 7))
                mi.then_inc(sem["wf"], 1)
                inc("wf")

    with nc.Block() as B:
        hv0b = cnt["hv"]
        row0b = cnt["row"]

        @B.vector
        def _(v):
            for n in range(8):
                v.tensor_copy(srow[0:1, 0:512], ps[n][0:1, :])
                v.drain().then_inc(sem["hv"], 1)
                inc("hv")
                v.wait_ge(sem["row"], row0b + (n + 1) * 32)
                v.tensor_add(b1eff[:, n * 4:(n + 1) * 4], b1eff[:, n * 4:(n + 1) * 4], mbc[:, 0:4])

        @B.sync
        def _(q):
            for n in range(8):
                q.wait_ge(sem["hv"], hv0b + n + 1)
                q.dma_start(out=rows32[0:1, 0:512], in_=srow[0:1, 0:512]).then_inc(sem["row"], 16)
                q.wait_ge(sem["row"], inc("row", 16))
                q.dma_start(out=mbc[:, 0:4],
                            in_=rows32[0:1, 0:512].rearrange("o (c p) -> (o p) c", p=P),
                            allow_slow_non_contiguous=True).then_inc(sem["row"], 16)
                q.wait_ge(sem["row"], inc("row", 16))

    for ch in range(NCH):
        T0 = ch * TC
        H0 = ch * TH

        if PHL < 1:
            pass
        else:
            with nc.Block() as B:
                @B.sync
                def _(q):
                    for dm in range(DM):
                        q.dma_start(out=xt[:, dm * TC:(dm + 1) * TC],
                                    in_=xbf[dm * P:(dm + 1) * P, T0:T0 + TC]).then_inc(sem["ld"], 16)
                        inc("ld", 16)
                    q.wait_ge(sem["ld"], cnt["ld"])

        if PHL < 2:
            pass
        else:
            with nc.Block() as B:
                ea0, mm0, hv0, cp0 = cnt["ea"], cnt["mm"], cnt["hv"], cnt["cp"]

                @B.scalar
                def _(s):
                    for kc in range(DM):
                        if kc >= 1:
                            s.wait_ge(sem["mm"], mm0 + kc)
                        s.activation(x2s[:, 0:TC],
                                     xt[:, kc * TC:(kc + 1) * TC], AF.Square,
                                     bias=0.0, scale=1.0).then_inc(sem["ea"], 1)
                        inc("ea")
                    s.wait_ge(sem["hv"], hv0 + 1)
                    s.activation(srow[0:1, 2 * TC:2 * TC + TC], srow[0:1, 1 * TC:1 * TC + TC], AF.Sqrt, bias=cst[0:1, 0:1], scale=1.0)
                    s.drain().then_inc(sem["cp"], 1)
                    inc("cp")

                @B.tensor
                def _(t):
                    for kc in range(DM):
                        for n in range(2):
                            t.matmul(ps[n][0:1, :], ones1[:, 0:1],
                                     xt[:, kc * TC + n * 512:kc * TC + (n + 1) * 512],
                                     start=(kc == 0), stop=(kc == DM - 1))
                        t.wait_ge(sem["ea"], ea0 + kc + 1)
                        for n in range(2):
                            mi = t.matmul(ps[2 + n][0:1, :], ones1[:, 0:1],
                                     x2s[:, n * 512:(n + 1) * 512],
                                     start=(kc == 0), stop=(kc == DM - 1))
                        mi.then_inc(sem["mm"], 1)
                        inc("mm")

                @B.vector
                def _(v):
                    v.wait_ge(sem["mm"], mm0 + DM)
                    v.tensor_copy(srow[0:1, 0:512], ps[0][0:1, :])
                    v.tensor_copy(srow[0:1, 512:1024], ps[1][0:1, :])
                    v.tensor_copy(srow[0:1, 1 * TC:1 * TC + 512], ps[2][0:1, :])
                    v.tensor_copy(srow[0:1, 1 * TC + 512:1 * TC + 1024], ps[3][0:1, :])
                    v.tensor_scalar_mul(srow[0:1, 0:TC], srow[0:1, 0:TC], 1.0 / 1024.0)
                    v.tensor_scalar_mul(srow[0:1, 1 * TC:1 * TC + TC], srow[0:1, 1 * TC:1 * TC + TC], 1.0 / 1024.0)
                    v.tensor_mul(srow[0:1, 2 * TC:2 * TC + TC], srow[0:1, 0:TC], srow[0:1, 0:TC])
                    v.tensor_sub(srow[0:1, 1 * TC:1 * TC + TC], srow[0:1, 1 * TC:1 * TC + TC], srow[0:1, 2 * TC:2 * TC + TC])
                    v.drain().then_inc(sem["hv"], 1)
                    inc("hv")
                    v.wait_ge(sem["cp"], cp0 + 1)
                    v.reciprocal(srow[0:1, 2 * TC:2 * TC + TC], srow[0:1, 2 * TC:2 * TC + TC])
                    v.tensor_mul(srow[0:1, 3 * TC:3 * TC + TC], srow[0:1, 0:TC], srow[0:1, 2 * TC:2 * TC + TC])
                    v.tensor_scalar_mul(srow[0:1, 3 * TC:3 * TC + TC], srow[0:1, 3 * TC:3 * TC + TC], -1.0)
                    v.drain().then_inc(sem["hv"], 1)
                    inc("hv")

                @B.sync
                def _(q):
                    q.wait_ge(sem["hv"], hv0 + 2)
                    q.dma_start(out=rows32[0:1, 2 * TC:4 * TC], in_=srow[0:1, 2 * TC:4 * TC]).then_inc(sem["row"], 16)
                    q.wait_ge(sem["row"], inc("row", 16))
                    q.dma_start(out=rbc[:], in_=rows32[0:1, 2 * TC:2 * TC + TC].broadcast_to([P, TC])).then_inc(sem["bc"], 16)
                    inc("bc", 16)
                    q.dma_start(out=mbc[:], in_=rows32[0:1, 3 * TC:3 * TC + TC].broadcast_to([P, TC])).then_inc(sem["bc"], 16)
                    inc("bc", 16)
                    q.wait_ge(sem["bc"], cnt["bc"])

        if PHL < 2:
            pass
        else:
            with nc.Block() as B:
                @B.vector
                def _(v):
                    u = u_v()
                    for dm in range(DM):
                        v.tensor_mul(u[:, dm * TC:(dm + 1) * TC], xt[:, dm * TC:(dm + 1) * TC], rbc[:])
                        v.tensor_add(u[:, dm * TC:(dm + 1) * TC], u[:, dm * TC:(dm + 1) * TC], mbc[:])

        if PHL < 3:
            pass
        else:
            with nc.Block() as B:
                wsA0, wsB0 = cnt["wsA"], cnt["wsB"]
                wf0, mm0, ev0 = cnt["wf"], cnt["mm"], cnt["ev"]

                @B.sync
                def _(q):
                    for i in range(32):
                        q4, kc = divmod(i, 8)
                        if i >= 2:
                            q.wait_ge(sem["wf"], wf0 + i - 1)
                        pn = "wsA" if i % 2 == 0 else "wsB"
                        q.dma_start(out=wst[i % 2][:, 0:512],
                                    in_=w_in_bf[kc * P:(kc + 1) * P, q4 * 512:(q4 + 1) * 512]
                                    ).then_inc(sem[pn], 16)
                        inc(pn, 16)

                @B.tensor
                def _(t):
                    u = u_v()
                    for i in range(32):
                        q4, kc = divmod(i, 8)
                        t.wait_ge(sem["wsA" if i % 2 == 0 else "wsB"],
                                  (wsA0 if i % 2 == 0 else wsB0) + (i // 2 + 1) * 16)
                        if kc == 0 and q4 >= 1:
                            t.wait_ge(sem["ev"], ev0 + q4)
                        for n in range(2):
                            for m in range(4):
                                mmi = t.matmul(ps[n * 4 + m][:],
                                               wst[i % 2][:, m * P:(m + 1) * P],
                                               u[:, kc * TC + n * 512:kc * TC + (n + 1) * 512],
                                               start=(kc == 0), stop=(kc == 7))
                        mmi.then_inc(sem["wf"], 1)
                        inc("wf")

                @B.vector
                def _(v):
                    for q4 in range(2):
                        v.wait_ge(sem["wf"], wf0 + (q4 + 1) * 8)
                        for n in range(2):
                            for m in range(4):
                                dc = q4 * 4 + m
                                base = dc * (TC + 3) + 3
                                v.tensor_copy(xin[:, base + n * 512:base + (n + 1) * 512],
                                              ps[n * 4 + m][:])
                        v.drain().then_inc(sem["ev"], 1)
                        inc("ev")

                @B.scalar
                def _(s):
                    for q4 in range(2, 4):
                        s.wait_ge(sem["wf"], wf0 + (q4 + 1) * 8)
                        for n in range(2):
                            for m in range(4):
                                dc = (q4 - 2) * 4 + m
                                s.activation(zg[:, dc * TC + n * 512:dc * TC + (n + 1) * 512],
                                             ps[n * 4 + m][:], AF.Silu, bias=0.0, scale=1.0)
                        s.drain().then_inc(sem["ev"], 1)
                        inc("ev")

        if PHL < 4:
            pass
        else:
            with nc.Block() as B:
                hv0 = cnt["hv"]

                @B.vector
                def _(v):
                    for dc in range(DC):
                        base = dc * (TC + 3)
                        v.tensor_copy(xin[:, base:base + 3], halo[:, dc * 3:(dc + 1) * 3])
                        v.tensor_copy(halo[:, dc * 3:(dc + 1) * 3], xin[:, base + TC:base + TC + 3])
                        v.tensor_scalar(xsi[:, dc * TC:(dc + 1) * TC], xin[:, base:base + TC],
                                        convw[:, dc * 4:dc * 4 + 1], None, ALU.mult)
                        for k in range(1, 4):
                            v.scalar_tensor_tensor(xsi[:, dc * TC:(dc + 1) * TC],
                                                   xin[:, base + k:base + k + TC],
                                                   convw[:, dc * 4 + k:dc * 4 + k + 1],
                                                   xsi[:, dc * TC:(dc + 1) * TC],
                                                   ALU.mult, ALU.add)
                        v.drain().then_inc(sem["hv"], 1)
                        inc("hv")

                @B.scalar
                def _(s):
                    for dc in range(DC):
                        s.wait_ge(sem["hv"], hv0 + dc + 1)
                        s.activation(xsi[:, dc * TC:(dc + 1) * TC], xsi[:, dc * TC:(dc + 1) * TC],
                                     AF.Silu, bias=convb[:, dc:dc + 1], scale=1.0)

        if PHL < 5:
            pass
        else:
            with nc.Block() as B:
                mm0, hv0, cc0, row0 = cnt["mm"], cnt["hv"], cnt["cc"], cnt["row"]

                @B.tensor
                def _(t):
                    for n in range(2):
                        for kc in range(DC):
                            mi = t.matmul(ps[n][0:33, :], w_x_sb[:, kc * 33:(kc + 1) * 33],
                                          xsi[:, kc * TC + n * 512:kc * TC + (n + 1) * 512],
                                          start=(kc == 0), stop=(kc == DC - 1))
                    mi.then_inc(sem["mm"], 1)
                    inc("mm")

                @B.vector
                def _(v):
                    v.wait_ge(sem["mm"], mm0 + 1)
                    v.tensor_copy(ssm_v[:, 0:512], ps[0][0:33, :])
                    v.tensor_copy(ssm_v[:, 512:1024], ps[1][0:33, :])
                    v.drain().then_inc(sem["hv"], 1)
                    inc("hv")

                @B.gpsimd
                def _(g):
                    g.wait_ge(sem["hv"], hv0 + 1)
                    g.dma_start(out=ssm_in, in_=ssm_v).then_inc(sem["lg"], 16)
                    g.wait_ge(sem["lg"], inc("lg", 16))
                    g.collective_compute("AllReduce", ALU.add, replica_groups=RG,
                                         ins=[ssm_in.opt()], outs=[ssm_out.opt()]).then_inc(sem["cc"], 1)
                    g.wait_ge(sem["cc"], inc("cc"))
                    g.dma_start(out=srow[0:1, 0:TC], in_=ssm_out[32:33, :]).then_inc(sem["lg"], 16)
                    inc("lg", 16)
                    g.dma_start(out=Bt_v, in_=ssm_out[0:16, :]).then_inc(sem["lg"], 16)
                    inc("lg", 16)
                    g.dma_start(out=Ct_v, in_=ssm_out[16:32, :]).then_inc(sem["lg"], 16)
                    g.wait_ge(sem["lg"], inc("lg", 16))

            with nc.Block() as B:
                ea0, hv0, mm0, row0 = cnt["ea"], cnt["hv"], cnt["mm"], cnt["row"]

                @B.scalar
                def _(s):
                    s.activation(srow[0:1, TC:2 * TC], srow[0:1, 0:TC], AF.Abs, bias=0.0, scale=1.0)
                    s.activation(srow[0:1, TC:2 * TC], srow[0:1, TC:2 * TC], AF.Exp, bias=0.0, scale=-1.0)
                    s.activation(srow[0:1, TC:2 * TC], srow[0:1, TC:2 * TC], AF.Ln, bias=cst[0:1, 1:2], scale=1.0)
                    s.activation(srow[0:1, 2 * TC:3 * TC], srow[0:1, 0:TC], AF.Relu, bias=0.0, scale=1.0)
                    s.drain().then_inc(sem["ea"], 1)
                    inc("ea")

                @B.vector
                def _(v):
                    v.wait_ge(sem["ea"], ea0 + 1)
                    v.tensor_add(srow[0:1, 0:TC], srow[0:1, TC:2 * TC], srow[0:1, 2 * TC:3 * TC])
                    v.tensor_copy(tmpb[0:1, :], srow[0:1, 0:TC])
                    v.drain().then_inc(sem["hv"], 1)
                    inc("hv")
                    v.wait_ge(sem["mm"], mm0 + 1)
                    for n in range(2):
                        v.tensor_tensor(dBt_v[:, n * 512:(n + 1) * 512],
                                        Bt_v[:, n * 512:(n + 1) * 512],
                                        ps[2 + n][0:16, :], ALU.mult)
                    v.tensor_copy(Cbt_v, Ct_v)
                    v.drain().then_inc(sem["hv"], 1)
                    inc("hv")

                @B.tensor
                def _(t):
                    t.wait_ge(sem["hv"], hv0 + 1)
                    for n in range(2):
                        mi = t.matmul(ps[2 + n][0:16, :], ones16[0:1, :],
                                      tmpb[0:1, n * 512:(n + 1) * 512], start=True, stop=True)
                    mi.then_inc(sem["mm"], 1)
                    inc("mm")

                @B.sync
                def _(q):
                    q.wait_ge(sem["hv"], hv0 + 2)
                    q.dma_start(out=rows16[0:16, :], in_=dBt_v).then_inc(sem["row"], 16)
                    inc("row", 16)
                    q.dma_start(out=rows16[16:32, :], in_=Cbt_v).then_inc(sem["row"], 16)
                    inc("row", 16)
                    q.dma_start(out=rows32[0:1, 0:TC], in_=srow[0:1, 0:TC]).then_inc(sem["row"], 16)
                    q.wait_ge(sem["row"], inc("row", 16))
                    q.dma_start(out=dbc_v, in_=rows32[0:1, 0:TC].broadcast_to([P, TC])).then_inc(sem["bc"], 16)
                    inc("bc", 16)
                    q.wait_ge(sem["bc"], cnt["bc"])

        if PHL < 6:
            pass
        else:
            with nc.Block() as B:
                ea0, hv0, cp0 = cnt["ea"], cnt["hv"], cnt["cp"]
                NG = NSG * DC

                def g_of(sg, dc):
                    return sg * DC + dc

                def hv_after_scan(g, si):
                    return hv0 + g * 5 + si + 1

                def hv_after_save(g):
                    return hv0 + g * 5 + 5

                bcA0, bcB0 = cnt["bcA"], cnt["bcB"]

                @B.sync
                def _(q):
                    for sg in range(NSG):
                        if sg >= 2:
                            q.wait_ge(sem["hv"], hv_after_save(g_of(sg - 2, DC - 1)))
                            q.wait_ge(sem["cp"], cp0 + (sg - 1))
                        pn = "bcA" if sg % 2 == 0 else "bcB"
                        for si in range(SG):
                            s_ = sg * SG + si
                            q.dma_start(out=dbb_v[sg % 2][:, si * TC:(si + 1) * TC],
                                        in_=rows16[s_:s_ + 1, :].broadcast_to([P, TC])
                                        ).then_inc(sem[pn], 16)
                            inc(pn, 16)
                            q.dma_start(out=cbb_v[sg % 2][:, si * TC:(si + 1) * TC],
                                        in_=rows16[16 + s_:17 + s_, :].broadcast_to([P, TC])
                                        ).then_inc(sem[pn], 16)
                            inc(pn, 16)

                @B.scalar
                def _(s):
                    k = 0
                    for sg in range(NSG):
                        for dc in range(DC):
                            for si in range(SG):
                                s_ = sg * SG + si
                                if k >= 2:
                                    km2 = k - 2
                                    g2, si2 = divmod(km2, SG)
                                    s.wait_ge(sem["hv"], hv_after_scan(g2, si2))
                                s.activation(dA_v[k % 2], dbc_v, AF.Exp, bias=0.0,
                                             scale=negA[:, dc * S + s_:dc * S + s_ + 1]
                                             ).then_inc(sem["ea"], 1)
                                inc("ea")
                                k += 1

                @B.vector
                def _(v):
                    k = 0
                    for sg in range(NSG):
                        v.wait_ge(sem["bcA" if sg % 2 == 0 else "bcB"],
                                  (bcA0 if sg % 2 == 0 else bcB0) + (sg // 2 + 1) * SG * 2 * 16)
                        for dc in range(DC):
                            g = g_of(sg, dc)
                            hb = hg_v[g % 2]
                            for si in range(SG):
                                s_ = sg * SG + si
                                v.tensor_tensor(pp_v[k % 2], xsi[:, dc * TC:(dc + 1) * TC],
                                                dbb_v[sg % 2][:, si * TC:(si + 1) * TC], ALU.mult)
                                v.wait_ge(sem["ea"], ea0 + k + 1)
                                v.tensor_tensor_scan(hb[:, si * TC:(si + 1) * TC],
                                                     dA_v[k % 2], pp_v[k % 2],
                                                     state[:, dc * S + s_:dc * S + s_ + 1],
                                                     ALU.mult, ALU.add).then_inc(sem["hv"], 1)
                                inc("hv")
                                k += 1
                            v.tensor_copy(state[:, dc * S + sg * SG:dc * S + (sg + 1) * SG],
                                          hb[:, TC - 1::TC]).then_inc(sem["hv"], 1)
                            inc("hv")
                            for si in range(SG):
                                v.tensor_tensor(hb[:, si * TC:(si + 1) * TC],
                                                hb[:, si * TC:(si + 1) * TC],
                                                cbb_v[sg % 2][:, si * TC:(si + 1) * TC], ALU.mult)
                            v.tensor_add(hb[:, 0:TC], hb[:, 0:TC], hb[:, TC:2 * TC])
                            v.tensor_add(hb[:, 2 * TC:3 * TC], hb[:, 2 * TC:3 * TC], hb[:, 3 * TC:4 * TC])
                            v.tensor_add(hb[:, 0:TC], hb[:, 0:TC], hb[:, 2 * TC:3 * TC])
                            if sg == 0:
                                v.tensor_copy(yb[:, dc * TC:(dc + 1) * TC], hb[:, 0:TC])
                            else:
                                v.tensor_add(yb[:, dc * TC:(dc + 1) * TC],
                                             yb[:, dc * TC:(dc + 1) * TC], hb[:, 0:TC])
                        v.drain().then_inc(sem["cp"], 1)
                        inc("cp")

        if PHL < 7:
            pass
        else:
            with nc.Block() as B:
                @B.vector
                def _(v):
                    for dc in range(DC):
                        v.scalar_tensor_tensor(yb[:, dc * TC:(dc + 1) * TC],
                                               xsi[:, dc * TC:(dc + 1) * TC],
                                               ddc[:, dc:dc + 1],
                                               yb[:, dc * TC:(dc + 1) * TC], ALU.mult, ALU.add)
                        v.tensor_mul(yb[:, dc * TC:(dc + 1) * TC], yb[:, dc * TC:(dc + 1) * TC],
                                     zg[:, dc * TC:(dc + 1) * TC])

            with nc.Block() as B:
                wsA0, wsB0 = cnt["wsA"], cnt["wsB"]
                wf0, mm0, ev0 = cnt["wf"], cnt["mm"], cnt["ev"]

                @B.sync
                def _(q):
                    for i in range(16):
                        mh, kc = divmod(i, 8)
                        if i >= 2:
                            q.wait_ge(sem["wf"], wf0 + i - 1)
                        pn = "wsA" if i % 2 == 0 else "wsB"
                        q.dma_start(out=wst[i % 2][:],
                                    in_=w_out_bf[kc * P:(kc + 1) * P, :]).then_inc(sem[pn], 16)
                        inc(pn, 16)

                @B.tensor
                def _(t):
                    for i in range(16):
                        mh, kc = divmod(i, 8)
                        t.wait_ge(sem["wsA" if i % 2 == 0 else "wsB"],
                                  (wsA0 if i % 2 == 0 else wsB0) + (i // 2 + 1) * 16)
                        if kc == 0 and mh == 1:
                            t.wait_ge(sem["ev"], ev0 + 1)
                        for n in range(2):
                            for m in range(4):
                                mmi = t.matmul(ps[n * 4 + m][:],
                                         wst[i % 2][:, (mh * 4 + m) * P:(mh * 4 + m + 1) * P],
                                         yb[:, kc * TC + n * 512:kc * TC + (n + 1) * 512],
                                         start=(kc == 0), stop=(kc == 7))
                        mmi.then_inc(sem["wf"], 1)
                        inc("wf")

                hv0p7 = cnt["hv"]
                sdA0, sdB0 = cnt["sdA"], cnt["sdB"]

                @B.scalar
                def _(s):
                    pieces = 0
                    for mh in range(2):
                        s.wait_ge(sem["wf"], wf0 + (mh + 1) * 8)
                        for n in range(2):
                            for m in range(4):
                                dm = mh * 4 + m
                                if pieces >= 2:
                                    s.wait_ge(sem["sdA" if pieces % 2 == 0 else "sdB"],
                                              (sdA0 if pieces % 2 == 0 else sdB0) + (pieces // 2) * 16)
                                s.activation(fstg_v[pieces % 2], ps[n * 4 + m][:],
                                             AF.Identity, bias=0.0, scale=1.0)
                                if n == 1 and m == 3:
                                    s.drain().then_inc(sem["ev"], 1)
                                    inc("ev")
                                else:
                                    s.drain()
                                pn = "sdA" if pieces % 2 == 0 else "sdB"
                                s.dma_start(out=mix_in[n, dm * P:(dm + 1) * P, :],
                                            in_=fstg_v[pieces % 2]).then_inc(sem[pn], 16)
                                inc(pn, 16)
                                pieces += 1

                @B.gpsimd
                def _(g):
                    for dm in range(DM):
                        g.dma_start(out=xo[:, dm * TH:(dm + 1) * TH],
                                    in_=xobf[dm * P:(dm + 1) * P, H0:H0 + TH]).then_inc(sem["lg"], 16)
                        inc("lg", 16)
                    g.wait_ge(sem["lg"], cnt["lg"])
                    g.wait_ge(sem["sdA"], sdA0 + 8 * 16)
                    g.wait_ge(sem["sdB"], sdB0 + 8 * 16)
                    g.collective_compute("ReduceScatter", ALU.add, replica_groups=RG,
                                         ins=[mix_in.opt()], outs=[mix_out.opt()]).then_inc(sem["cc"], 1)
                    g.wait_ge(sem["cc"], inc("cc"))
                    for dm in range(DM):
                        g.dma_start(out=mixh_v[:, dm * TH:(dm + 1) * TH],
                                    in_=mix_out[dm * P:(dm + 1) * P, :]).then_inc(sem["lg"], 16)
                        inc("lg", 16)
                    g.wait_ge(sem["lg"], cnt["lg"])

        if PHL < 8:
            pass
        else:
            def ln_block(src_bf, width, n_halves, extra=None):
                with nc.Block() as B:
                    ea0, mm0, hv0, cp0 = cnt["ea"], cnt["mm"], cnt["hv"], cnt["cp"]

                    @B.scalar
                    def _(s):
                        for kc in range(DM):
                            if kc >= 1:
                                s.wait_ge(sem["mm"], mm0 + kc)
                            s.activation(x2s[:, 0:width],
                                         src_bf[:, kc * width:(kc + 1) * width], AF.Square,
                                         bias=0.0, scale=1.0).then_inc(sem["ea"], 1)
                            inc("ea")
                        s.wait_ge(sem["hv"], hv0 + 1)
                        s.activation(srow[0:1, 2 * TC:2 * TC + width], srow[0:1, 1 * TC:1 * TC + width], AF.Sqrt,
                                     bias=cst[0:1, 0:1], scale=1.0)
                        s.drain().then_inc(sem["cp"], 1)
                        inc("cp")

                    @B.tensor
                    def _(t):
                        for kc in range(DM):
                            for n in range(n_halves):
                                t.matmul(ps[n][0:1, :], ones1[:, 0:1],
                                         src_bf[:, kc * width + n * 512:kc * width + (n + 1) * 512],
                                         start=(kc == 0), stop=(kc == DM - 1))
                            t.wait_ge(sem["ea"], ea0 + kc + 1)
                            for n in range(n_halves):
                                mi = t.matmul(ps[2 + n][0:1, :], ones1[:, 0:1],
                                         x2s[:, n * 512:(n + 1) * 512],
                                         start=(kc == 0), stop=(kc == DM - 1))
                            mi.then_inc(sem["mm"], 1)
                            inc("mm")

                    @B.vector
                    def _(v):
                        v.wait_ge(sem["mm"], mm0 + DM)
                        for n in range(n_halves):
                            v.tensor_copy(srow[0:1, n * 512:(n + 1) * 512], ps[n][0:1, :])
                            v.tensor_copy(srow[0:1, 1 * TC + n * 512:1 * TC + (n + 1) * 512], ps[2 + n][0:1, :])
                        w = width
                        v.tensor_scalar_mul(srow[0:1, 0:w], srow[0:1, 0:w], 1.0 / 1024.0)
                        v.tensor_scalar_mul(srow[0:1, 1 * TC:1 * TC + w], srow[0:1, 1 * TC:1 * TC + w], 1.0 / 1024.0)
                        v.tensor_mul(srow[0:1, 2 * TC:2 * TC + w], srow[0:1, 0:w], srow[0:1, 0:w])
                        v.tensor_sub(srow[0:1, 1 * TC:1 * TC + w], srow[0:1, 1 * TC:1 * TC + w], srow[0:1, 2 * TC:2 * TC + w])
                        v.drain().then_inc(sem["hv"], 1)
                        inc("hv")
                        v.wait_ge(sem["cp"], cp0 + 1)
                        v.reciprocal(srow[0:1, 2 * TC:2 * TC + w], srow[0:1, 2 * TC:2 * TC + w])
                        v.tensor_mul(srow[0:1, 3 * TC:3 * TC + w], srow[0:1, 0:w], srow[0:1, 2 * TC:2 * TC + w])
                        v.tensor_scalar_mul(srow[0:1, 3 * TC:3 * TC + w], srow[0:1, 3 * TC:3 * TC + w], -1.0)
                        if extra is not None:
                            extra(v)
                        v.drain().then_inc(sem["hv"], 1)
                        inc("hv")

                    @B.sync
                    def _(q):
                        q.wait_ge(sem["hv"], hv0 + 2)
                        q.dma_start(out=rows32[0:1, 2 * TC:4 * TC], in_=srow[0:1, 2 * TC:4 * TC]).then_inc(sem["row"], 16)
                        q.wait_ge(sem["row"], inc("row", 16))
                        q.dma_start(out=rbc[:, 0:width],
                                    in_=rows32[0:1, 2 * TC:2 * TC + width].broadcast_to([P, width])).then_inc(sem["bc"], 16)
                        inc("bc", 16)
                        q.dma_start(out=mbc[:, 0:width],
                                    in_=rows32[0:1, 3 * TC:3 * TC + width].broadcast_to([P, width])).then_inc(sem["bc"], 16)
                        inc("bc", 16)
                        q.wait_ge(sem["bc"], cnt["bc"])

            with nc.Block() as B:
                @B.vector
                def _(v):
                    for dm in range(DM):
                        v.tensor_copy(yb[:, dm * TH:(dm + 1) * TH], mixh_v[:, dm * TH:(dm + 1) * TH])

            mxb = yb[:, 0:DM * TH]
            ln_block(mxb, TH, 1)

            with nc.Block() as B:
                @B.vector
                def _(v):
                    for dm in range(DM):
                        v.tensor_mul(tmpb[:, 0:TH], mxb[:, dm * TH:(dm + 1) * TH], rbc[:, 0:TH])
                        v.tensor_add(tmpb[:, 0:TH], tmpb[:, 0:TH], mbc[:, 0:TH])
                        v.tensor_add(x1b[:, dm * TH:(dm + 1) * TH], tmpb[:, 0:TH],
                                     xo[:, dm * TH:(dm + 1) * TH])

        if PHL < 9:
            pass
        else:
            ln_block(x1b, TH, 1)

            with nc.Block() as B:
                ea0, hv0 = cnt["ea"], cnt["hv"]

                @B.vector
                def _(v):
                    v.tensor_mul(srow[0:1, 0:TH], srow[0:1, 2 * TC:2 * TC + TH], srow[0:1, 2 * TC:2 * TC + TH])
                    v.tensor_mul(srow[0:1, 0:TH], srow[0:1, 0:TH], srow[0:1, 1 * TC:1 * TC + TH])
                    v.drain().then_inc(sem["hv"], 1)
                    inc("hv")
                    v.wait_ge(sem["ea"], ea0 + 1)
                    v.reciprocal(srow[0:1, 0:TH], srow[0:1, 0:TH])

                @B.scalar
                def _(s):
                    s.wait_ge(sem["hv"], hv0 + 1)
                    s.activation(srow[0:1, 0:TH], srow[0:1, 0:TH], AF.Sqrt, bias=cst[0:1, 0:1], scale=1.0)
                    s.drain().then_inc(sem["ea"], 1)
                    inc("ea")

                @B.sync
                def _(q):
                    pass

            with nc.Block() as B:
                hv0 = cnt["hv"]

                @B.vector
                def _(v):
                    v.drain().then_inc(sem["hv"], 1)
                    inc("hv")

                @B.sync
                def _(q):
                    q.wait_ge(sem["hv"], cnt["hv"])
                    q.dma_start(out=rows32[0:1, 0:TH], in_=srow[0:1, 0:TH]).then_inc(sem["row"], 16)
                    q.wait_ge(sem["row"], inc("row", 16))
                    q.dma_start(out=r4bc_v, in_=rows32[0:1, 0:TH].broadcast_to([P, TH])).then_inc(sem["bc"], 16)
                    q.wait_ge(sem["bc"], inc("bc", 16))

            with nc.Block() as B:
                @B.vector
                def _(v):
                    hb = yb[:, 0:DM * TH]
                    for dm in range(DM):
                        v.tensor_mul(tmpb[:, 0:TH], x1b[:, dm * TH:(dm + 1) * TH], rbc[:, 0:TH])
                        v.tensor_add(u2b[:, dm * TH:(dm + 1) * TH], tmpb[:, 0:TH], mbc[:, 0:TH])
                        v.tensor_mul(hb[:, dm * TH:(dm + 1) * TH],
                                     u2b[:, dm * TH:(dm + 1) * TH], r4bc_v)

            with nc.Block() as B:
                wsA0, wsB0 = cnt["wsA"], cnt["wsB"]
                wf0, mm0, ev0 = cnt["wf"], cnt["mm"], cnt["ev"]
                hb = yb[:, 0:DM * TH]

                @B.sync
                def _(q):
                    for i in range(32):
                        mg, kc = divmod(i, 8)
                        if i >= 2:
                            q.wait_ge(sem["wf"], wf0 + i - 1)
                        pn = "wsA" if i % 2 == 0 else "wsB"
                        q.dma_start(out=wst[i % 2][:],
                                    in_=w1bf[kc * P:(kc + 1) * P, mg * 1024:(mg + 1) * 1024]
                                    ).then_inc(sem[pn], 16)
                        inc(pn, 16)

                @B.tensor
                def _(t):
                    for i in range(32):
                        mg, kc = divmod(i, 8)
                        t.wait_ge(sem["wsA" if i % 2 == 0 else "wsB"],
                                  (wsA0 if i % 2 == 0 else wsB0) + (i // 2 + 1) * 16)
                        if kc == 0 and mg >= 1:
                            t.wait_ge(sem["ev"], ev0 + mg)
                        for m in range(8):
                            mmi = t.matmul(ps[m][:], wst[i % 2][:, m * P:(m + 1) * P],
                                     hb[:, kc * TH:(kc + 1) * TH],
                                     start=(kc == 0), stop=(kc == 7))
                        mmi.then_inc(sem["wf"], 1)
                        inc("wf")

                @B.scalar
                def _(s):
                    for mg in range(4):
                        s.wait_ge(sem["wf"], wf0 + (mg + 1) * 8)
                        for m in range(8):
                            mc = mg * 8 + m
                            s.activation(ffnh_v[:, mc * TH:(mc + 1) * TH], ps[m][:],
                                         AF.Silu, bias=b1eff[:, mc:mc + 1], scale=1.0)
                        s.drain().then_inc(sem["ev"], 1)
                        inc("ev")

            with nc.Block() as B:
                wsA0, wsB0 = cnt["wsA"], cnt["wsB"]
                wf0, mm0, ea0 = cnt["wf"], cnt["mm"], cnt["ea"]

                @B.sync
                def _(q):
                    for kc in range(32):
                        if kc >= 2:
                            q.wait_ge(sem["wf"], wf0 + kc - 1)
                        pn = "wsA" if kc % 2 == 0 else "wsB"
                        q.dma_start(out=wst[kc % 2][:],
                                    in_=w2bf[kc * P:(kc + 1) * P, :]).then_inc(sem[pn], 16)
                        inc(pn, 16)

                @B.tensor
                def _(t):
                    for kc in range(32):
                        t.wait_ge(sem["wsA" if kc % 2 == 0 else "wsB"],
                                  (wsA0 if kc % 2 == 0 else wsB0) + (kc // 2 + 1) * 16)
                        for m in range(8):
                            mmi = t.matmul(ps[m][:], wst[kc % 2][:, m * P:(m + 1) * P],
                                     ffnh_v[:, kc * TH:(kc + 1) * TH],
                                     start=(kc == 0), stop=(kc == 31))
                        mmi.then_inc(sem["wf"], 1)
                        inc("wf")

                hv0m5 = cnt["hv"]

                @B.scalar
                def _(s):
                    s.wait_ge(sem["wf"], wf0 + 32)
                    for m in range(8):
                        if m >= 2:
                            s.wait_ge(sem["hv"], hv0m5 + m - 1)
                        s.activation(x2s[:, (m % 2) * TH:(m % 2) * TH + TH], ps[m][:],
                                     AF.Identity, bias=b2h[:, m:m + 1], scale=0.5)
                        s.drain().then_inc(sem["ea"], 1)
                        inc("ea")

                @B.vector
                def _(v):
                    for m in range(8):
                        v.wait_ge(sem["ea"], ea0 + m + 1)
                        v.tensor_add(o2b[:, m * TH:(m + 1) * TH],
                                     x2s[:, (m % 2) * TH:(m % 2) * TH + TH],
                                     u2b[:, m * TH:(m + 1) * TH])
                        v.drain().then_inc(sem["hv"], 1)
                        inc("hv")

            ln_block(o2b, TH, 1)

            with nc.Block() as B:
                osA0, osB0 = cnt["osA"], cnt["osB"]
                hv0f = cnt["hv"]

                @B.vector
                def _(v):
                    for dm in range(DM):
                        if dm >= 2:
                            v.wait_ge(sem["osA" if dm % 2 == 0 else "osB"],
                                      (osA0 if dm % 2 == 0 else osB0) + (dm // 2) * 16)
                        v.tensor_mul(tmpb[:, 0:TH], o2b[:, dm * TH:(dm + 1) * TH], rbc[:, 0:TH])
                        v.tensor_add(fstg_v[dm % 2], tmpb[:, 0:TH], mbc[:, 0:TH])
                        v.tensor_add(fstg_v[dm % 2], fstg_v[dm % 2], x1b[:, dm * TH:(dm + 1) * TH])
                        v.drain().then_inc(sem["hv"], 1)
                        inc("hv")

                @B.sync
                def _(q):
                    for dm in range(DM):
                        q.wait_ge(sem["hv"], hv0f + dm + 1)
                        pn = "osA" if dm % 2 == 0 else "osB"
                        q.dma_start(out=out[dm * P:(dm + 1) * P, H0:H0 + TH],
                                    in_=fstg_v[dm % 2]).then_inc(sem[pn], 16)
                        inc(pn, 16)

    with nc.Block() as B:
        @B.sync
        def _(q):
            q.wait_ge(sem["osA"], cnt["osA"])
            q.wait_ge(sem["osB"], cnt["osB"])

    nc.finalize()
    return nc


_NC_CACHE = None


def kernel(L0_attn_x, L0_attn_W_in, L0_attn_conv_w, L0_attn_conv_b, L0_attn_W_x,
           L0_attn_A_log, L0_attn_D, L0_attn_W_out, L0_ffn_ln_gamma, L0_ffn_ln_beta,
           L0_ffn_fc1_q, L0_ffn_fc1_scales, L0_ffn_fc1_bias, L0_ffn_fc2_q,
           L0_ffn_fc2_scales, L0_ffn_fc2_bias):
    global _NC_CACHE
    x = np.asarray(L0_attn_x, dtype=np.float32)
    W_in = np.ascontiguousarray(np.asarray(L0_attn_W_in, np.float32))
    conv_w = np.asarray(L0_attn_conv_w, np.float32)
    conv_b = np.asarray(L0_attn_conv_b, np.float32)
    W_x = np.asarray(L0_attn_W_x, np.float32)
    A_log = np.asarray(L0_attn_A_log, np.float32)
    Dv = np.asarray(L0_attn_D, np.float32)
    W_out = np.asarray(L0_attn_W_out, np.float32)
    gam = np.asarray(L0_ffn_ln_gamma, np.float32)
    bet = np.asarray(L0_ffn_ln_beta, np.float32)
    q1 = np.asarray(L0_ffn_fc1_q, np.int32)
    s1 = np.asarray(L0_ffn_fc1_scales, np.float32)
    b1 = np.asarray(L0_ffn_fc1_bias, np.float32)
    q2 = np.asarray(L0_ffn_fc2_q, np.int32)
    s2 = np.asarray(L0_ffn_fc2_scales, np.float32)
    b2 = np.asarray(L0_ffn_fc2_bias, np.float32)

    col = lambda v, nc_: np.ascontiguousarray(v.reshape(nc_, P).T)

    in_maps = []
    for c in range(8):
        b, j = divmod(c, 2)
        sl = slice(j * 1024, (j + 1) * 1024)
        xTb = np.ascontiguousarray(x[b].T)
        xoT = np.ascontiguousarray(
            xTb.reshape(1024, NCH, 2, TH)[:, :, j].reshape(1024, NCH * TH))
        w_in_slc = np.concatenate([W_in[sl], W_in[2048 + (j * 1024):2048 + (j + 1) * 1024]], 0)
        a_slc = A_log[sl]
        a_c = np.ascontiguousarray(a_slc.reshape(DC, P, S).transpose(1, 0, 2).reshape(P, DC * S))
        cw = conv_w[sl, 0, :]
        cw_c = np.ascontiguousarray(cw.reshape(DC, P, 4).transpose(1, 0, 2).reshape(P, DC * 4))
        in_maps.append({
            "xT": xTb,
            "xoT": xoT,
            "w_in_T": np.ascontiguousarray(w_in_slc.T),
            "w_x_T": np.ascontiguousarray(W_x[:, sl].T),
            "w_out_T": np.ascontiguousarray(W_out[:, sl].T),
            "a_log_c": a_c,
            "conv_w_c": cw_c,
            "conv_b_c": col(conv_b[sl], DC),
            "d_c": col(Dv[sl], DC),
            "gam_c": col(gam, DM),
            "bet_c": col(bet, DM),
            "q1": q1,
            "s1": s1,
            "b1_c": col(b1, 32),
            "q2": q2,
            "s2": s2,
            "b2_c": col(b2, DM),
        })

    if _NC_CACHE is None:
        _NC_CACHE = build_nc()
    res = run_bass_kernel_spmd(_NC_CACHE, in_maps, core_ids=list(range(8)))

    outp = np.empty((4, 8192, 1024), dtype=np.float32)
    for c in range(8):
        b, j = divmod(c, 2)
        o = res.results[c]["out"]
        o = o.reshape(1024, NCH, TH)
        for ch in range(NCH):
            outp[b, ch * TC + j * TH: ch * TC + (j + 1) * TH, :] = o[:, ch, :].T
    return outp


if __name__ == "__main__":
    import reference
    inp = reference.setup_inputs(0)
    got = kernel(**{k: np.asarray(v) for k, v in inp.items()})
    ref = np.asarray(reference.reference(**inp))
    num = float(((got - ref) ** 2).mean())
    den = float((ref ** 2).mean())
    print("res_var_ratio:", num / den)


# version 09
# speedup vs baseline: 1.1216x; 1.0104x over previous
import numpy as np
import ml_dtypes
import concourse.bass as bass
import concourse.mybir as mybir
from concourse.bass_utils import run_bass_kernel_spmd

F32 = mybir.dt.float32
BF16 = mybir.dt.bfloat16
I32 = mybir.dt.int32
AF = mybir.ActivationFunctionType
ALU = mybir.AluOpType

P = 128
TC = 1024
NCH = 8
TH = 512
DM = 8
DC = 8
S = 16
SG = 4
NSG = S // SG
EPS = 1e-5
RG = [[0, 1], [2, 3], [4, 5], [6, 7]]
PHL = 99


def build_nc():
    nc = bass.Bass("TRN2", target_bir_lowering=False, detect_race_conditions=False)
    dt = lambda n, s, d, k="Internal": nc.dram_tensor(n, s, d, kind=k).ap()

    xT = dt("xT", [1024, 8192], F32, "ExternalInput")
    xoT = dt("xoT", [1024, 4096], F32, "ExternalInput")
    w_in_T = dt("w_in_T", [1024, 2048], F32, "ExternalInput")
    w_x_T = dt("w_x_T", [1024, 33], F32, "ExternalInput")
    w_out_T = dt("w_out_T", [1024, 1024], F32, "ExternalInput")
    a_log_c = dt("a_log_c", [P, DC * S], F32, "ExternalInput")
    conv_w_c = dt("conv_w_c", [P, DC * 4], F32, "ExternalInput")
    conv_b_c = dt("conv_b_c", [P, DC], F32, "ExternalInput")
    d_c = dt("d_c", [P, DC], F32, "ExternalInput")
    gam_c = dt("gam_c", [P, DM], F32, "ExternalInput")
    bet_c = dt("bet_c", [P, DM], F32, "ExternalInput")
    q1 = dt("q1", [1024, 4096], I32, "ExternalInput")
    s1 = dt("s1", [8, 4096], F32, "ExternalInput")
    b1_c = dt("b1_c", [P, 32], F32, "ExternalInput")
    q2 = dt("q2", [4096, 1024], I32, "ExternalInput")
    s2 = dt("s2", [32, 1024], F32, "ExternalInput")
    b2_c = dt("b2_c", [P, DM], F32, "ExternalInput")
    out = dt("out", [1024, 4096], F32, "ExternalOutput")

    xbf = dt("xbf", [1024, 8192], BF16)
    xobf = dt("xobf", [1024, 4096], BF16)
    w_in_bf = dt("w_in_bf", [1024, 2048], BF16)
    w_out_bf = dt("w_out_bf", [1024, 1024], BF16)
    w1bf = dt("w1bf", [1024, 4096], BF16)
    w2bf = dt("w2bf", [4096, 1024], BF16)
    rows32 = dt("rows32", [1, 4 * TC], F32)
    rows16 = dt("rows16", [32, TC], BF16)
    ssm_in = dt("ssm_in", [33, TC], F32)
    ssm_out = dt("ssm_out", [33, TC], F32)
    mix_in = dt("mix_in", [2, 1024, TH], F32)
    mix_out = dt("mix_out", [1024, TH], F32)

    sbuf = nc.alloc_sbuf_tensor
    w_x_sb = sbuf("w_x_sb", [P, DC * 33], BF16)
    negA = sbuf("negA", [P, DC * S], F32)
    convw = sbuf("convw", [P, DC * 4], F32)
    convb = sbuf("convb", [P, DC], F32)
    ddc = sbuf("ddc", [P, DC], F32)
    gamc = sbuf("gamc", [P, DM], F32)
    b1eff = sbuf("b1eff", [P, 32], F32)
    b2h = sbuf("b2h", [P, DM], F32)
    betc = sbuf("betc", [P, DM], F32)
    ones1 = sbuf("ones1", [P, 1], BF16)
    ones16 = sbuf("ones16", [1, S], BF16)
    cst = sbuf("cst", [P, 2], F32)
    state = sbuf("state", [P, DC * S], F32)
    halo = sbuf("halo", [P, DC * 3], BF16)

    xt = sbuf("xt", [P, DM * TC], BF16)
    xin = sbuf("xin", [P, DC * (TC + 3)], BF16)
    xsi = sbuf("xsi", [P, DC * TC], BF16)
    zg = sbuf("zg", [P, DC * TC], BF16)
    yb = sbuf("yb", [P, DC * TC], BF16)
    x2s = sbuf("x2s", [P, TC], BF16)
    x1b = sbuf("x1b", [P, DM * TH], BF16)
    u2b = sbuf("u2b", [P, DM * TH], BF16)
    o2b = sbuf("o2b", [P, DM * TH], BF16)
    wst = [sbuf(f"wst{i}", [P, 1024], BF16) for i in range(2)]
    rbc = sbuf("rbc", [P, TC], F32)
    mbc = sbuf("mbc", [P, TC], F32)
    srow = sbuf("srow", [1, 4 * TC], F32)
    tmpb = sbuf("tmpb", [P, TC], BF16)

    A = sbuf("A", [P, 12288], F32)
    Ab = A[:].bitcast(BF16)
    u_v = lambda: Ab[:, 0:DM * TC]
    hg_v = [Ab[:, 0:SG * TC], Ab[:, SG * TC:2 * SG * TC]]
    dbb_v = [Ab[:, 8192 + i * SG * TC: 8192 + (i + 1) * SG * TC] for i in range(2)]
    cbb_v = [Ab[:, 16384 + i * SG * TC: 16384 + (i + 1) * SG * TC] for i in range(2)]
    ffnh_v = Ab[:, 0:32 * TH]
    xo = Ab[:, 16384:16384 + DM * TH]

    U = sbuf("U", [P, 4096], F32)
    Ub = U[:].bitcast(BF16)
    prew_v = U[:, 0:4096]
    ssm_v = U[0:33, 0:1024]
    Bt_v = U[0:16, 0:1024]
    Ct_v = U[0:16, 1024:2048]
    dBt_v = Ub[0:16, 4096:5120]
    Cbt_v = Ub[0:16, 5120:6144]
    dbc_v = U[:, 0:1024]
    dA_v = [U[:, 1024:2048], U[:, 2048:3072]]
    pp_v = [Ub[:, 6144:7168], Ub[:, 7168:8192]]
    mixh_v = U[:, 0:4096]
    fstg_v = [U[:, 0:512], U[:, 512:1024]]
    r4bc_v = U[:, 1024:1536]

    preb = xt[:, 0:4096]
    pres = xt[:, 4096:8192]
    pres_f = A[:, 0:4096]

    ps = [nc.alloc_psum_tensor(f"ps{i}", [P, 512], F32) for i in range(8)]

    names = ["ld", "wsA", "wsB", "row", "bc", "bcA", "bcB", "osA", "osB", "cc", "mm", "ev", "ea", "hv", "cp", "wf", "lg", "sg", "sdA", "sdB"]
    sem = {n: nc.alloc_semaphore(n) for n in names}
    cnt = {n: 0 for n in names}

    def inc(n, k=1):
        cnt[n] += k
        return cnt[n]

    with nc.Block() as B:
        @B.sync
        def _(q):
            for dst, src in [(negA, a_log_c), (convw, conv_w_c), (convb, conv_b_c),
                             (ddc, d_c), (gamc, gam_c), (betc, bet_c),
                             (b1eff, b1_c), (b2h, b2_c)]:
                q.dma_start(out=dst[:], in_=src).then_inc(sem["ld"], 16)
                inc("ld", 16)
            q.wait_ge(sem["ld"], cnt["ld"])

        @B.gpsimd
        def _(g):
            for kc in range(DC):
                g.dma_start(out=w_x_sb[:, kc * 33:(kc + 1) * 33],
                            in_=w_x_T[kc * P:(kc + 1) * P, :]).then_inc(sem["lg"], 16)
                inc("lg", 16)
            g.wait_ge(sem["lg"], cnt["lg"])

        @B.vector
        def _(v):
            v.memset(state[:], 0.0)
            v.memset(halo[:], 0.0)
            v.memset(ones1[:], 1.0)
            v.memset(ones16[:], 1.0)
            v.memset(cst[:, 0:1], EPS)
            v.memset(cst[:, 1:2], 1.0)

    with nc.Block() as B:
        @B.scalar
        def _(s):
            s.activation(negA[:], negA[:], AF.Exp, bias=0.0, scale=1.0)

    with nc.Block() as B:
        @B.vector
        def _(v):
            v.tensor_scalar_mul(negA[:], negA[:], -1.0)
            v.tensor_scalar_mul(b2h[:], b2h[:], 0.5)

    rounds = []
    for kc in range(8):
        for h in range(2):
            rounds.append((xT[kc * P:(kc + 1) * P, h * 4096:(h + 1) * 4096],
                           xbf[kc * P:(kc + 1) * P, h * 4096:(h + 1) * 4096], 4096))
    for kc in range(8):
        rounds.append((xoT[kc * P:(kc + 1) * P, :], xobf[kc * P:(kc + 1) * P, :], 4096))
    for kc in range(8):
        rounds.append((w_in_T[kc * P:(kc + 1) * P, :], w_in_bf[kc * P:(kc + 1) * P, :], 2048))
    for kc in range(8):
        rounds.append((w_out_T[kc * P:(kc + 1) * P, :], w_out_bf[kc * P:(kc + 1) * P, :], 1024))

    with nc.Block() as B:
        sdA0c, sdB0c = cnt["sdA"], cnt["sdB"]

        @B.gpsimd
        def _(g):
            for i, (src, dst, n) in enumerate(rounds):
                bufn = (preb if i % 2 == 0 else pres)[:, 0:n]
                pn = "sdA" if i % 2 == 0 else "sdB"
                pn0 = sdA0c if i % 2 == 0 else sdB0c
                if i >= 2:
                    g.wait_ge(sem[pn], pn0 + (i // 2) * 16)
                g.dma_start(out=bufn, in_=src).then_inc(sem["lg"], 16)
                g.wait_ge(sem["lg"], inc("lg", 16))
                g.dma_start(out=dst, in_=bufn).then_inc(sem[pn], 16)
                inc(pn, 16)
            g.wait_ge(sem["sdA"], cnt["sdA"])
            g.wait_ge(sem["sdB"], cnt["sdB"])

    with nc.Block() as B:
        ea0, hv0 = cnt["ea"], cnt["hv"]
        sgA0, sgB0 = cnt["sdA"], cnt["sdB"]
        qb = [preb, pres]
        sb_ = [pres_f, A[:, 4096:8192]]
        rb = [xsi[:, 0:4096], xsi[:, 4096:8192]]

        @B.gpsimd
        def _(g):
            for kc in range(8):
                if kc >= 2:
                    g.wait_ge(sem["hv"], hv0 + kc - 1)
                g.dma_start(out=qb[kc % 2][:, 0:4096], in_=q1[kc * P:(kc + 1) * P, :]).then_inc(sem["lg"], 16)
                inc("lg", 16)
                g.dma_start(out=sb_[kc % 2], in_=s1[kc:kc + 1, :].broadcast_to([P, 4096])).then_inc(sem["lg"], 16)
                g.wait_ge(sem["lg"], inc("lg", 16))
                g.sem_inc(sem["ea"], 1)
                inc("ea")
                if kc >= 1:
                    g.wait_ge(sem["hv"], hv0 + kc)
                    pn, pn0 = ("sdA", sgA0) if (kc - 1) % 2 == 0 else ("sdB", sgB0)
                    if kc >= 3:
                        g.wait_ge(sem[pn], pn0 + ((kc - 1) // 2) * 16)
                    g.dma_start(out=w1bf[(kc - 1) * P:kc * P, :], in_=rb[(kc - 1) % 2]).then_inc(sem[pn], 16)
                    inc(pn, 16)
            g.wait_ge(sem["hv"], hv0 + 8)
            g.dma_start(out=w1bf[7 * P:8 * P, :], in_=rb[7 % 2]).then_inc(sem["sdB"], 16)
            inc("sdB", 16)
            for kc in range(32):
                if kc >= 2:
                    g.wait_ge(sem["hv"], hv0 + 8 + kc - 1)
                g.dma_start(out=qb[kc % 2][:, 0:1024], in_=q2[kc * P:(kc + 1) * P, :]).then_inc(sem["lg"], 16)
                inc("lg", 16)
                g.dma_start(out=sb_[kc % 2][:, 0:1024], in_=s2[kc:kc + 1, :].broadcast_to([P, 1024])).then_inc(sem["lg"], 16)
                g.wait_ge(sem["lg"], inc("lg", 16))
                g.sem_inc(sem["ea"], 1)
                inc("ea")
                if kc >= 1:
                    g.wait_ge(sem["hv"], hv0 + 8 + kc)
                    pn, pn0 = ("sdA", sgA0) if (kc - 1) % 2 == 0 else ("sdB", sgB0)
                    g.wait_ge(sem[pn], cnt[pn] - 16)
                    g.dma_start(out=w2bf[(kc - 1) * P:kc * P, :], in_=rb[(kc - 1) % 2][:, 0:1024]).then_inc(sem[pn], 16)
                    inc(pn, 16)
            g.wait_ge(sem["hv"], hv0 + 40)
            g.dma_start(out=w2bf[31 * P:32 * P, :], in_=rb[31 % 2][:, 0:1024]).then_inc(sem["sdB"], 16)
            g.wait_ge(sem["sdB"], inc("sdB", 16))
            g.wait_ge(sem["sdA"], cnt["sdA"])

        @B.vector
        def _(v):
            for kc in range(8):
                v.wait_ge(sem["ea"], ea0 + kc + 1)
                if kc >= 2:
                    pn, pn0 = ("sdA", sgA0) if kc % 2 == 0 else ("sdB", sgB0)
                    pass
                v.scalar_tensor_tensor(prew_v, qb[kc % 2][:, 0:4096], -128.0, sb_[kc % 2], ALU.add, ALU.mult)
                v.tensor_scalar(rb[kc % 2], prew_v, gamc[:, kc:kc + 1], None, ALU.mult)
                v.drain().then_inc(sem["hv"], 1)
                inc("hv")
            for kc in range(32):
                v.wait_ge(sem["ea"], ea0 + 8 + kc + 1)
                v.scalar_tensor_tensor(prew_v[:, 0:1024], qb[kc % 2][:, 0:1024], -128.0,
                                       sb_[kc % 2][:, 0:1024], ALU.add, ALU.mult)
                v.tensor_copy(rb[kc % 2][:, 0:1024], prew_v[:, 0:1024])
                v.drain().then_inc(sem["hv"], 1)
                inc("hv")

    with nc.Block() as B:
        ea0, hv0, wf0 = cnt["ea"], cnt["hv"], cnt["wf"]

        @B.gpsimd
        def _(g):
            for kc in range(8):
                if kc >= 1:
                    g.wait_ge(sem["wf"], wf0 + kc)
                g.dma_start(out=preb[:], in_=q1[kc * P:(kc + 1) * P, :]).then_inc(sem["lg"], 16)
                inc("lg", 16)
                g.dma_start(out=pres_f, in_=s1[kc:kc + 1, :].broadcast_to([P, 4096])).then_inc(sem["lg"], 16)
                g.wait_ge(sem["lg"], inc("lg", 16))
                g.sem_inc(sem["ea"], 1)
                inc("ea")

        @B.vector
        def _(v):
            for kc in range(8):
                v.wait_ge(sem["ea"], ea0 + kc + 1)
                v.scalar_tensor_tensor(prew_v, preb[:], -128.0, pres_f, ALU.add, ALU.mult)
                v.tensor_scalar(preb[:], prew_v, betc[:, kc:kc + 1], None, ALU.mult)
                v.drain().then_inc(sem["hv"], 1)
                inc("hv")

        @B.tensor
        def _(t):
            for kc in range(8):
                t.wait_ge(sem["hv"], hv0 + kc + 1)
                for n in range(8):
                    mi = t.matmul(ps[n][0:1, :], ones1[:, 0:1], preb[:, n * 512:(n + 1) * 512],
                             start=(kc == 0), stop=(kc == 7))
                mi.then_inc(sem["wf"], 1)
                inc("wf")

    with nc.Block() as B:
        hv0b = cnt["hv"]
        row0b = cnt["row"]

        @B.vector
        def _(v):
            for n in range(8):
                v.tensor_copy(srow[0:1, 0:512], ps[n][0:1, :])
                v.drain().then_inc(sem["hv"], 1)
                inc("hv")
                v.wait_ge(sem["row"], row0b + (n + 1) * 32)
                v.tensor_add(b1eff[:, n * 4:(n + 1) * 4], b1eff[:, n * 4:(n + 1) * 4], mbc[:, 0:4])

        @B.sync
        def _(q):
            for n in range(8):
                q.wait_ge(sem["hv"], hv0b + n + 1)
                q.dma_start(out=rows32[0:1, 0:512], in_=srow[0:1, 0:512]).then_inc(sem["row"], 16)
                q.wait_ge(sem["row"], inc("row", 16))
                q.dma_start(out=mbc[:, 0:4],
                            in_=rows32[0:1, 0:512].rearrange("o (c p) -> (o p) c", p=P),
                            allow_slow_non_contiguous=True).then_inc(sem["row"], 16)
                q.wait_ge(sem["row"], inc("row", 16))

    for ch in range(NCH):
        T0 = ch * TC
        H0 = ch * TH

        if PHL < 1:
            pass
        else:
            with nc.Block() as B:
                @B.sync
                def _(q):
                    for dm in range(DM):
                        q.dma_start(out=xt[:, dm * TC:(dm + 1) * TC],
                                    in_=xbf[dm * P:(dm + 1) * P, T0:T0 + TC]).then_inc(sem["ld"], 16)
                        inc("ld", 16)
                    q.wait_ge(sem["ld"], cnt["ld"])

        if PHL < 2:
            pass
        else:
            with nc.Block() as B:
                ea0, mm0, hv0, cp0 = cnt["ea"], cnt["mm"], cnt["hv"], cnt["cp"]

                @B.scalar
                def _(s):
                    for kc in range(DM):
                        if kc >= 1:
                            s.wait_ge(sem["mm"], mm0 + kc)
                        s.activation(x2s[:, 0:TC],
                                     xt[:, kc * TC:(kc + 1) * TC], AF.Square,
                                     bias=0.0, scale=1.0).then_inc(sem["ea"], 1)
                        inc("ea")
                    s.wait_ge(sem["hv"], hv0 + 1)
                    s.activation(srow[0:1, 2 * TC:2 * TC + TC], srow[0:1, 1 * TC:1 * TC + TC], AF.Sqrt, bias=cst[0:1, 0:1], scale=1.0)
                    s.drain().then_inc(sem["cp"], 1)
                    inc("cp")

                @B.tensor
                def _(t):
                    for kc in range(DM):
                        for n in range(2):
                            t.matmul(ps[n][0:1, :], ones1[:, 0:1],
                                     xt[:, kc * TC + n * 512:kc * TC + (n + 1) * 512],
                                     start=(kc == 0), stop=(kc == DM - 1))
                        t.wait_ge(sem["ea"], ea0 + kc + 1)
                        for n in range(2):
                            mi = t.matmul(ps[2 + n][0:1, :], ones1[:, 0:1],
                                     x2s[:, n * 512:(n + 1) * 512],
                                     start=(kc == 0), stop=(kc == DM - 1))
                        mi.then_inc(sem["mm"], 1)
                        inc("mm")

                @B.vector
                def _(v):
                    v.wait_ge(sem["mm"], mm0 + DM)
                    v.tensor_copy(srow[0:1, 0:512], ps[0][0:1, :])
                    v.tensor_copy(srow[0:1, 512:1024], ps[1][0:1, :])
                    v.tensor_copy(srow[0:1, 1 * TC:1 * TC + 512], ps[2][0:1, :])
                    v.tensor_copy(srow[0:1, 1 * TC + 512:1 * TC + 1024], ps[3][0:1, :])
                    v.tensor_scalar_mul(srow[0:1, 0:TC], srow[0:1, 0:TC], 1.0 / 1024.0)
                    v.tensor_scalar_mul(srow[0:1, 1 * TC:1 * TC + TC], srow[0:1, 1 * TC:1 * TC + TC], 1.0 / 1024.0)
                    v.tensor_mul(srow[0:1, 2 * TC:2 * TC + TC], srow[0:1, 0:TC], srow[0:1, 0:TC])
                    v.tensor_sub(srow[0:1, 1 * TC:1 * TC + TC], srow[0:1, 1 * TC:1 * TC + TC], srow[0:1, 2 * TC:2 * TC + TC])
                    v.drain().then_inc(sem["hv"], 1)
                    inc("hv")
                    v.wait_ge(sem["cp"], cp0 + 1)
                    v.reciprocal(srow[0:1, 2 * TC:2 * TC + TC], srow[0:1, 2 * TC:2 * TC + TC])
                    v.tensor_mul(srow[0:1, 3 * TC:3 * TC + TC], srow[0:1, 0:TC], srow[0:1, 2 * TC:2 * TC + TC])
                    v.tensor_scalar_mul(srow[0:1, 3 * TC:3 * TC + TC], srow[0:1, 3 * TC:3 * TC + TC], -1.0)
                    v.drain().then_inc(sem["hv"], 1)
                    inc("hv")

                @B.sync
                def _(q):
                    q.wait_ge(sem["hv"], hv0 + 2)
                    q.dma_start(out=rows32[0:1, 2 * TC:4 * TC], in_=srow[0:1, 2 * TC:4 * TC]).then_inc(sem["row"], 16)
                    q.wait_ge(sem["row"], inc("row", 16))
                    q.dma_start(out=rbc[:], in_=rows32[0:1, 2 * TC:2 * TC + TC].broadcast_to([P, TC])).then_inc(sem["bc"], 16)
                    inc("bc", 16)
                    q.dma_start(out=mbc[:], in_=rows32[0:1, 3 * TC:3 * TC + TC].broadcast_to([P, TC])).then_inc(sem["bc"], 16)
                    inc("bc", 16)
                    q.wait_ge(sem["bc"], cnt["bc"])

        if PHL < 2:
            pass
        else:
            with nc.Block() as B:
                @B.vector
                def _(v):
                    u = u_v()
                    for dm in range(DM):
                        v.tensor_mul(u[:, dm * TC:(dm + 1) * TC], xt[:, dm * TC:(dm + 1) * TC], rbc[:])
                        v.tensor_add(u[:, dm * TC:(dm + 1) * TC], u[:, dm * TC:(dm + 1) * TC], mbc[:])

        if PHL < 3:
            pass
        else:
            with nc.Block() as B:
                wsA0, wsB0 = cnt["wsA"], cnt["wsB"]
                wf0, mm0, ev0 = cnt["wf"], cnt["mm"], cnt["ev"]

                @B.sync
                def _(q):
                    for i in range(32):
                        q4, kc = divmod(i, 8)
                        if i >= 2:
                            q.wait_ge(sem["wf"], wf0 + i - 1)
                        pn = "wsA" if i % 2 == 0 else "wsB"
                        q.dma_start(out=wst[i % 2][:, 0:512],
                                    in_=w_in_bf[kc * P:(kc + 1) * P, q4 * 512:(q4 + 1) * 512]
                                    ).then_inc(sem[pn], 16)
                        inc(pn, 16)

                @B.tensor
                def _(t):
                    u = u_v()
                    for i in range(32):
                        q4, kc = divmod(i, 8)
                        t.wait_ge(sem["wsA" if i % 2 == 0 else "wsB"],
                                  (wsA0 if i % 2 == 0 else wsB0) + (i // 2 + 1) * 16)
                        if kc == 0 and q4 >= 1:
                            t.wait_ge(sem["ev"], ev0 + q4)
                        for n in range(2):
                            for m in range(4):
                                mmi = t.matmul(ps[n * 4 + m][:],
                                               wst[i % 2][:, m * P:(m + 1) * P],
                                               u[:, kc * TC + n * 512:kc * TC + (n + 1) * 512],
                                               start=(kc == 0), stop=(kc == 7))
                        mmi.then_inc(sem["wf"], 1)
                        inc("wf")

                @B.vector
                def _(v):
                    for q4 in range(2):
                        v.wait_ge(sem["wf"], wf0 + (q4 + 1) * 8)
                        for n in range(2):
                            for m in range(4):
                                dc = q4 * 4 + m
                                base = dc * (TC + 3) + 3
                                v.tensor_copy(xin[:, base + n * 512:base + (n + 1) * 512],
                                              ps[n * 4 + m][:])
                        v.drain().then_inc(sem["ev"], 1)
                        inc("ev")

                @B.scalar
                def _(s):
                    for q4 in range(2, 4):
                        s.wait_ge(sem["wf"], wf0 + (q4 + 1) * 8)
                        for n in range(2):
                            for m in range(4):
                                dc = (q4 - 2) * 4 + m
                                s.activation(zg[:, dc * TC + n * 512:dc * TC + (n + 1) * 512],
                                             ps[n * 4 + m][:], AF.Silu, bias=0.0, scale=1.0)
                        s.drain().then_inc(sem["ev"], 1)
                        inc("ev")

        if PHL < 4:
            pass
        else:
            with nc.Block() as B:
                hv0 = cnt["hv"]

                @B.vector
                def _(v):
                    for dc in range(DC):
                        base = dc * (TC + 3)
                        v.tensor_copy(xin[:, base:base + 3], halo[:, dc * 3:(dc + 1) * 3])
                        v.tensor_copy(halo[:, dc * 3:(dc + 1) * 3], xin[:, base + TC:base + TC + 3])
                        v.tensor_scalar(xsi[:, dc * TC:(dc + 1) * TC], xin[:, base:base + TC],
                                        convw[:, dc * 4:dc * 4 + 1], None, ALU.mult)
                        for k in range(1, 4):
                            v.scalar_tensor_tensor(xsi[:, dc * TC:(dc + 1) * TC],
                                                   xin[:, base + k:base + k + TC],
                                                   convw[:, dc * 4 + k:dc * 4 + k + 1],
                                                   xsi[:, dc * TC:(dc + 1) * TC],
                                                   ALU.mult, ALU.add)
                        v.drain().then_inc(sem["hv"], 1)
                        inc("hv")

                @B.scalar
                def _(s):
                    for dc in range(DC):
                        s.wait_ge(sem["hv"], hv0 + dc + 1)
                        s.activation(xsi[:, dc * TC:(dc + 1) * TC], xsi[:, dc * TC:(dc + 1) * TC],
                                     AF.Silu, bias=convb[:, dc:dc + 1], scale=1.0)

        if PHL < 5:
            pass
        else:
            with nc.Block() as B:
                mm0, hv0, cc0, row0 = cnt["mm"], cnt["hv"], cnt["cc"], cnt["row"]

                @B.tensor
                def _(t):
                    for n in range(2):
                        for kc in range(DC):
                            mi = t.matmul(ps[n][0:33, :], w_x_sb[:, kc * 33:(kc + 1) * 33],
                                          xsi[:, kc * TC + n * 512:kc * TC + (n + 1) * 512],
                                          start=(kc == 0), stop=(kc == DC - 1))
                    mi.then_inc(sem["mm"], 1)
                    inc("mm")

                @B.vector
                def _(v):
                    v.wait_ge(sem["mm"], mm0 + 1)
                    v.tensor_copy(ssm_v[:, 0:512], ps[0][0:33, :])
                    v.tensor_copy(ssm_v[:, 512:1024], ps[1][0:33, :])
                    v.drain().then_inc(sem["hv"], 1)
                    inc("hv")

                @B.gpsimd
                def _(g):
                    g.wait_ge(sem["hv"], hv0 + 1)
                    g.dma_start(out=ssm_in, in_=ssm_v).then_inc(sem["lg"], 16)
                    g.wait_ge(sem["lg"], inc("lg", 16))
                    g.collective_compute("AllReduce", ALU.add, replica_groups=RG,
                                         ins=[ssm_in.opt()], outs=[ssm_out.opt()]).then_inc(sem["cc"], 1)
                    g.wait_ge(sem["cc"], inc("cc"))
                    g.dma_start(out=srow[0:1, 0:TC], in_=ssm_out[32:33, :]).then_inc(sem["lg"], 16)
                    inc("lg", 16)
                    g.dma_start(out=Bt_v, in_=ssm_out[0:16, :]).then_inc(sem["lg"], 16)
                    inc("lg", 16)
                    g.dma_start(out=Ct_v, in_=ssm_out[16:32, :]).then_inc(sem["lg"], 16)
                    g.wait_ge(sem["lg"], inc("lg", 16))

            with nc.Block() as B:
                ea0, hv0, mm0, row0 = cnt["ea"], cnt["hv"], cnt["mm"], cnt["row"]

                @B.scalar
                def _(s):
                    s.activation(srow[0:1, TC:2 * TC], srow[0:1, 0:TC], AF.Abs, bias=0.0, scale=1.0)
                    s.activation(srow[0:1, TC:2 * TC], srow[0:1, TC:2 * TC], AF.Exp, bias=0.0, scale=-1.0)
                    s.activation(srow[0:1, TC:2 * TC], srow[0:1, TC:2 * TC], AF.Ln, bias=cst[0:1, 1:2], scale=1.0)
                    s.activation(srow[0:1, 2 * TC:3 * TC], srow[0:1, 0:TC], AF.Relu, bias=0.0, scale=1.0)
                    s.drain().then_inc(sem["ea"], 1)
                    inc("ea")

                @B.vector
                def _(v):
                    v.wait_ge(sem["ea"], ea0 + 1)
                    v.tensor_add(srow[0:1, 0:TC], srow[0:1, TC:2 * TC], srow[0:1, 2 * TC:3 * TC])
                    v.tensor_copy(tmpb[0:1, :], srow[0:1, 0:TC])
                    v.drain().then_inc(sem["hv"], 1)
                    inc("hv")
                    v.wait_ge(sem["mm"], mm0 + 1)
                    for n in range(2):
                        v.tensor_tensor(dBt_v[:, n * 512:(n + 1) * 512],
                                        Bt_v[:, n * 512:(n + 1) * 512],
                                        ps[2 + n][0:16, :], ALU.mult)
                    v.tensor_copy(Cbt_v, Ct_v)
                    v.drain().then_inc(sem["hv"], 1)
                    inc("hv")

                @B.tensor
                def _(t):
                    t.wait_ge(sem["hv"], hv0 + 1)
                    for n in range(2):
                        mi = t.matmul(ps[2 + n][0:16, :], ones16[0:1, :],
                                      tmpb[0:1, n * 512:(n + 1) * 512], start=True, stop=True)
                    mi.then_inc(sem["mm"], 1)
                    inc("mm")

                @B.sync
                def _(q):
                    q.wait_ge(sem["hv"], hv0 + 2)
                    q.dma_start(out=rows16[0:16, :], in_=dBt_v).then_inc(sem["row"], 16)
                    inc("row", 16)
                    q.dma_start(out=rows16[16:32, :], in_=Cbt_v).then_inc(sem["row"], 16)
                    inc("row", 16)
                    q.dma_start(out=rows32[0:1, 0:TC], in_=srow[0:1, 0:TC]).then_inc(sem["row"], 16)
                    q.wait_ge(sem["row"], inc("row", 16))
                    q.dma_start(out=dbc_v, in_=rows32[0:1, 0:TC].broadcast_to([P, TC])).then_inc(sem["bc"], 16)
                    inc("bc", 16)
                    q.wait_ge(sem["bc"], cnt["bc"])

        if PHL < 6:
            pass
        else:
            with nc.Block() as B:
                ea0, hv0, cp0 = cnt["ea"], cnt["hv"], cnt["cp"]
                NG = NSG * DC

                def g_of(sg, dc):
                    return sg * DC + dc

                def hv_after_scan(g, si):
                    return hv0 + g * 5 + si + 1

                def hv_after_save(g):
                    return hv0 + g * 5 + 5

                bcA0, bcB0 = cnt["bcA"], cnt["bcB"]

                @B.sync
                def _(q):
                    for sg in range(NSG):
                        if sg >= 2:
                            q.wait_ge(sem["hv"], hv_after_save(g_of(sg - 2, DC - 1)))
                            q.wait_ge(sem["cp"], cp0 + (sg - 1))
                        pn = "bcA" if sg % 2 == 0 else "bcB"
                        for si in range(SG):
                            s_ = sg * SG + si
                            q.dma_start(out=dbb_v[sg % 2][:, si * TC:(si + 1) * TC],
                                        in_=rows16[s_:s_ + 1, :].broadcast_to([P, TC])
                                        ).then_inc(sem[pn], 16)
                            inc(pn, 16)
                            q.dma_start(out=cbb_v[sg % 2][:, si * TC:(si + 1) * TC],
                                        in_=rows16[16 + s_:17 + s_, :].broadcast_to([P, TC])
                                        ).then_inc(sem[pn], 16)
                            inc(pn, 16)

                @B.scalar
                def _(s):
                    k = 0
                    for sg in range(NSG):
                        for dc in range(DC):
                            for si in range(SG):
                                s_ = sg * SG + si
                                if k >= 2:
                                    km2 = k - 2
                                    g2, si2 = divmod(km2, SG)
                                    s.wait_ge(sem["hv"], hv_after_scan(g2, si2))
                                s.activation(dA_v[k % 2], dbc_v, AF.Exp, bias=0.0,
                                             scale=negA[:, dc * S + s_:dc * S + s_ + 1]
                                             ).then_inc(sem["ea"], 1)
                                inc("ea")
                                k += 1

                @B.vector
                def _(v):
                    k = 0
                    for sg in range(NSG):
                        v.wait_ge(sem["bcA" if sg % 2 == 0 else "bcB"],
                                  (bcA0 if sg % 2 == 0 else bcB0) + (sg // 2 + 1) * SG * 2 * 16)
                        for dc in range(DC):
                            g = g_of(sg, dc)
                            hb = hg_v[g % 2]
                            for si in range(SG):
                                s_ = sg * SG + si
                                v.tensor_tensor(pp_v[k % 2], xsi[:, dc * TC:(dc + 1) * TC],
                                                dbb_v[sg % 2][:, si * TC:(si + 1) * TC], ALU.mult)
                                v.wait_ge(sem["ea"], ea0 + k + 1)
                                v.tensor_tensor_scan(hb[:, si * TC:(si + 1) * TC],
                                                     dA_v[k % 2], pp_v[k % 2],
                                                     state[:, dc * S + s_:dc * S + s_ + 1],
                                                     ALU.mult, ALU.add).then_inc(sem["hv"], 1)
                                inc("hv")
                                k += 1
                            v.tensor_copy(state[:, dc * S + sg * SG:dc * S + (sg + 1) * SG],
                                          hb[:, TC - 1::TC]).then_inc(sem["hv"], 1)
                            inc("hv")
                            for si in range(SG):
                                v.tensor_tensor(hb[:, si * TC:(si + 1) * TC],
                                                hb[:, si * TC:(si + 1) * TC],
                                                cbb_v[sg % 2][:, si * TC:(si + 1) * TC], ALU.mult)
                            v.tensor_add(hb[:, 0:TC], hb[:, 0:TC], hb[:, TC:2 * TC])
                            v.tensor_add(hb[:, 2 * TC:3 * TC], hb[:, 2 * TC:3 * TC], hb[:, 3 * TC:4 * TC])
                            v.tensor_add(hb[:, 0:TC], hb[:, 0:TC], hb[:, 2 * TC:3 * TC])
                            if sg == 0:
                                v.tensor_copy(yb[:, dc * TC:(dc + 1) * TC], hb[:, 0:TC])
                            else:
                                v.tensor_add(yb[:, dc * TC:(dc + 1) * TC],
                                             yb[:, dc * TC:(dc + 1) * TC], hb[:, 0:TC])
                        v.drain().then_inc(sem["cp"], 1)
                        inc("cp")

        if PHL < 7:
            pass
        else:
            with nc.Block() as B:
                @B.vector
                def _(v):
                    for dc in range(DC):
                        v.scalar_tensor_tensor(yb[:, dc * TC:(dc + 1) * TC],
                                               xsi[:, dc * TC:(dc + 1) * TC],
                                               ddc[:, dc:dc + 1],
                                               yb[:, dc * TC:(dc + 1) * TC], ALU.mult, ALU.add)
                        v.tensor_mul(yb[:, dc * TC:(dc + 1) * TC], yb[:, dc * TC:(dc + 1) * TC],
                                     zg[:, dc * TC:(dc + 1) * TC])

            with nc.Block() as B:
                wsA0, wsB0 = cnt["wsA"], cnt["wsB"]
                wf0, mm0, ev0 = cnt["wf"], cnt["mm"], cnt["ev"]

                @B.sync
                def _(q):
                    for i in range(16):
                        mh, kc = divmod(i, 8)
                        if i >= 2:
                            q.wait_ge(sem["wf"], wf0 + i - 1)
                        pn = "wsA" if i % 2 == 0 else "wsB"
                        q.dma_start(out=wst[i % 2][:],
                                    in_=w_out_bf[kc * P:(kc + 1) * P, :]).then_inc(sem[pn], 16)
                        inc(pn, 16)

                @B.tensor
                def _(t):
                    for i in range(16):
                        mh, kc = divmod(i, 8)
                        t.wait_ge(sem["wsA" if i % 2 == 0 else "wsB"],
                                  (wsA0 if i % 2 == 0 else wsB0) + (i // 2 + 1) * 16)
                        if kc == 0 and mh == 1:
                            t.wait_ge(sem["ev"], ev0 + 1)
                        for n in range(2):
                            for m in range(4):
                                mmi = t.matmul(ps[n * 4 + m][:],
                                         wst[i % 2][:, (mh * 4 + m) * P:(mh * 4 + m + 1) * P],
                                         yb[:, kc * TC + n * 512:kc * TC + (n + 1) * 512],
                                         start=(kc == 0), stop=(kc == 7))
                        mmi.then_inc(sem["wf"], 1)
                        inc("wf")

                hv0p7 = cnt["hv"]
                sdA0, sdB0 = cnt["sdA"], cnt["sdB"]

                @B.scalar
                def _(s):
                    pieces = 0
                    for mh in range(2):
                        s.wait_ge(sem["wf"], wf0 + (mh + 1) * 8)
                        for n in range(2):
                            for m in range(4):
                                dm = mh * 4 + m
                                if pieces >= 2:
                                    s.wait_ge(sem["sdA" if pieces % 2 == 0 else "sdB"],
                                              (sdA0 if pieces % 2 == 0 else sdB0) + (pieces // 2) * 16)
                                s.activation(fstg_v[pieces % 2], ps[n * 4 + m][:],
                                             AF.Identity, bias=0.0, scale=1.0)
                                if n == 1 and m == 3:
                                    s.drain().then_inc(sem["ev"], 1)
                                    inc("ev")
                                else:
                                    s.drain()
                                pn = "sdA" if pieces % 2 == 0 else "sdB"
                                s.dma_start(out=mix_in[n, dm * P:(dm + 1) * P, :],
                                            in_=fstg_v[pieces % 2]).then_inc(sem[pn], 16)
                                inc(pn, 16)
                                pieces += 1

                @B.gpsimd
                def _(g):
                    for dm in range(DM):
                        g.dma_start(out=xo[:, dm * TH:(dm + 1) * TH],
                                    in_=xobf[dm * P:(dm + 1) * P, H0:H0 + TH]).then_inc(sem["lg"], 16)
                        inc("lg", 16)
                    g.wait_ge(sem["lg"], cnt["lg"])
                    g.wait_ge(sem["sdA"], sdA0 + 8 * 16)
                    g.wait_ge(sem["sdB"], sdB0 + 8 * 16)
                    g.collective_compute("ReduceScatter", ALU.add, replica_groups=RG,
                                         ins=[mix_in.opt()], outs=[mix_out.opt()]).then_inc(sem["cc"], 1)
                    g.wait_ge(sem["cc"], inc("cc"))
                    for dm in range(DM):
                        g.dma_start(out=mixh_v[:, dm * TH:(dm + 1) * TH],
                                    in_=mix_out[dm * P:(dm + 1) * P, :]).then_inc(sem["lg"], 16)
                        inc("lg", 16)
                    g.wait_ge(sem["lg"], cnt["lg"])

        if PHL < 8:
            pass
        else:
            def ln_block(src_bf, width, n_halves, extra=None):
                with nc.Block() as B:
                    ea0, mm0, hv0, cp0 = cnt["ea"], cnt["mm"], cnt["hv"], cnt["cp"]

                    @B.scalar
                    def _(s):
                        for kc in range(DM):
                            if kc >= 1:
                                s.wait_ge(sem["mm"], mm0 + kc)
                            s.activation(x2s[:, 0:width],
                                         src_bf[:, kc * width:(kc + 1) * width], AF.Square,
                                         bias=0.0, scale=1.0).then_inc(sem["ea"], 1)
                            inc("ea")
                        s.wait_ge(sem["hv"], hv0 + 1)
                        s.activation(srow[0:1, 2 * TC:2 * TC + width], srow[0:1, 1 * TC:1 * TC + width], AF.Sqrt,
                                     bias=cst[0:1, 0:1], scale=1.0)
                        s.drain().then_inc(sem["cp"], 1)
                        inc("cp")

                    @B.tensor
                    def _(t):
                        for kc in range(DM):
                            for n in range(n_halves):
                                t.matmul(ps[n][0:1, :], ones1[:, 0:1],
                                         src_bf[:, kc * width + n * 512:kc * width + (n + 1) * 512],
                                         start=(kc == 0), stop=(kc == DM - 1))
                            t.wait_ge(sem["ea"], ea0 + kc + 1)
                            for n in range(n_halves):
                                mi = t.matmul(ps[2 + n][0:1, :], ones1[:, 0:1],
                                         x2s[:, n * 512:(n + 1) * 512],
                                         start=(kc == 0), stop=(kc == DM - 1))
                            mi.then_inc(sem["mm"], 1)
                            inc("mm")

                    @B.vector
                    def _(v):
                        v.wait_ge(sem["mm"], mm0 + DM)
                        for n in range(n_halves):
                            v.tensor_copy(srow[0:1, n * 512:(n + 1) * 512], ps[n][0:1, :])
                            v.tensor_copy(srow[0:1, 1 * TC + n * 512:1 * TC + (n + 1) * 512], ps[2 + n][0:1, :])
                        w = width
                        v.tensor_scalar_mul(srow[0:1, 0:w], srow[0:1, 0:w], 1.0 / 1024.0)
                        v.tensor_scalar_mul(srow[0:1, 1 * TC:1 * TC + w], srow[0:1, 1 * TC:1 * TC + w], 1.0 / 1024.0)
                        v.tensor_mul(srow[0:1, 2 * TC:2 * TC + w], srow[0:1, 0:w], srow[0:1, 0:w])
                        v.tensor_sub(srow[0:1, 1 * TC:1 * TC + w], srow[0:1, 1 * TC:1 * TC + w], srow[0:1, 2 * TC:2 * TC + w])
                        v.drain().then_inc(sem["hv"], 1)
                        inc("hv")
                        v.wait_ge(sem["cp"], cp0 + 1)
                        v.reciprocal(srow[0:1, 2 * TC:2 * TC + w], srow[0:1, 2 * TC:2 * TC + w])
                        v.tensor_mul(srow[0:1, 3 * TC:3 * TC + w], srow[0:1, 0:w], srow[0:1, 2 * TC:2 * TC + w])
                        v.tensor_scalar_mul(srow[0:1, 3 * TC:3 * TC + w], srow[0:1, 3 * TC:3 * TC + w], -1.0)
                        if extra is not None:
                            extra(v)
                        v.drain().then_inc(sem["hv"], 1)
                        inc("hv")

                    @B.sync
                    def _(q):
                        q.wait_ge(sem["hv"], hv0 + 2)
                        q.dma_start(out=rows32[0:1, 2 * TC:4 * TC], in_=srow[0:1, 2 * TC:4 * TC]).then_inc(sem["row"], 16)
                        q.wait_ge(sem["row"], inc("row", 16))
                        q.dma_start(out=rbc[:, 0:width],
                                    in_=rows32[0:1, 2 * TC:2 * TC + width].broadcast_to([P, width])).then_inc(sem["bc"], 16)
                        inc("bc", 16)
                        q.dma_start(out=mbc[:, 0:width],
                                    in_=rows32[0:1, 3 * TC:3 * TC + width].broadcast_to([P, width])).then_inc(sem["bc"], 16)
                        inc("bc", 16)
                        q.wait_ge(sem["bc"], cnt["bc"])

            with nc.Block() as B:
                @B.vector
                def _(v):
                    for dm in range(DM):
                        v.tensor_copy(yb[:, dm * TH:(dm + 1) * TH], mixh_v[:, dm * TH:(dm + 1) * TH])

            mxb = yb[:, 0:DM * TH]
            ln_block(mxb, TH, 1)

            with nc.Block() as B:
                @B.vector
                def _(v):
                    for dm in range(DM):
                        v.tensor_mul(tmpb[:, 0:TH], mxb[:, dm * TH:(dm + 1) * TH], rbc[:, 0:TH])
                        v.tensor_add(tmpb[:, 0:TH], tmpb[:, 0:TH], mbc[:, 0:TH])
                        v.tensor_add(x1b[:, dm * TH:(dm + 1) * TH], tmpb[:, 0:TH],
                                     xo[:, dm * TH:(dm + 1) * TH])

        if PHL < 9:
            pass
        else:
            ln_block(x1b, TH, 1)

            with nc.Block() as B:
                ea0, hv0 = cnt["ea"], cnt["hv"]

                @B.vector
                def _(v):
                    v.tensor_mul(srow[0:1, 0:TH], srow[0:1, 2 * TC:2 * TC + TH], srow[0:1, 2 * TC:2 * TC + TH])
                    v.tensor_mul(srow[0:1, 0:TH], srow[0:1, 0:TH], srow[0:1, 1 * TC:1 * TC + TH])
                    v.drain().then_inc(sem["hv"], 1)
                    inc("hv")
                    v.wait_ge(sem["ea"], ea0 + 1)
                    v.reciprocal(srow[0:1, 0:TH], srow[0:1, 0:TH])

                @B.scalar
                def _(s):
                    s.wait_ge(sem["hv"], hv0 + 1)
                    s.activation(srow[0:1, 0:TH], srow[0:1, 0:TH], AF.Sqrt, bias=cst[0:1, 0:1], scale=1.0)
                    s.drain().then_inc(sem["ea"], 1)
                    inc("ea")

                @B.sync
                def _(q):
                    pass

            with nc.Block() as B:
                hv0 = cnt["hv"]

                @B.vector
                def _(v):
                    v.drain().then_inc(sem["hv"], 1)
                    inc("hv")

                @B.sync
                def _(q):
                    q.wait_ge(sem["hv"], cnt["hv"])
                    q.dma_start(out=rows32[0:1, 0:TH], in_=srow[0:1, 0:TH]).then_inc(sem["row"], 16)
                    q.wait_ge(sem["row"], inc("row", 16))
                    q.dma_start(out=r4bc_v, in_=rows32[0:1, 0:TH].broadcast_to([P, TH])).then_inc(sem["bc"], 16)
                    q.wait_ge(sem["bc"], inc("bc", 16))

            with nc.Block() as B:
                @B.vector
                def _(v):
                    hb = yb[:, 0:DM * TH]
                    for dm in range(DM):
                        v.tensor_mul(tmpb[:, 0:TH], x1b[:, dm * TH:(dm + 1) * TH], rbc[:, 0:TH])
                        v.tensor_add(u2b[:, dm * TH:(dm + 1) * TH], tmpb[:, 0:TH], mbc[:, 0:TH])
                        v.tensor_mul(hb[:, dm * TH:(dm + 1) * TH],
                                     u2b[:, dm * TH:(dm + 1) * TH], r4bc_v)

            with nc.Block() as B:
                wsA0, wsB0 = cnt["wsA"], cnt["wsB"]
                wf0, mm0, ev0 = cnt["wf"], cnt["mm"], cnt["ev"]
                hb = yb[:, 0:DM * TH]

                @B.sync
                def _(q):
                    for i in range(32):
                        mg, kc = divmod(i, 8)
                        if i >= 2:
                            q.wait_ge(sem["wf"], wf0 + i - 1)
                        pn = "wsA" if i % 2 == 0 else "wsB"
                        q.dma_start(out=wst[i % 2][:],
                                    in_=w1bf[kc * P:(kc + 1) * P, mg * 1024:(mg + 1) * 1024]
                                    ).then_inc(sem[pn], 16)
                        inc(pn, 16)

                @B.tensor
                def _(t):
                    for i in range(32):
                        mg, kc = divmod(i, 8)
                        t.wait_ge(sem["wsA" if i % 2 == 0 else "wsB"],
                                  (wsA0 if i % 2 == 0 else wsB0) + (i // 2 + 1) * 16)
                        if kc == 0 and mg >= 1:
                            t.wait_ge(sem["ev"], ev0 + mg)
                        for m in range(8):
                            mmi = t.matmul(ps[m][:], wst[i % 2][:, m * P:(m + 1) * P],
                                     hb[:, kc * TH:(kc + 1) * TH],
                                     start=(kc == 0), stop=(kc == 7))
                        mmi.then_inc(sem["wf"], 1)
                        inc("wf")

                @B.scalar
                def _(s):
                    for mg in range(4):
                        s.wait_ge(sem["wf"], wf0 + (mg + 1) * 8)
                        for m in range(8):
                            mc = mg * 8 + m
                            s.activation(ffnh_v[:, mc * TH:(mc + 1) * TH], ps[m][:],
                                         AF.Silu, bias=b1eff[:, mc:mc + 1], scale=1.0)
                        s.drain().then_inc(sem["ev"], 1)
                        inc("ev")

            with nc.Block() as B:
                wsA0, wsB0 = cnt["wsA"], cnt["wsB"]
                wf0, mm0, ea0 = cnt["wf"], cnt["mm"], cnt["ea"]

                @B.sync
                def _(q):
                    for kc in range(32):
                        if kc >= 2:
                            q.wait_ge(sem["wf"], wf0 + kc - 1)
                        pn = "wsA" if kc % 2 == 0 else "wsB"
                        q.dma_start(out=wst[kc % 2][:],
                                    in_=w2bf[kc * P:(kc + 1) * P, :]).then_inc(sem[pn], 16)
                        inc(pn, 16)

                @B.tensor
                def _(t):
                    for kc in range(32):
                        t.wait_ge(sem["wsA" if kc % 2 == 0 else "wsB"],
                                  (wsA0 if kc % 2 == 0 else wsB0) + (kc // 2 + 1) * 16)
                        for m in range(8):
                            mmi = t.matmul(ps[m][:], wst[kc % 2][:, m * P:(m + 1) * P],
                                     ffnh_v[:, kc * TH:(kc + 1) * TH],
                                     start=(kc == 0), stop=(kc == 31))
                        mmi.then_inc(sem["wf"], 1)
                        inc("wf")

                hv0m5 = cnt["hv"]

                @B.scalar
                def _(s):
                    s.wait_ge(sem["wf"], wf0 + 32)
                    for m in range(8):
                        if m >= 2:
                            s.wait_ge(sem["hv"], hv0m5 + m - 1)
                        s.activation(x2s[:, (m % 2) * TH:(m % 2) * TH + TH], ps[m][:],
                                     AF.Identity, bias=b2h[:, m:m + 1], scale=0.5)
                        s.drain().then_inc(sem["ea"], 1)
                        inc("ea")

                @B.vector
                def _(v):
                    for m in range(8):
                        v.wait_ge(sem["ea"], ea0 + m + 1)
                        v.tensor_add(o2b[:, m * TH:(m + 1) * TH],
                                     x2s[:, (m % 2) * TH:(m % 2) * TH + TH],
                                     u2b[:, m * TH:(m + 1) * TH])
                        v.drain().then_inc(sem["hv"], 1)
                        inc("hv")

            ln_block(o2b, TH, 1)

            with nc.Block() as B:
                osA0, osB0 = cnt["osA"], cnt["osB"]
                hv0f = cnt["hv"]

                @B.vector
                def _(v):
                    for dm in range(DM):
                        if dm >= 2:
                            v.wait_ge(sem["osA" if dm % 2 == 0 else "osB"],
                                      (osA0 if dm % 2 == 0 else osB0) + (dm // 2) * 16)
                        v.tensor_mul(tmpb[:, 0:TH], o2b[:, dm * TH:(dm + 1) * TH], rbc[:, 0:TH])
                        v.tensor_add(fstg_v[dm % 2], tmpb[:, 0:TH], mbc[:, 0:TH])
                        v.tensor_add(fstg_v[dm % 2], fstg_v[dm % 2], x1b[:, dm * TH:(dm + 1) * TH])
                        v.drain().then_inc(sem["hv"], 1)
                        inc("hv")

                @B.sync
                def _(q):
                    for dm in range(DM):
                        q.wait_ge(sem["hv"], hv0f + dm + 1)
                        pn = "osA" if dm % 2 == 0 else "osB"
                        q.dma_start(out=out[dm * P:(dm + 1) * P, H0:H0 + TH],
                                    in_=fstg_v[dm % 2]).then_inc(sem[pn], 16)
                        inc(pn, 16)

    with nc.Block() as B:
        @B.sync
        def _(q):
            q.wait_ge(sem["osA"], cnt["osA"])
            q.wait_ge(sem["osB"], cnt["osB"])

    nc.finalize()
    return nc


_NC_CACHE = None


def kernel(L0_attn_x, L0_attn_W_in, L0_attn_conv_w, L0_attn_conv_b, L0_attn_W_x,
           L0_attn_A_log, L0_attn_D, L0_attn_W_out, L0_ffn_ln_gamma, L0_ffn_ln_beta,
           L0_ffn_fc1_q, L0_ffn_fc1_scales, L0_ffn_fc1_bias, L0_ffn_fc2_q,
           L0_ffn_fc2_scales, L0_ffn_fc2_bias):
    global _NC_CACHE
    x = np.asarray(L0_attn_x, dtype=np.float32)
    W_in = np.ascontiguousarray(np.asarray(L0_attn_W_in, np.float32))
    conv_w = np.asarray(L0_attn_conv_w, np.float32)
    conv_b = np.asarray(L0_attn_conv_b, np.float32)
    W_x = np.asarray(L0_attn_W_x, np.float32)
    A_log = np.asarray(L0_attn_A_log, np.float32)
    Dv = np.asarray(L0_attn_D, np.float32)
    W_out = np.asarray(L0_attn_W_out, np.float32)
    gam = np.asarray(L0_ffn_ln_gamma, np.float32)
    bet = np.asarray(L0_ffn_ln_beta, np.float32)
    q1 = np.asarray(L0_ffn_fc1_q, np.int32)
    s1 = np.asarray(L0_ffn_fc1_scales, np.float32)
    b1 = np.asarray(L0_ffn_fc1_bias, np.float32)
    q2 = np.asarray(L0_ffn_fc2_q, np.int32)
    s2 = np.asarray(L0_ffn_fc2_scales, np.float32)
    b2 = np.asarray(L0_ffn_fc2_bias, np.float32)

    col = lambda v, nc_: np.ascontiguousarray(v.reshape(nc_, P).T)

    in_maps = []
    for c in range(8):
        b, j = divmod(c, 2)
        sl = slice(j * 1024, (j + 1) * 1024)
        xTb = np.ascontiguousarray(x[b].T)
        xoT = np.ascontiguousarray(
            xTb.reshape(1024, NCH, 2, TH)[:, :, j].reshape(1024, NCH * TH))
        w_in_slc = np.concatenate([W_in[sl], W_in[2048 + (j * 1024):2048 + (j + 1) * 1024]], 0)
        a_slc = A_log[sl]
        a_c = np.ascontiguousarray(a_slc.reshape(DC, P, S).transpose(1, 0, 2).reshape(P, DC * S))
        cw = conv_w[sl, 0, :]
        cw_c = np.ascontiguousarray(cw.reshape(DC, P, 4).transpose(1, 0, 2).reshape(P, DC * 4))
        in_maps.append({
            "xT": xTb,
            "xoT": xoT,
            "w_in_T": np.ascontiguousarray(w_in_slc.T),
            "w_x_T": np.ascontiguousarray(W_x[:, sl].T),
            "w_out_T": np.ascontiguousarray(W_out[:, sl].T),
            "a_log_c": a_c,
            "conv_w_c": cw_c,
            "conv_b_c": col(conv_b[sl], DC),
            "d_c": col(Dv[sl], DC),
            "gam_c": col(gam, DM),
            "bet_c": col(bet, DM),
            "q1": q1,
            "s1": s1,
            "b1_c": col(b1, 32),
            "q2": q2,
            "s2": s2,
            "b2_c": col(b2, DM),
        })

    if _NC_CACHE is None:
        _NC_CACHE = build_nc()
    res = run_bass_kernel_spmd(_NC_CACHE, in_maps, core_ids=list(range(8)))

    outp = np.empty((4, 8192, 1024), dtype=np.float32)
    for c in range(8):
        b, j = divmod(c, 2)
        o = res.results[c]["out"]
        o = o.reshape(1024, NCH, TH)
        for ch in range(NCH):
            outp[b, ch * TC + j * TH: ch * TC + (j + 1) * TH, :] = o[:, ch, :].T
    return outp


if __name__ == "__main__":
    import reference
    inp = reference.setup_inputs(0)
    got = kernel(**{k: np.asarray(v) for k, v in inp.items()})
    ref = np.asarray(reference.reference(**inp))
    num = float(((got - ref) ** 2).mean())
    den = float((ref ** 2).mean())
    print("res_var_ratio:", num / den)
